# version 22
# baseline (speedup 1.0000x reference)
import numpy as np
import ml_dtypes
import concourse.bass as bass
import concourse.mybir as mybir
from concourse.bass_utils import run_bass_kernel_spmd

F32 = mybir.dt.float32
BF16 = mybir.dt.bfloat16
I32 = mybir.dt.int32
ALU = mybir.AluOpType
AF = mybir.ActivationFunctionType

D = 1024
DJ = 8
H = 8
DFF = 4096
FJ = 32
S = 4096
SC = S // 128
TOWN = 2048
TT = 256
NTILE = TOWN // TT
TC = TT // 128
EPS_LN = 1e-5
EPS_ATTN = 1e-6
KVW = 132

ENGINES = ("sync", "tensor", "vector", "scalar", "gpsimd")
NDMASEM = 4


class Tile:
    __slots__ = ("name", "writes", "readers")

    def __init__(self, name):
        self.name = name
        self.writes = []
        self.readers = []


class Op:
    __slots__ = ("idx", "engine", "build", "deps", "is_dma", "needs_inc",
                 "semval", "seq", "slot", "virtual")

    def __init__(self, idx, engine, build, deps, is_dma):
        self.virtual = False
        self.idx = idx
        self.engine = engine
        self.build = build
        self.deps = deps
        self.is_dma = is_dma
        self.needs_inc = False
        self.semval = None
        self.seq = None
        self.slot = 0


class Prog:
    label = ""

    def __init__(self):
        self.ops = []
        self.labels = []
        self.last_by_engine = {e: None for e in ENGINES}
        self._cap = None

    def emit(self, engine, build, reads=(), writes=(), extra_deps=(),
             is_dma=False):
        if self._cap is not None:
            self._cap.append((engine, build, tuple(reads), tuple(writes),
                              tuple(extra_deps), is_dma))
            return None
        deps = set(extra_deps)
        for t in reads:
            deps.update(t.writes)
        for t in writes:
            deps.update(t.writes)
            deps.update(t.readers)
        idx = len(self.ops)
        op = Op(idx, engine, build, sorted(deps), is_dma)
        self.ops.append(op)
        self.labels.append(Prog.label)
        self.last_by_engine[engine] = idx
        for t in reads:
            if idx not in t.readers:
                t.readers.append(idx)
        for t in writes:
            t.writes = [idx]
            t.readers = []
        return idx

    def barrier(self):
        lasts = [i for i in self.last_by_engine.values() if i is not None]
        for e in ENGINES:
            deps = [i for i in lasts if self.ops[i].engine != e]
            idx = self.emit(e, lambda nc: None, extra_deps=deps)
            self.ops[idx].virtual = True

    def finalize(self):
        dma_rr = {e: 0 for e in ENGINES}
        chains = {}
        for op in self.ops:
            if op.is_dma:
                op.slot = dma_rr[op.engine] % NDMASEM
                dma_rr[op.engine] += 1
                chains.setdefault((op.engine, op.slot), []).append(op.idx)
        consumers = {}
        for op in self.ops:
            for d in op.deps:
                consumers.setdefault(d, []).append(op.idx)
        for lst in chains.values():
            for i in range(1, len(lst)):
                cur = self.ops[lst[i]]
                prev = [c for c in consumers.get(lst[i - 1], ())
                        if c < cur.idx]
                if prev:
                    cur.deps = sorted(set(cur.deps) | set(prev))
        self.slot_chain = chains
        memo = {}

        def expand(i):
            if i in memo:
                return memo[i]
            op = self.ops[i]
            if not op.virtual:
                memo[i] = (i,)
                return memo[i]
            r = []
            for d in op.deps:
                r.extend(expand(d))
            memo[i] = tuple(sorted(set(r)))
            return memo[i]

        for op in self.ops:
            if any(self.ops[d].virtual for d in op.deps):
                nd = set()
                for d in op.deps:
                    nd.update(expand(d))
                op.deps = sorted(nd)
        for op in self.ops:
            for d in op.deps:
                dep = self.ops[d]
                if dep.is_dma or dep.engine != op.engine:
                    dep.needs_inc = True
        cnt = {e: 0 for e in ENGINES}
        dma_cnt = {e: [0] * NDMASEM for e in ENGINES}
        seq = {e: 0 for e in ENGINES}
        for op in self.ops:
            op.seq = seq[op.engine]
            seq[op.engine] += 1
            if op.is_dma:
                dma_cnt[op.engine][op.slot] += 1
                op.semval = dma_cnt[op.engine][op.slot] * 16
                op.needs_inc = True
            elif op.needs_inc:
                cnt[op.engine] += 1
                op.semval = cnt[op.engine]
        self.by_engine = {e: [op for op in self.ops if op.engine == e]
                          for e in ENGINES}

    def run_engine(self, nc, engine, eng, sems, dma_sems):
        waited = {}
        drained_upto = -1
        for op in self.by_engine[engine]:
            need_drain = False
            for d in op.deps:
                dep = self.ops[d]
                if dep.engine == engine and not dep.is_dma:
                    if engine != "tensor" and dep.seq > drained_upto:
                        need_drain = True
                else:
                    key = (dep.engine, dep.is_dma, dep.slot)
                    val = dep.semval
                    assert val is not None, (op.idx, d, dep.engine, engine)
                    if dep.is_dma:
                        ch = self.slot_chain[(dep.engine, dep.slot)]
                        import bisect
                        pos = bisect.bisect_left(ch, op.idx) - 1
                        val = self.ops[ch[pos]].semval
                    if waited.get(key, -1) < val:
                        if dep.is_dma:
                            sem = dma_sems[dep.engine][dep.slot]
                        else:
                            sem = sems[dep.engine]
                        eng.wait_ge(sem, val)
                        waited[key] = val
            if need_drain:
                eng.drain()
                drained_upto = op.seq - 1
            ins = op.build(nc)
            if ins is not None:
                if op.is_dma:
                    ins.then_inc(dma_sems[engine][op.slot], 16)
                elif op.needs_inc:
                    ins.then_inc(sems[engine], 1)



def _threefry2x32(k0, k1, x0, x1):
    def rotl(x, r):
        return ((x << np.uint32(r)) | (x >> np.uint32(32 - r))).astype(np.uint32)
    k0 = np.asarray(k0, np.uint32)
    k1 = np.asarray(k1, np.uint32)
    x0 = np.asarray(x0, np.uint32).copy()
    x1 = np.asarray(x1, np.uint32).copy()
    ks2 = (k0 ^ k1 ^ np.uint32(0x1BD11BDA)).astype(np.uint32)
    x0 = (x0 + k0).astype(np.uint32)
    x1 = (x1 + k1).astype(np.uint32)
    ks = [k0, k1, ks2]
    rot = ((13, 15, 26, 6), (17, 29, 16, 24))
    for i in range(5):
        for r in rot[i % 2]:
            x0 = (x0 + x1).astype(np.uint32)
            x1 = rotl(x1, r) ^ x0
        x0 = (x0 + ks[(i + 1) % 3]).astype(np.uint32)
        x1 = (x1 + ks[(i + 2) % 3] + np.uint32(i + 1)).astype(np.uint32)
    return x0, x1

def build_nc():
    nc = bass.Bass("TRN2", target_bir_lowering=False, debug=False)

    xb = nc.dram_tensor("xb", [S, D], F32, kind="ExternalInput").ap()
    wq = nc.dram_tensor("wq", [D, D], F32, kind="ExternalInput").ap()
    wk = nc.dram_tensor("wk", [D, D], F32, kind="ExternalInput").ap()
    wv = nc.dram_tensor("wv", [D, D], F32, kind="ExternalInput").ap()
    w1a = nc.dram_tensor("w1a", [DFF, D], F32, kind="ExternalInput").ap()
    w2a = nc.dram_tensor("w2a", [D, DFF], F32, kind="ExternalInput").ap()
    W1f = nc.dram_tensor("W1f", [DFF, D], F32, kind="ExternalInput").ap()
    W2f = nc.dram_tensor("W2f", [D, DFF], F32, kind="ExternalInput").ap()
    b1a_c = nc.dram_tensor("b1a_c", [128, FJ], F32, kind="ExternalInput").ap()
    b2a_r = nc.dram_tensor("b2a_r", [1, D], F32, kind="ExternalInput").ap()
    b1f_c = nc.dram_tensor("b1f_c", [128, FJ], F32, kind="ExternalInput").ap()
    b2f_r = nc.dram_tensor("b2f_r", [1, D], F32, kind="ExternalInput").ap()
    g1_r = nc.dram_tensor("g1_r", [1, D], F32, kind="ExternalInput").ap()
    be1_r = nc.dram_tensor("be1_r", [1, D], F32, kind="ExternalInput").ap()
    g2_r = nc.dram_tensor("g2_r", [1, D], F32, kind="ExternalInput").ap()
    be2_r = nc.dram_tensor("be2_r", [1, D], F32, kind="ExternalInput").ap()
    dmask = nc.dram_tensor("dmask", [TOWN, D], BF16, kind="ExternalInput").ap()
    eye_in = nc.dram_tensor("eye", [128, 128], F32, kind="ExternalInput").ap()
    out = nc.dram_tensor("out", [TOWN, D], F32, kind="ExternalOutput").ap()

    s_wq_hi = nc.dram_tensor("s_wq_hi", [128, DJ * D], BF16, kind="Internal").ap()
    s_wq_lo = nc.dram_tensor("s_wq_lo", [128, DJ * D], BF16, kind="Internal").ap()
    s_w1a_hi = nc.dram_tensor("s_w1a_hi", [128, FJ * D], BF16, kind="Internal").ap()
    s_w1a_lo = nc.dram_tensor("s_w1a_lo", [128, FJ * D], BF16, kind="Internal").ap()
    s_w2a_hi = nc.dram_tensor("s_w2a_hi", [128, FJ * D], BF16, kind="Internal").ap()
    s_w2a_lo = nc.dram_tensor("s_w2a_lo", [128, FJ * D], BF16, kind="Internal").ap()
    s_W1q = nc.dram_tensor("s_W1q", [128, FJ * D], BF16, kind="Internal").ap()
    s_W2q = nc.dram_tensor("s_W2q", [128, FJ * D], BF16, kind="Internal").ap()

    p = Prog()
    ctxs = []
    tiles = {}
    scr_t = {nm: Tile(nm) for nm in
             ("s_wq_hi", "s_wq_lo", "s_w1a_hi", "s_w1a_lo", "s_w2a_hi",
              "s_w2a_lo", "s_W1q", "s_W2q")}

    def sb(name, shape, dtype):
        cm = nc.sbuf_tensor(name, shape, dtype)
        h = cm.__enter__()
        ctxs.append(cm)
        t = Tile(name)
        tiles[name] = (t, h.ap())
        return t, h.ap()

    def psum(name):
        cm = nc.psum_tensor(name, [128, 512], F32)
        h = cm.__enter__()
        ctxs.append(cm)
        t = Tile(name)
        return t, h.ap()

    def semaphore(name):
        cm = nc.semaphore(name)
        s = cm.__enter__()
        ctxs.append(cm)
        return s

    sems = {e: semaphore("sem_" + e) for e in ENGINES}
    dma_sems = {e: [semaphore(f"dma_{e}{i}") for i in range(NDMASEM)]
                for e in ENGINES}

    eye_t, eye = sb("eyet", [128, 128], F32)
    ones1_t, ones1 = sb("ones1", [1, 128], F32)
    b1a_t, b1a = sb("b1a_t", [128, FJ], F32)
    b1f_t, b1f = sb("b1f_t", [128, FJ], F32)
    b1fq_t, b1fq = sb("b1fq_t", [128, FJ], F32)
    b2a_t, b2a = sb("b2a_row", [1, D], F32)
    b2f_t, b2f = sb("b2f_row", [1, D], F32)
    b2fq_t, b2fq = sb("b2fq_row", [1, D], F32)
    grow = {}
    for nm in ("g1", "be1", "g2", "be2"):
        grow[nm] = sb(nm + "row", [1, D], F32)
    gbc = {}
    for nm in ("g1", "be1"):
        gbc[nm] = sb(nm + "bc", [128, D], F32)
    QW = 1024
    qI = [sb(f"qI{i}", [128, QW], I32) for i in range(4)]
    qF = [sb(f"qF{i}", [128, QW], F32) for i in range(3)]
    kvh_t, kvh = sb("kv_hi", [128, H * KVW], BF16)
    kvl_t, kvl = sb("kv_lo", [128, H * KVW], BF16)
    z_tt, z_ap = sb("z_t", [128, 1], F32)
    pb = [psum(f"pb{i}") for i in range(8)]

    def E(engine, fn, reads=(), writes=(), extra=(), is_dma=False):
        return p.emit(engine, fn, reads=reads, writes=writes,
                      extra_deps=extra, is_dma=is_dma)

    fill_q = []

    def drain_fill(n=10 ** 9):
        while fill_q and n > 0:
            a = fill_q.pop(0)
            cap, p._cap = p._cap, None
            p.emit(a[0], a[1], reads=a[2], writes=a[3], extra_deps=a[4],
                   is_dma=a[5])
            p._cap = cap
            n -= 1

    def vts(dst, dt_, src, st_, s1, s2, op0, op1=None, sc=()):
        if op1 is None:
            fn = lambda nc, d=dst, s=src, a=s1, o0=op0: \
                nc.vector.tensor_scalar(d, s, a, None, op0=o0)
        else:
            fn = lambda nc, d=dst, s=src, a=s1, b=s2, o0=op0, o1=op1: \
                nc.vector.tensor_scalar(d, s, a, b, op0=o0, op1=o1)
        return E("vector", fn, reads=[st_] + list(sc), writes=[dt_])

    def vtt(dst, dt_, a, at_, b, bt_, op):
        fn = lambda nc, d=dst, x=a, y=b, o=op: \
            nc.vector.tensor_tensor(d, x, y, op=o)
        return E("vector", fn, reads=[at_, bt_], writes=[dt_])

    def vstt(dst, dt_, in0, t0, scalar, in1, t1, op0, op1, sc=()):
        fn = lambda nc, d=dst, x=in0, s=scalar, y=in1, o0=op0, o1=op1: \
            nc.vector.scalar_tensor_tensor(d, x, s, y, op0=o0, op1=o1)
        return E("vector", fn, reads=[t0, t1] + list(sc), writes=[dt_])

    def vcp(dst, dt_, src, st_):
        fn = lambda nc, d=dst, s=src: nc.vector.tensor_copy(d, s)
        return E("vector", fn, reads=[st_], writes=[dt_])

    def sact(dst, dt_, src, st_, func, bias=None, scale=1.0, bt_=None):
        if bias is None:
            fn = lambda nc, d=dst, s=src, f=func, sc_=scale: \
                nc.scalar.activation(d, s, f, scale=sc_)
        else:
            fn = lambda nc, d=dst, s=src, f=func, b=bias, sc_=scale: \
                nc.scalar.activation(d, s, f, bias=b, scale=sc_)
        rd = [st_] + ([bt_] if bt_ is not None else [])
        return E("scalar", fn, reads=rd, writes=[dt_])

    def dmain(queue, dst, dt_, src):
        fn = lambda nc, d=dst, s=src: \
            getattr(nc, queue).dma_start(out=d, in_=s)
        return E(queue, fn, writes=[dt_], is_dma=True)

    def dmaout(queue, dst, src, st_):
        fn = lambda nc, d=dst, s=src: \
            getattr(nc, queue).dma_start(out=d, in_=s)
        return E(queue, fn, reads=[st_], is_dma=True)

    def mm(lhs, lt_, rhs, rt_, dst, dt_, start, stop):
        fn = lambda nc, l=lhs, r=rhs, d=dst, s0=start, s1=stop: \
            nc.tensor.matmul(d, l, r, start=s0, stop=s1)
        return E("tensor", fn, reads=[lt_, rt_], writes=[dt_])

    def transp(dst, dt_, src, st_):
        fn = lambda nc, d=dst, s=src: nc.tensor.transpose(d, s, eye[:, :])
        return E("tensor", fn, reads=[st_, eye_t], writes=[dt_])

    def memset(t_, ap_, val):
        fn = lambda nc, a=ap_, v=val: nc.gpsimd.memset(a, v)
        return E("gpsimd", fn, writes=[t_])

    def quant_chain(src, st_, dst, dt_, W, Mb, emin, P=128, pools=None):
        pI, pF = pools if pools is not None else (qI, qF)
        I1 = pI[0][1][0:P, 0:W]; i1 = pI[0][0]
        I2 = pI[1][1][0:P, 0:W]; i2 = pI[1][0]
        I3 = pI[2][1][0:P, 0:W]; i3 = pI[2][0]
        I4 = pI[3][1][0:P, 0:W]; i4 = pI[3][0]
        F1 = pF[0][1][0:P, 0:W]; f1 = pF[0][0]
        F2 = pF[1][1][0:P, 0:W]; f2 = pF[1][0]
        F3 = pF[2][1][0:P, 0:W]; f3 = pF[2][0]
        bits = src.bitcast(I32)
        vts(I1, i1, bits, st_, 23, 255, ALU.logical_shift_right,
            ALU.bitwise_and)
        vts(I2, i2, I1, i1, 127 + emin, -1, ALU.max, ALU.mult)
        vts(I3, i3, I2, i2, Mb - 1 + 254, 8388608, ALU.add, ALU.mult)
        vts(I1, i1, bits, st_, -2147483648, None, ALU.bitwise_and)
        vts(I4, i4, I2, i2, -1, -Mb, ALU.mult, ALU.add)
        vstt(I4, i4, I4, i4, 8388608, I1, i1, ALU.mult, ALU.add)
        vtt(F1, f1, src, st_, I3.bitcast(F32), i3, ALU.mult)
        vstt(F2, f2, F1, f1, -1.0, F1, f1, ALU.mult, ALU.max)
        vcp(I3, i3, F2, f2)
        vcp(F3, f3, I3, i3)
        vtt(F1, f1, F2, f2, F3, f3, ALU.subtract)
        vts(F2, f2, F1, f1, -0.25, -1.0, ALU.is_ge, ALU.add)
        vstt(F2, f2, F1, f1, 0.25, F2, f2, ALU.is_gt, ALU.add)
        vstt(F2, f2, F1, f1, 0.75, F2, f2, ALU.is_ge, ALU.add)
        vstt(F3, f3, F3, f3, 2.0, F2, f2, ALU.mult, ALU.add)
        vtt(dst, dt_, F3, f3, I4.bitcast(F32), i4, ALU.mult)

    def hq_chain(src, st_, dst, dt_, W):
        I1 = qI[0][1][:, 0:W]; i1 = qI[0][0]
        I2 = qI[1][1][:, 0:W]; i2 = qI[1][0]
        I3 = qI[2][1][:, 0:W]; i3 = qI[2][0]
        I4 = qI[3][1][:, 0:W]; i4 = qI[3][0]
        F1 = qF[0][1][:, 0:W]; f1 = qF[0][0]
        F2 = qF[1][1][:, 0:W]; f2 = qF[1][0]
        F3 = qF[2][1][:, 0:W]; f3 = qF[2][0]
        bits = src.bitcast(I32)
        vts(I1, i1, bits, st_, 23, 255, ALU.logical_shift_right,
            ALU.bitwise_and)
        vts(I2, i2, I1, i1, 113, -1, ALU.max, ALU.mult)
        vts(I3, i3, I1, i1, 113, 8, ALU.min, ALU.add)
        vts(I3, i3, I3, i3, 8388608, None, ALU.mult)
        vts(I1, i1, I2, i2, 2 + 254, 8388608, ALU.add, ALU.mult)
        vts(I4, i4, I2, i2, -1, -2, ALU.mult, ALU.add)
        vts(I4, i4, I4, i4, 8388608, None, ALU.mult)
        vtt(F1, f1, src, st_, I1.bitcast(F32), i1, ALU.mult)
        vtt(F2, f2, F1, f1, I3.bitcast(F32), i3, ALU.subtract)
        vtt(F3, f3, F1, f1, I3.bitcast(F32), i3, ALU.add)
        vts(F1, f1, F2, f2, 0.5, None, ALU.is_gt)
        for j in range(1, 8):
            if j % 2 == 0:
                vstt(F1, f1, F2, f2, j + 0.5, F1, f1, ALU.is_gt, ALU.add)
            else:
                vstt(F1, f1, F3, f3, j + 0.5, F1, f1, ALU.is_ge, ALU.add)
        vtt(dst, dt_, F1, f1, I4.bitcast(F32), i4, ALU.mult)

    NLN = 4
    lnp = [sb(f"lnp{i}", [128, 48], F32) for i in range(NLN)]
    lnpi = [sb(f"lnpi{i}", [128, 2], I32) for i in range(NLN)]
    _ln_rr = [0]

    def ln_multi(jobs):
        nch = len(jobs)
        assert nch <= 2
        sl = _ln_rr[0] % NLN
        _ln_rr[0] += 1
        pt_, pp = lnp[sl]
        it_, ip = lnpi[sl]
        for c, (xap, xt_, _, _) in enumerate(jobs):
            E("vector", lambda nc, d=pp[:, 16 * c:16 * c + 6],
              s=xap[:, 0:512]: nc.vector.bn_stats(d, s),
              reads=[xt_], writes=[pt_])
            E("vector", lambda nc, d=pp[:, 16 * c + 6:16 * c + 12],
              s=xap[:, 512:1024]: nc.vector.bn_stats(d, s),
              reads=[xt_], writes=[pt_])
            E("vector", lambda nc, d=pp[:, 16 * c + 12:16 * c + 14],
              s=pp[:, 16 * c:16 * c + 12]: nc.vector.bn_aggr(d, s),
              reads=[pt_], writes=[pt_])
        VA = pp[:, 13:13 + 16 * nch].rearrange("q (c e) -> q c e",
                                               e=16)[:, :, 0:1]
        T1 = pp[:, 40:40 + nch]
        R = pp[:, 42:42 + nch]
        S = pp[:, 44:44 + nch]
        E("vector", lambda nc, d=T1, s=VA: nc.vector.tensor_scalar(
            d, s, EPS_LN, None, op0=ALU.add), reads=[pt_], writes=[pt_])
        E("vector", lambda nc, d=ip[:, 0:nch], s=T1.bitcast(I32):
          nc.vector.tensor_scalar(d, s, 1, None,
                                  op0=ALU.logical_shift_right),
          reads=[pt_], writes=[it_])
        E("vector", lambda nc, d=ip[:, 0:nch], s=ip[:, 0:nch]:
          nc.vector.tensor_scalar(d, s, -1, 0x5f3759df, op0=ALU.mult,
                                  op1=ALU.add), reads=[it_], writes=[it_])
        E("vector", lambda nc, d=R, s=ip[:, 0:nch].bitcast(F32):
          nc.vector.tensor_copy(d, s), reads=[it_], writes=[pt_])
        for _ in range(2):
            E("vector", lambda nc, d=S, a=R, b=R:
              nc.vector.tensor_tensor(d, a, b, op=ALU.mult),
              reads=[pt_], writes=[pt_])
            E("vector", lambda nc, d=S, a=S, b=T1:
              nc.vector.tensor_tensor(d, a, b, op=ALU.mult),
              reads=[pt_], writes=[pt_])
            E("vector", lambda nc, d=S, s=S: nc.vector.tensor_scalar(
                  d, s, -0.5, 1.5, op0=ALU.mult, op1=ALU.add),
              reads=[pt_], writes=[pt_])
            E("vector", lambda nc, d=R, a=R, b=S:
              nc.vector.tensor_tensor(d, a, b, op=ALU.mult),
              reads=[pt_], writes=[pt_])
        for c, (xap, xt_, dstap, dt_) in enumerate(jobs):
            E("vector", lambda nc, d=dstap, s=xap,
              m_=pp[:, 16 * c + 12:16 * c + 13], r_=R[:, c:c + 1]:
              nc.vector.tensor_scalar(d, s, m_, r_, op0=ALU.subtract,
                                      op1=ALU.mult),
              reads=[xt_, pt_], writes=[dt_])

    def ln_norm(xap, xt_, dstap, dt_):
        ln_multi([(xap, xt_, dstap, dt_)])

    dmain("sync", eye[:, :], eye_t, eye_in[:, :])
    dmain("sync", b1a[:, :], b1a_t, b1a_c[:, :])
    dmain("sync", b1f[:, :], b1f_t, b1f_c[:, :])
    dmain("sync", b2a[0:1, :], b2a_t, b2a_r[:, :])
    dmain("sync", b2f[0:1, :], b2f_t, b2f_r[:, :])
    dmain("sync", grow["g1"][1][0:1, :], grow["g1"][0], g1_r[:, :])
    dmain("sync", grow["be1"][1][0:1, :], grow["be1"][0], be1_r[:, :])
    dmain("sync", grow["g2"][1][0:1, :], grow["g2"][0], g2_r[:, :])
    dmain("sync", grow["be2"][1][0:1, :], grow["be2"][0], be2_r[:, :])
    memset(ones1_t, ones1[0:1, :], 1.0)
    for nm in ("g1", "be1"):
        row_t, row_ap = grow[nm]
        bc_t, bc_ap = gbc[nm]
        for og in range(2):
            bt_, bank = pb[5]
            mm(ones1[0:1, 0:128], ones1_t,
               row_ap[0:1, og * 512:og * 512 + 512], row_t,
               bank[:, :], bt_, start=True, stop=True)
            E("scalar", lambda nc, d=bc_ap[:, og * 512:og * 512 + 512],
              s=bank[:, :]: nc.scalar.copy(d, s),
              reads=[bt_], writes=[bc_t])
    quant_chain(b1f[:, 0:FJ], b1f_t, b1fq[:, 0:FJ], b1fq_t, FJ, 2, -14)
    quant_chain(b2f[0:1, 0:D], b2f_t, b2fq[0:1, 0:D], b2fq_t, D, 2, -14, P=1)

    scope1 = []
    def _enter1(name, shape, dtype):
        cm = nc.sbuf_tensor(name, shape, dtype)
        h = cm.__enter__(); scope1.append(cm); return h
    wtmp_h = [_enter1(f"wtmp{i}", [128, D], F32) for i in range(2)]
    pthi_h = [_enter1(f"pthi{i}", [128, D], BF16) for i in range(2)]
    ptlo_h = [_enter1(f"ptlo{i}", [128, D], BF16) for i in range(2)]
    ptf_h = [_enter1(f"ptf{i}", [128, D], F32) for i in range(2)]
    wkhi_h = _enter1("wkT_hi", [128, DJ * D], BF16)
    wklo_h = _enter1("wkT_lo", [128, DJ * D], BF16)
    wvhi_h = _enter1("wvT_hi", [128, DJ * D], BF16)
    wvlo_h = _enter1("wvT_lo", [128, DJ * D], BF16)
    p1xs_h = [_enter1(f"p1xs{i}", [128, D], F32) for i in range(2)]
    p1xh_h = [_enter1(f"p1xh{i}", [128, D], F32) for i in range(2)]
    p1x2hi_h = [_enter1(f"p1x2hi{i}", [128, D], BF16) for i in range(2)]
    p1x2lo_h = [_enter1(f"p1x2lo{i}", [128, D], BF16) for i in range(2)]
    p1khi_h = [_enter1(f"p1khi{i}", [128, D], BF16) for i in range(2)]
    p1klo_h = [_enter1(f"p1klo{i}", [128, D], BF16) for i in range(2)]
    p1vhi_h = [_enter1(f"p1vhi{i}", [128, H * KVW], BF16) for i in range(2)]
    p1vlo_h = [_enter1(f"p1vlo{i}", [128, H * KVW], BF16) for i in range(2)]
    p1tmp_h = [_enter1("p1tmp0", [128, 512], F32)] * 2
    p1tmpk_h = _enter1("p1tmpk", [128, 512], F32)
    if True:
        wtmpP = [(Tile(f"wtmp{i}"), wtmp_h[i].ap()) for i in range(2)]
        pthiP = [(Tile(f"pthi{i}"), pthi_h[i].ap()) for i in range(2)]
        ptloP = [(Tile(f"ptlo{i}"), ptlo_h[i].ap()) for i in range(2)]
        ptfP = [(Tile(f"ptf{i}"), ptf_h[i].ap()) for i in range(2)]
        wkhi_t, wkhi = Tile("wkhi"), wkhi_h.ap()
        wklo_t, wklo = Tile("wklo"), wklo_h.ap()
        wvhi_t, wvhi = Tile("wvhi"), wvhi_h.ap()
        wvlo_t, wvlo = Tile("wvlo"), wvlo_h.ap()

        def transpose_blocks(src, st_, nsub, sink, banks=(6, 7)):
            for g0 in range(0, nsub, 4):
                bt_, bank = pb[banks[(g0 // 4) % len(banks)]]
                gn = min(4, nsub - g0)
                for gg in range(gn):
                    jj = g0 + gg
                    transp(bank[:, gg * 128:(gg + 1) * 128], bt_,
                           src[:, jj * 128:(jj + 1) * 128], st_)
                sink(g0, gn, bank, bt_)

        _prep_rr = [0]

        def prep_split_one(src_dram, oc, strided_in, dst_hi, dst_lo,
                           dst_hi_t, dst_lo_t, res_hi=None, res_hi_t=None,
                           res_lo=None, res_lo_t=None):
            pi = _prep_rr[0] % 2
            _prep_rr[0] += 1
            wtmp_t, wtmp = wtmpP[pi]
            pthi_t, pthi = pthiP[pi]
            ptlo_t, ptlo = ptloP[pi]
            if strided_in:
                src_ap = src_dram[:, oc * 128:(oc + 1) * 128]
                dmain("scalar",
                      wtmp[:, 0:D].rearrange("q (s c) -> q s c", c=128),
                      wtmp_t,
                      src_ap.rearrange("(s q) c -> q s c", q=128))
            else:
                dmain("scalar", wtmp[:, 0:D], wtmp_t,
                      src_dram[oc * 128:(oc + 1) * 128, :])

            def sink(g0, gn, bank, bt_):
                w = gn * 128
                vcp(pthi[:, g0 * 128:g0 * 128 + w], pthi_t,
                    bank[:, 0:w], bt_)
                vstt(ptlo[:, g0 * 128:g0 * 128 + w], ptlo_t,
                     pthi[:, g0 * 128:g0 * 128 + w], pthi_t, -1.0,
                     bank[:, 0:w], bt_, ALU.mult, ALU.add)
            transpose_blocks(wtmp[:, 0:D], wtmp_t, DJ, sink,
                             banks=(6 + pi,))
            if dst_hi is not None:
                E("gpsimd", lambda nc, d=dst_hi[:, oc * D:(oc + 1) * D],
                  s=pthi[:, 0:D]: nc.gpsimd.dma_start(out=d, in_=s),
                  reads=[pthi_t], writes=[dst_hi_t], is_dma=True)
                E("gpsimd", lambda nc, d=dst_lo[:, oc * D:(oc + 1) * D],
                  s=ptlo[:, 0:D]: nc.gpsimd.dma_start(out=d, in_=s),
                  reads=[ptlo_t], writes=[dst_lo_t], is_dma=True)
            else:
                dh = res_hi[:, :].rearrange("q (j d) -> q j d", d=D)[
                    :, :, oc * 128:(oc + 1) * 128]
                dl = res_lo[:, :].rearrange("q (j d) -> q j d", d=D)[
                    :, :, oc * 128:(oc + 1) * 128]
                sh = pthi[:, 0:D].rearrange("q (j r) -> q j r", r=128)
                sl = ptlo[:, 0:D].rearrange("q (j r) -> q j r", r=128)
                E("vector", lambda nc, d=dh, s=sh: nc.vector.tensor_copy(
                    d, s), reads=[pthi_t], writes=[res_hi_t])
                E("vector", lambda nc, d=dl, s=sl: nc.vector.tensor_copy(
                    d, s), reads=[ptlo_t], writes=[res_lo_t])

        def prep_quant_one(src_dram, oc, strided_in, dst, dst_t):
            pi = _prep_rr[0] % 2
            _prep_rr[0] += 1
            wtmp_t, wtmp = wtmpP[pi]
            pthi_t, pthi = pthiP[pi]
            ptf_t, ptf = ptfP[pi]
            if strided_in:
                src_ap = src_dram[:, oc * 128:(oc + 1) * 128]
                dmain("scalar",
                      wtmp[:, 0:D].rearrange("q (s c) -> q s c", c=128),
                      wtmp_t,
                      src_ap.rearrange("(s q) c -> q s c", q=128))
            else:
                dmain("scalar", wtmp[:, 0:D], wtmp_t,
                      src_dram[oc * 128:(oc + 1) * 128, :])
            quant_chain(wtmp[:, 0:D], wtmp_t, ptf[:, 0:D], ptf_t, D, 2, -14)

            def qsink(g0, gn, bank, bt_):
                w = gn * 128
                vcp(pthi[:, g0 * 128:g0 * 128 + w], pthi_t,
                    bank[:, 0:w], bt_)
            transpose_blocks(ptf[:, 0:D], ptf_t, DJ, qsink,
                             banks=(6 + pi,))
            E("gpsimd", lambda nc, d=dst[:, oc * D:(oc + 1) * D],
              s=pthi[:, 0:D]: nc.gpsimd.dma_start(out=d, in_=s),
              reads=[pthi_t], writes=[dst_t], is_dma=True)

        for oc in range(DJ):
            prep_split_one(wk, oc, False, None, None, None, None,
                           wkhi, wkhi_t, wklo, wklo_t)
        for oc in range(DJ):
            prep_split_one(wv, oc, False, None, None, None, None,
                           wvhi, wvhi_t, wvlo, wvlo_t)
        prep_units = []
        for oc in range(DJ):
            prep_units.append(lambda oc=oc: prep_split_one(
                wq, oc, False, s_wq_hi, s_wq_lo,
                scr_t["s_wq_hi"], scr_t["s_wq_lo"]))
        splits = []
        quants = []
        for oc in range(FJ):
            splits.append(lambda oc=oc: prep_split_one(
                w1a, oc, False, s_w1a_hi, s_w1a_lo,
                scr_t["s_w1a_hi"], scr_t["s_w1a_lo"]))
        for oc in range(FJ):
            splits.append(lambda oc=oc: prep_split_one(
                w2a, oc, True, s_w2a_hi, s_w2a_lo,
                scr_t["s_w2a_hi"], scr_t["s_w2a_lo"]))
        for oc in range(FJ):
            quants.append(lambda oc=oc: prep_quant_one(
                W1f, oc, False, s_W1q, scr_t["s_W1q"]))
        for oc in range(FJ):
            quants.append(lambda oc=oc: prep_quant_one(
                W2f, oc, True, s_W2q, scr_t["s_W2q"]))
        for i in range(max(len(splits), len(quants))):
            if i < len(splits):
                prep_units.append(splits[i])
            if i < len(quants):
                prep_units.append(quants[i])
        for u in prep_units[:DJ]:
            u()
        prep_units = prep_units[DJ:]

        p1tmp_shared = (Tile("p1tmp0"), p1tmp_h[0].ap())
        p1tmpk_shared = (Tile("p1tmpk"), p1tmpk_h.ap())
        P1 = [dict(
            tmp=p1tmp_shared,
            tmpk=p1tmpk_shared,
            xs=(Tile(f"p1xs{i}"), p1xs_h[i].ap()),
            xh=(Tile(f"p1xh{i}"), p1xh_h[i].ap()),
            x2hi=(Tile(f"p1x2hi{i}"), p1x2hi_h[i].ap()),
            x2lo=(Tile(f"p1x2lo{i}"), p1x2lo_h[i].ap()),
            khi=(Tile(f"p1khi{i}"), p1khi_h[i].ap()),
            klo=(Tile(f"p1klo{i}"), p1klo_h[i].ap()),
            vhi=(Tile(f"p1vhi{i}"), p1vhi_h[i].ap()),
            vlo=(Tile(f"p1vlo{i}"), p1vlo_h[i].ap()),
        ) for i in range(2)]
        for i in range(2):
            memset(P1[i]["vhi"][0], P1[i]["vhi"][1][:, :], 0.0)
            memset(P1[i]["vlo"][0], P1[i]["vlo"][1][:, :], 0.0)
            for h in range(H):
                memset(P1[i]["vhi"][0],
                       P1[i]["vhi"][1][:, h * KVW + 128:h * KVW + 129], 1.0)

        kv_started = [False, False, False]
        nprep = len(prep_units)

        def p1_ln(si):
            par = P1[si % 2]
            p1xs_t, p1xs = par["xs"]
            p1xh_t, p1xh = par["xh"]
            dmain("sync", p1xs[:, :], p1xs_t, xb[si * 128:(si + 1) * 128, :])
            ln_norm(p1xs, p1xs_t, p1xh[:, :], p1xh_t)
            E("vector", lambda nc, d=p1xh[:, :], a=p1xh[:, :],
              b=gbc["g1"][1][:, :]: nc.vector.tensor_tensor(
                  d, a, b, op=ALU.mult),
              reads=[p1xh_t, gbc["g1"][0]], writes=[p1xh_t])
            E("vector", lambda nc, d=p1xh[:, :], a=p1xh[:, :],
              b=gbc["be1"][1][:, :]: nc.vector.tensor_tensor(
                  d, a, b, op=ALU.add),
              reads=[p1xh_t, gbc["be1"][0]], writes=[p1xh_t])

        def p1_tr(si):
            par = P1[si % 2]
            p1xh_t, p1xh = par["xh"]
            p1x2hi_t, p1x2hi = par["x2hi"]
            p1x2lo_t, p1x2lo = par["x2lo"]

            def split_sink_round(g0, bank, bt_):
                E("scalar", lambda nc, d=p1x2hi[:, g0 * 128:g0 * 128 + 512],
                  s=bank[:, :]: nc.scalar.copy(d, s),
                  reads=[bt_], writes=[p1x2hi_t])
                E("vector", lambda nc, d=p1x2lo[:, g0 * 128:g0 * 128 + 512],
                  a=p1x2hi[:, g0 * 128:g0 * 128 + 512], b=bank[:, :]:
                  nc.vector.scalar_tensor_tensor(
                      d, a, -1.0, b, op0=ALU.mult, op1=ALU.add),
                  reads=[bt_, p1x2hi_t], writes=[p1x2lo_t])
            for g0 in range(0, DJ, 4):
                bt_, bank = pb[7 if (g0 // 4) % 2 == 0 else 6]
                for gg in range(4):
                    jj = g0 + gg
                    transp(bank[:, gg * 128:(gg + 1) * 128], bt_,
                           p1xh[:, jj * 128:(jj + 1) * 128], p1xh_t)
                split_sink_round(g0, bank, bt_)

        def p1_kv(si):
            par = P1[si % 2]
            p1x2hi_t, p1x2hi = par["x2hi"]
            p1x2lo_t, p1x2lo = par["x2lo"]
            p1khi_t, p1khi = par["khi"]
            p1klo_t, p1klo = par["klo"]
            p1vhi_t, p1vhi = par["vhi"]
            p1vlo_t, p1vlo = par["vlo"]
            p1tmp_t, p1tmp = par["tmp"]
            p1tmpk_t, p1tmpk = par["tmpk"]
            for (whi_ap, whi_t2, wlo_ap, wlo_t2, isk) in (
                    (wkhi, wkhi_t, wklo, wklo_t, True),
                    (wvhi, wvhi_t, wvlo, wvlo_t, False)):
                for og in range(2):
                    if isk:
                        bt_, bank = pb[3 + og]
                    else:
                        bt_, bank = pb[5 if og == 0 else 3]
                    first = True
                    for dj in range(DJ):
                        lh = p1x2hi[:, dj * 128:(dj + 1) * 128]
                        ll = p1x2lo[:, dj * 128:(dj + 1) * 128]
                        rh = whi_ap[:, dj * D + og * 512:
                                    dj * D + og * 512 + 512]
                        rl = wlo_ap[:, dj * D + og * 512:
                                    dj * D + og * 512 + 512]
                        mm(lh, p1x2hi_t, rh, whi_t2, bank[:, :], bt_,
                           start=first, stop=False)
                        first = False
                        mm(lh, p1x2hi_t, rl, wlo_t2, bank[:, :], bt_,
                           start=False, stop=False)
                        mm(ll, p1x2lo_t, rh, whi_t2, bank[:, :], bt_,
                           start=False, stop=(dj == DJ - 1))
                    if isk:
                        sact(p1tmp[:, :], p1tmp_t, bank[:, :], bt_, AF.Relu,
                             scale=-1.0)
                        sact(p1tmp[:, :], p1tmp_t, p1tmp[:, :], p1tmp_t,
                             AF.Exp, scale=-1.0)
                        E("vector", lambda nc, d=p1tmpk[:, :],
                          s=bank[:, :], e2=p1tmp[:, :]:
                          nc.vector.scalar_tensor_tensor(
                              d, s, 0.0, e2, op0=ALU.max, op1=ALU.add),
                          reads=[bt_, p1tmp_t], writes=[p1tmpk_t])
                        E("scalar", lambda nc,
                          d=p1khi[:, og * 512:og * 512 + 512],
                          s=p1tmpk[:, :]: nc.scalar.copy(d, s),
                          reads=[p1tmpk_t], writes=[p1khi_t])
                        E("vector", lambda nc,
                          d=p1klo[:, og * 512:og * 512 + 512],
                          a=p1khi[:, og * 512:og * 512 + 512],
                          b=p1tmpk[:, :]: nc.vector.scalar_tensor_tensor(
                              d, a, -1.0, b, op0=ALU.mult, op1=ALU.add),
                          reads=[p1tmpk_t, p1khi_t], writes=[p1klo_t])
                    else:
                        h0 = og * 4
                        vdst = p1vhi[:, h0 * KVW:(h0 + 4) * KVW].rearrange(
                            "q (h w) -> q h w", w=KVW)[:, :, 0:128]
                        ldst = p1vlo[:, h0 * KVW:(h0 + 4) * KVW].rearrange(
                            "q (h w) -> q h w", w=KVW)[:, :, 0:128]
                        psrc = bank[:, :].rearrange("q (h w) -> q h w", w=128)
                        E("scalar", lambda nc, d=vdst, s=psrc:
                          nc.scalar.copy(d, s), reads=[bt_], writes=[p1vhi_t])
                        E("vector", lambda nc, d=ldst, a=vdst, b=psrc:
                          nc.vector.scalar_tensor_tensor(
                              d, a, -1.0, b, op0=ALU.mult, op1=ALU.add),
                          reads=[bt_, p1vhi_t], writes=[p1vlo_t])
            for h in range(H):
                bt_, bank = pb[h // 3]
                off = (h % 3) * KVW
                dst = bank[:, off:off + KVW]
                lh = p1khi[:, h * 128:(h + 1) * 128]
                ll = p1klo[:, h * 128:(h + 1) * 128]
                rh = p1vhi[:, h * KVW:(h + 1) * KVW]
                rl = p1vlo[:, h * KVW:(h + 1) * KVW]
                st = not kv_started[h // 3]
                kv_started[h // 3] = True
                last = (si == SC - 1) and (h % 3 == 2 or h == 7)
                mm(lh, p1khi_t, rh, p1vhi_t, dst, bt_, start=st, stop=False)
                mm(lh, p1khi_t, rl, p1vlo_t, dst, bt_, start=False,
                   stop=False)
                mm(ll, p1klo_t, rh, p1vhi_t, dst, bt_, start=False,
                   stop=last)

        p1_ln(0)
        p1_tr(0)
        def run_zipped(units):
            k = 0
            while k < len(units):
                if k + 1 < len(units):
                    p._cap = []
                    units[k]()
                    t0 = p._cap
                    p._cap = []
                    units[k + 1]()
                    t1 = p._cap
                    p._cap = None
                    i0 = i1 = 0
                    while i0 < len(t0) or i1 < len(t1):
                        if i0 < len(t0):
                            a = t0[i0]; i0 += 1
                            p.emit(a[0], a[1], reads=a[2], writes=a[3],
                                   extra_deps=a[4], is_dma=a[5])
                        if i1 < len(t1):
                            a = t1[i1]; i1 += 1
                            p.emit(a[0], a[1], reads=a[2], writes=a[3],
                                   extra_deps=a[4], is_dma=a[5])
                    k += 2
                else:
                    units[k]()
                    k += 1

        for si in range(SC):
            lo_u = (si * nprep) // SC
            hi_u = ((si + 1) * nprep) // SC
            run_zipped(prep_units[lo_u:hi_u])
            if si + 1 < SC:
                p1_ln(si + 1)
            p1_kv(si)
            if si + 1 < SC:
                p1_tr(si + 1)
        for h in range(H):
            bt_, bank = pb[h // 3]
            off = (h % 3) * KVW
            E("scalar", lambda nc, d=kvh[:, h * KVW:(h + 1) * KVW],
              s=bank[:, off:off + KVW]: nc.scalar.copy(d, s),
              reads=[bt_], writes=[kvh_t])
            E("vector", lambda nc, d=kvl[:, h * KVW:(h + 1) * KVW],
              a=kvh[:, h * KVW:(h + 1) * KVW], b=bank[:, off:off + KVW]:
              nc.vector.scalar_tensor_tensor(
                  d, a, -1.0, b, op0=ALU.mult, op1=ALU.add),
              reads=[bt_, kvh_t], writes=[kvl_t])

    for cm in reversed(scope1):
        cm.__exit__(None, None, None)
    p.barrier()

    scope2 = []
    def _enter2(name, shape, dtype):
        cm = nc.sbuf_tensor(name, shape, dtype)
        h = cm.__enter__(); scope2.append(cm); return h
    xt2_h = [_enter2(f"xt2_{i}", [128, TC * D], F32) for i in range(2)]
    xorig_h = [_enter2("xorig0", [128, TC * D], F32)] * 2
    xh2_h = [_enter2(f"xh2_{i}", [128, D], F32) for i in range(2)]
    x2hi2_h = _enter2("x2hi2", [128, DJ * TT], BF16)
    x2lo2_h = _enter2("x2lo2", [128, DJ * TT], BF16)
    wbufhi_h = [_enter2(f"wbuf_hi{i}", [128, 2 * D], BF16) for i in range(2)]
    wbuflo_h = [_enter2(f"wbuf_lo{i}", [128, 2 * D], BF16) for i in range(2)]
    w2blo_h = [_enter2(f"w2b_lo{i}", [128, 2 * D], BF16) for i in range(2)]
    Qhi_h = _enter2("Qhi", [128, H * TT], BF16)
    Qlo_h = _enter2("Qlo", [128, H * TT], BF16)
    qtmp_h = [_enter2(f"qtmp{i}", [128, TT], F32) for i in range(2)]
    qtmp2_h = [_enter2(f"qtmp2_{i}", [128, TT], F32) for i in range(2)]
    hgrp_h = [_enter2(f"hgrp{i}", [128, 4 * TT], F32) for i in range(2)]
    hgrphi_h = [_enter2(f"hgrp_hi{i}", [128, 4 * TT], BF16) for i in range(2)]
    hgrplo_h = [_enter2(f"hgrp_lo{i}", [128, 4 * TT], BF16) for i in range(2)]
    y1g_h = [_enter2(f"y1g{i}", [128, 4 * TT], F32) for i in range(2)]
    hqg_h = [_enter2(f"hqg{i}", [128, 4 * TT], BF16) for i in range(2)]
    xq2_h = _enter2("xq2", [128, DJ * TT], BF16)
    y2f_h = _enter2("y2f", [128, TC * D], F32)
    dmt_h = [_enter2(f"dmt{i}", [128, TC * D], BF16) for i in range(2)]
    gb2_h = {nm: _enter2(nm + "bc", [128, D], F32) for nm in ("g2", "be2")}
    if True:
        xt2P = [(Tile(f"xt2_{i}"), xt2_h[i].ap()) for i in range(2)]
        _xo = (Tile("xorig0"), xorig_h[0].ap())
        xorigP = [_xo, _xo]
        xh2P = [(Tile(f"xh2_{i}"), xh2_h[i].ap()) for i in range(2)]
        x2f2_t, x2f2 = None, None
        x2hi2_t, x2hi2 = Tile("x2hi2"), x2hi2_h.ap()
        x2lo2_t, x2lo2 = Tile("x2lo2"), x2lo2_h.ap()
        wbufhiP = [(Tile(f"wbufhi{i}"), wbufhi_h[i].ap()) for i in range(2)]
        wbufloP = [(Tile(f"wbuflo{i}"), wbuflo_h[i].ap()) for i in range(2)]
        w2bloP = [(Tile(f"w2blo{i}"), w2blo_h[i].ap()) for i in range(2)]
        Qhi_t, Qhi = Tile("Qhi"), Qhi_h.ap()
        Qlo_t, Qlo = Tile("Qlo"), Qlo_h.ap()
        w2bhiP = [(Qhi_t, Qhi), (Qlo_t, Qlo)]
        qtmpP = [(Tile(f"qtmp{i}"), qtmp_h[i].ap()) for i in range(2)]
        qtmp2P = [(Tile(f"qtmp2_{i}"), qtmp2_h[i].ap()) for i in range(2)]
        hgrpP = [(Tile(f"hgrp{i}"), hgrp_h[i].ap()) for i in range(2)]
        hgrphiP = [(Tile(f"hgrphi{i}"), hgrphi_h[i].ap()) for i in range(2)]
        hgrploP = [(Tile(f"hgrplo{i}"), hgrplo_h[i].ap()) for i in range(2)]
        y1gP = [(Tile(f"y1g{i}"), y1g_h[i].ap()) for i in range(2)]
        hqgP = [(Tile(f"hqg{i}"), hqg_h[i].ap()) for i in range(2)]
        xq2_t, xq2 = Tile("xq2"), xq2_h.ap()
        y2f_t, y2f = Tile("y2f"), y2f_h.ap()
        x2f2_t, x2f2 = y2f_t, y2f
        dmtP = [(Tile(f"dmt{i}"), dmt_h[i].ap()) for i in range(2)]
        for nm in ("g2", "be2"):
            gbc[nm] = (Tile(nm + "bc"), gb2_h[nm].ap())
            row_t, row_ap = grow[nm]
            bc_t, bc_ap = gbc[nm]
            for og in range(2):
                bt_, bank = pb[5]
                mm(ones1[0:1, 0:128], ones1_t,
                   row_ap[0:1, og * 512:og * 512 + 512], row_t,
                   bank[:, :], bt_, start=True, stop=True)
                E("scalar", lambda nc, d=bc_ap[:, og * 512:og * 512 + 512],
                  s=bank[:, :]: nc.scalar.copy(d, s),
                  reads=[bt_], writes=[bc_t])

        def transpose_fold2(src_t, src_ap, gname, bname):
            ln_multi([(src_ap[:, c * D:(c + 1) * D], src_t,
                       xh2P[c % 2][1][:, :], xh2P[c % 2][0])
                      for c in range(TC)])
            for c in range(TC):
                xh2_t, xh2 = xh2P[c % 2]
                E("vector", lambda nc, d=xh2[:, :], a=xh2[:, :],
                  b=gbc[gname][1][:, :]: nc.vector.tensor_tensor(
                      d, a, b, op=ALU.mult),
                  reads=[xh2_t, gbc[gname][0]], writes=[xh2_t])
                E("vector", lambda nc, d=xh2[:, :], a=xh2[:, :],
                  b=gbc[bname][1][:, :]: nc.vector.tensor_tensor(
                      d, a, b, op=ALU.add),
                  reads=[xh2_t, gbc[bname][0]], writes=[xh2_t])
                for g0 in range(0, DJ, 4):
                    bt_, bank = pb[6 + (g0 // 4) % 2]
                    for gg in range(4):
                        jj = g0 + gg
                        transp(bank[:, gg * 128:(gg + 1) * 128], bt_,
                               xh2[:, jj * 128:(jj + 1) * 128], xh2_t)
                    dsth = x2hi2[:, :].rearrange(
                        "q (j r) -> q j r", r=TT)[:, g0:g0 + 4,
                                                  c * 128:c * 128 + 128]
                    dstl = x2lo2[:, :].rearrange(
                        "q (j r) -> q j r", r=TT)[:, g0:g0 + 4,
                                                  c * 128:c * 128 + 128]
                    psrc = bank[:, :].rearrange("q (j r) -> q j r", r=128)
                    E("scalar", lambda nc, d=dsth, s=psrc:
                      nc.scalar.copy(d, s), reads=[bt_], writes=[x2hi2_t])
                    E("vector", lambda nc, d=dstl, a=dsth, b=psrc:
                      nc.vector.scalar_tensor_tensor(
                          d, a, -1.0, b, op0=ALU.mult, op1=ALU.add),
                      reads=[bt_, x2hi2_t], writes=[x2lo2_t])

        def tile_body(ti):
            tp = ti % 2
            xt2_t, xt2 = xt2P[tp]
            xorig_t, xorig = xorigP[tp]
            dmt_t, dmt = dmtP[tp]
            def head1():
                rows_r = xb[ti * TT:(ti + 1) * TT, :].rearrange("(c q) d -> q c d", q=128)
                xt2_3d = xt2[:, :].rearrange("q (c d) -> q c d", d=D)
                xo_3d = xorig[:, :].rearrange("q (c d) -> q c d", d=D)
                dmain("sync", xt2_3d, xt2_t, rows_r)
                dmain("sync", xo_3d, xorig_t, rows_r)
                dm_r = dmask[ti * TT:(ti + 1) * TT, :].rearrange("(c q) d -> q c d", q=128)
                dmain("sync", dmt[:, :].rearrange("q (c d) -> q c d", d=D), dmt_t, dm_r)

                transpose_fold2(xt2_t, xt2, "g1", "be1")

                for oc in range(H):
                    ws = (oc // 2) % 2
                    wbufhi_t, wbufhi = wbufhiP[ws]
                    wbuflo_t, wbuflo = wbufloP[ws]
                    qtmp_t, qtmp = qtmpP[oc % 2]
                    qtmp2_t, qtmp2 = qtmp2P[oc % 2]
                    if oc % 2 == 0:
                        E("gpsimd", lambda nc, d=wbufhi[:, 0:2 * D],
                          s=s_wq_hi[:, oc * D:(oc + 2) * D]:
                          nc.gpsimd.dma_start(out=d, in_=s),
                          reads=[scr_t["s_wq_hi"]], writes=[wbufhi_t],
                          is_dma=True)
                        E("gpsimd", lambda nc, d=wbuflo[:, 0:2 * D],
                          s=s_wq_lo[:, oc * D:(oc + 2) * D]:
                          nc.gpsimd.dma_start(out=d, in_=s),
                          reads=[scr_t["s_wq_lo"]], writes=[wbuflo_t],
                          is_dma=True)
                    off = (oc % 2) * D
                    bt_, bank = pb[oc % 4]
                    first = True
                    for dj in range(DJ):
                        lh = wbufhi[:, off + dj * 128:off + (dj + 1) * 128]
                        ll = wbuflo[:, off + dj * 128:off + (dj + 1) * 128]
                        rh = x2hi2[:, dj * TT:(dj + 1) * TT]
                        rl = x2lo2[:, dj * TT:(dj + 1) * TT]
                        mm(lh, wbufhi_t, rh, x2hi2_t, bank[:, 0:TT], bt_,
                           start=first, stop=False)
                        first = False
                        mm(lh, wbufhi_t, rl, x2lo2_t, bank[:, 0:TT], bt_,
                           start=False, stop=False)
                        mm(ll, wbuflo_t, rh, x2hi2_t, bank[:, 0:TT], bt_,
                           start=False, stop=(dj == DJ - 1))
                    sact(qtmp[:, :], qtmp_t, bank[:, 0:TT], bt_, AF.Relu,
                         scale=-1.0)
                    sact(qtmp[:, :], qtmp_t, qtmp[:, :], qtmp_t, AF.Exp,
                         scale=-1.0)
                    E("vector", lambda nc, d=qtmp2[:, :], s=bank[:, 0:TT],
                      e2=qtmp[:, :]: nc.vector.scalar_tensor_tensor(
                          d, s, 0.0, e2, op0=ALU.max, op1=ALU.add),
                      reads=[bt_, qtmp_t], writes=[qtmp2_t])
                    E("scalar", lambda nc, d=Qhi[:, oc * TT:(oc + 1) * TT],
                      s=qtmp2[:, :]: nc.scalar.copy(d, s),
                      reads=[qtmp2_t], writes=[Qhi_t])
                    vstt(Qlo[:, oc * TT:(oc + 1) * TT], Qlo_t,
                         Qhi[:, oc * TT:(oc + 1) * TT], Qhi_t, -1.0,
                         qtmp2[:, :], qtmp2_t, ALU.mult, ALU.add)
                    drain_fill(4)

            def head2():
                for h in range(H):
                    for c in range(TC):
                        bt_, bank = pb[6 + (h * TC + c) % 2]
                        lh = Qhi[:, h * TT + c * 128:h * TT + (c + 1) * 128]
                        ll = Qlo[:, h * TT + c * 128:h * TT + (c + 1) * 128]
                        rh = kvh[:, h * KVW:(h + 1) * KVW]
                        rl = kvl[:, h * KVW:(h + 1) * KVW]
                        dst = bank[:, 0:KVW]
                        mm(lh, Qhi_t, rh, kvh_t, dst, bt_, start=True, stop=False)
                        mm(lh, Qhi_t, rl, kvl_t, dst, bt_, start=False,
                           stop=False)
                        mm(ll, Qlo_t, rh, kvh_t, dst, bt_, start=False, stop=True)
                        E("vector", lambda nc, d=z_ap[:, :], s=bank[:, 128:129]:
                          nc.vector.tensor_scalar(d, s, EPS_ATTN, None,
                                                  op0=ALU.add),
                          reads=[bt_], writes=[z_tt])
                        E("vector", lambda nc: nc.vector.reciprocal(
                            z_ap[:, :], z_ap[:, :]), reads=[z_tt], writes=[z_tt])
                        xap = xt2[:, c * D + h * 128:c * D + (h + 1) * 128]
                        E("vector", lambda nc, d=xap, s=bank[:, 0:128],
                          xi=xap: nc.vector.scalar_tensor_tensor(
                              d, s, z_ap[:, 0:1], xi, op0=ALU.mult, op1=ALU.add),
                          reads=[bt_, z_tt, xt2_t], writes=[xt2_t])
                        drain_fill(1)

                transpose_fold2(xt2_t, xt2, "g2", "be2")
            def ffnA():

                def a_w1_block(grp):
                    gp = grp % 2
                    hgrp_t, hgrp = hgrpP[gp]
                    hgrphi_t, hgrphi = hgrphiP[gp]
                    hgrplo_t, hgrplo = hgrploP[gp]
                    for gg in range(4):
                        oc = grp * 4 + gg
                        ws = (oc // 2) % 2
                        off = (oc % 2) * D
                        wbufhi_t, wbufhi = wbufhiP[ws]
                        wbuflo_t, wbuflo = wbufloP[ws]
                        if oc % 2 == 0:
                            E("gpsimd", lambda nc, d=wbufhi[:, 0:2 * D],
                              s=s_w1a_hi[:, oc * D:(oc + 2) * D]:
                              nc.gpsimd.dma_start(out=d, in_=s),
                              reads=[scr_t["s_w1a_hi"]], writes=[wbufhi_t],
                              is_dma=True)
                            E("gpsimd", lambda nc, d=wbuflo[:, 0:2 * D],
                              s=s_w1a_lo[:, oc * D:(oc + 2) * D]:
                              nc.gpsimd.dma_start(out=d, in_=s),
                              reads=[scr_t["s_w1a_lo"]], writes=[wbuflo_t],
                              is_dma=True)
                        bt_, bank = pb[4 + oc % 4]
                        first = True
                        for dj in range(DJ):
                            lh = wbufhi[:, off + dj * 128:off + (dj + 1) * 128]
                            ll = wbuflo[:, off + dj * 128:off + (dj + 1) * 128]
                            rh = x2hi2[:, dj * TT:(dj + 1) * TT]
                            rl = x2lo2[:, dj * TT:(dj + 1) * TT]
                            mm(lh, wbufhi_t, rh, x2hi2_t, bank[:, 0:TT], bt_,
                               start=first, stop=False)
                            first = False
                            mm(lh, wbufhi_t, rl, x2lo2_t, bank[:, 0:TT], bt_,
                               start=False, stop=False)
                            mm(ll, wbuflo_t, rh, x2hi2_t, bank[:, 0:TT], bt_,
                               start=False, stop=(dj == DJ - 1))
                        E("scalar", lambda nc, d=hgrp[:, gg * TT:(gg + 1) * TT],
                          s=bank[:, 0:TT], b=b1a[:, oc:oc + 1]:
                          nc.scalar.activation(d, s, AF.Gelu, bias=b, scale=1.0),
                          reads=[bt_, b1a_t], writes=[hgrp_t])
                        E("scalar", lambda nc,
                          d=hgrphi[:, gg * TT:(gg + 1) * TT],
                          s=hgrp[:, gg * TT:(gg + 1) * TT]:
                          nc.scalar.copy(d, s), reads=[hgrp_t],
                          writes=[hgrphi_t])
                        vstt(hgrplo[:, gg * TT:(gg + 1) * TT], hgrplo_t,
                             hgrphi[:, gg * TT:(gg + 1) * TT], hgrphi_t, -1.0,
                             hgrp[:, gg * TT:(gg + 1) * TT], hgrp_t,
                             ALU.mult, ALU.add)

                def a_w2_block(grp):
                    gp = grp % 2
                    hgrphi_t, hgrphi = hgrphiP[gp]
                    hgrplo_t, hgrplo = hgrploP[gp]
                    for gg in range(4):
                        dffc = grp * 4 + gg
                        ws = (dffc // 2) % 2
                        off = (dffc % 2) * D
                        w2bhi_t, w2bhi = w2bhiP[ws]
                        w2blo_t, w2blo = w2bloP[ws]
                        if dffc % 2 == 0:
                            E("gpsimd", lambda nc, d=w2bhi[:, 0:2 * D],
                              s=s_w2a_hi[:, dffc * D:(dffc + 2) * D]:
                              nc.gpsimd.dma_start(out=d, in_=s),
                              reads=[scr_t["s_w2a_hi"]], writes=[w2bhi_t],
                              is_dma=True)
                            E("gpsimd", lambda nc, d=w2blo[:, 0:2 * D],
                              s=s_w2a_lo[:, dffc * D:(dffc + 2) * D]:
                              nc.gpsimd.dma_start(out=d, in_=s),
                              reads=[scr_t["s_w2a_lo"]], writes=[w2blo_t],
                              is_dma=True)
                        for c in range(TC):
                            lh = hgrphi[:, gg * TT + c * 128:
                                        gg * TT + (c + 1) * 128]
                            ll = hgrplo[:, gg * TT + c * 128:
                                        gg * TT + (c + 1) * 128]
                            for og in range(2):
                                bt_, bank = pb[c * 2 + og]
                                rh = w2bhi[:, off + og * 512:
                                           off + og * 512 + 512]
                                rl = w2blo[:, off + og * 512:
                                           off + og * 512 + 512]
                                mm(lh, hgrphi_t, rh, w2bhi_t, bank[:, :], bt_,
                                   start=(dffc == 0), stop=False)
                                mm(lh, hgrphi_t, rl, w2blo_t, bank[:, :], bt_,
                                   start=False, stop=False)
                                mm(ll, hgrplo_t, rh, w2bhi_t, bank[:, :], bt_,
                                   start=False, stop=False)

                a_w1_block(0)
                drain_fill(4)
                for grp in range(1, FJ // 4):
                    a_w1_block(grp)
                    drain_fill(4)
                    a_w2_block(grp - 1)
                    drain_fill(4)
                a_w2_block(FJ // 4 - 1)
                drain_fill()
                for c in range(TC):
                    for og in range(2):
                        bt_, bank = pb[c * 2 + og]
                        mm(ones1[0:1, 0:128], ones1_t,
                           b2a[0:1, og * 512:og * 512 + 512], b2a_t,
                           bank[:, :], bt_, start=False, stop=True)
                        xap = xt2[:, c * D + og * 512:c * D + og * 512 + 512]
                        E("vector", lambda nc, d=xap, a=xap, b=bank[:, :]:
                          nc.vector.tensor_tensor(d, a, b, op=ALU.add),
                          reads=[bt_, xt2_t], writes=[xt2_t])

                for c in range(TC):
                    xap = xt2[:, c * D:(c + 1) * D]
                    E("vector", lambda nc, d=xap, a=xap,
                      b=xorig[:, c * D:(c + 1) * D]:
                      nc.vector.tensor_tensor(d, a, b, op=ALU.add),
                      reads=[xt2_t, xorig_t], writes=[xt2_t])
                    xh2_t, xh2 = xh2P[c % 2]
                    ln_norm(xap, xt2_t, xh2[:, :], xh2_t)
                    vcp(xap, xt2_t, xh2[:, :], xh2_t)

                for c in range(TC):
                    for g0 in range(0, DJ, 4):
                        bt_, bank = pb[6 + (g0 // 4) % 2]
                        for gg in range(4):
                            jj = g0 + gg
                            transp(bank[:, gg * 128:(gg + 1) * 128], bt_,
                                   xt2[:, c * D + jj * 128:c * D + (jj + 1) * 128],
                                   xt2_t)
                        dstf = x2f2[:, :].rearrange(
                            "q (j r) -> q j r", r=TT)[:, g0:g0 + 4,
                                                      c * 128:c * 128 + 128]
                        psrc = bank[:, :].rearrange("q (j r) -> q j r", r=128)
                        E("scalar", lambda nc, d=dstf, s=psrc:
                          nc.scalar.copy(d, s), reads=[bt_], writes=[x2f2_t])
            def ffnQ():
                for blk in range(DJ * TT // QW):
                    quant_chain(x2f2[:, blk * QW:(blk + 1) * QW], x2f2_t,
                                xq2[:, blk * QW:(blk + 1) * QW], xq2_t,
                                QW, 2, -14)

            def ffnB():
                drain_fill()

                def q_w1_block(grp):
                    gp = grp % 2
                    y1g_t, y1g = y1gP[gp]
                    hqg_t, hqg = hqgP[gp]
                    for gg in range(4):
                        oc = grp * 4 + gg
                        ws = (oc // 2) % 2
                        off = (oc % 2) * D
                        wbufhi_t, wbufhi = wbufhiP[ws]
                        if oc % 2 == 0:
                            E("gpsimd", lambda nc, d=wbufhi[:, 0:2 * D],
                              s=s_W1q[:, oc * D:(oc + 2) * D]:
                              nc.gpsimd.dma_start(out=d, in_=s),
                              reads=[scr_t["s_W1q"]], writes=[wbufhi_t],
                              is_dma=True)
                        bt_, bank = pb[4 + oc % 4]
                        first = True
                        for dj in range(DJ):
                            l_ = wbufhi[:, off + dj * 128:off + (dj + 1) * 128]
                            r_ = xq2[:, dj * TT:(dj + 1) * TT]
                            mm(l_, wbufhi_t, r_, xq2_t, bank[:, 0:TT], bt_,
                               start=first, stop=(dj == DJ - 1))
                            first = False
                        E("scalar", lambda nc, d=y1g[:, gg * TT:(gg + 1) * TT],
                          s=bank[:, 0:TT], b=b1fq[:, oc:oc + 1]:
                          nc.scalar.activation(d, s, AF.Identity, bias=b, scale=1.0),
                          reads=[bt_, b1fq_t], writes=[y1g_t])
                    hq_chain(y1g[:, 0:4 * TT], y1g_t, hqg[:, 0:4 * TT], hqg_t,
                             4 * TT)

                def q_w2_block(grp):
                    gp = grp % 2
                    hqg_t, hqg = hqgP[gp]
                    for gg in range(4):
                        dffc = grp * 4 + gg
                        ws = (dffc // 2) % 2
                        off = (dffc % 2) * D
                        w2blo_t, w2blo = w2bloP[ws]
                        if dffc % 2 == 0:
                            E("gpsimd", lambda nc, d=w2blo[:, 0:2 * D],
                              s=s_W2q[:, dffc * D:(dffc + 2) * D]:
                              nc.gpsimd.dma_start(out=d, in_=s),
                              reads=[scr_t["s_W2q"]], writes=[w2blo_t],
                              is_dma=True)
                        for c in range(TC):
                            l_ = hqg[:, gg * TT + c * 128:gg * TT + (c + 1) * 128]
                            for og in range(2):
                                bt_, bank = pb[c * 2 + og]
                                r_ = w2blo[:, off + og * 512:
                                           off + og * 512 + 512]
                                mm(l_, hqg_t, r_, w2blo_t, bank[:, :], bt_,
                                   start=(dffc == 0), stop=False)

                q_w1_block(0)
                for grp in range(1, FJ // 4):
                    q_w1_block(grp)
                    q_w2_block(grp - 1)
                q_w2_block(FJ // 4 - 1)
                for c in range(TC):
                    for og in range(2):
                        bt_, bank = pb[c * 2 + og]
                        mm(ones1[0:1, 0:128], ones1_t,
                           b2fq[0:1, og * 512:og * 512 + 512], b2fq_t,
                           bank[:, :], bt_, start=False, stop=True)
                        vcp(y2f[:, c * D + og * 512:c * D + og * 512 + 512],
                            y2f_t, bank[:, :], bt_)
            def tail():
                for blk in range(TC * D // QW):
                    quant_chain(y2f[:, blk * QW:(blk + 1) * QW], y2f_t,
                                y2f[:, blk * QW:(blk + 1) * QW], y2f_t,
                                QW, 7, -95)
                for c in range(TC):
                    xap = xt2[:, c * D:(c + 1) * D]
                    E("vector", lambda nc, d=y2f[:, c * D:(c + 1) * D],
                      a=y2f[:, c * D:(c + 1) * D], b=dmt[:, c * D:(c + 1) * D]:
                      nc.vector.tensor_tensor(d, a, b, op=ALU.mult),
                      reads=[y2f_t, dmt_t], writes=[y2f_t])
                    E("vector", lambda nc, d=xap, a=xap,
                      b=y2f[:, c * D:(c + 1) * D]:
                      nc.vector.tensor_tensor(d, a, b, op=ALU.add),
                      reads=[xt2_t, y2f_t], writes=[xt2_t])
                    xh2_t, xh2 = xh2P[c % 2]
                    ln_norm(xap, xt2_t, xh2[:, :], xh2_t)
                    vcp(xap, xt2_t, xh2[:, :], xh2_t)
                orow = out[ti * TT:(ti + 1) * TT, :].rearrange("(c q) d -> q c d", q=128)
                dmaout("gpsimd", orow, xt2[:, :].rearrange("q (c d) -> q c d", d=D), xt2_t)
            return head1, head2, ffnA, ffnQ, ffnB, tail

        stages = [tile_body(ti) for ti in range(NTILE)]
        H1, H2, FA, FQ, FB, TL = 0, 1, 2, 3, 4, 5
        stages[0][H1](); stages[0][H2](); stages[0][FA]()
        for ti in range(NTILE):
            p._cap = []
            stages[ti][FQ]()
            fill_q.extend(p._cap); p._cap = None
            if ti + 1 < NTILE:
                stages[ti + 1][H1]()
                stages[ti + 1][H2]()
            stages[ti][FB]()
            p._cap = []
            stages[ti][TL]()
            fill_q.extend(p._cap); p._cap = None
            if ti + 1 < NTILE:
                stages[ti + 1][FA]()
        drain_fill()


        last_dma = [op.idx for op in p.ops
                    if op.is_dma and op.engine == "gpsimd"]
        fidx = p.emit("gpsimd", lambda nc: None, extra_deps=last_dma[-NTILE:])
        p.ops[fidx].virtual = True

    for cm in reversed(scope2):
        cm.__exit__(None, None, None)
    p.finalize()
    with nc.Block() as block:
        @block.sync
        def _(eng):
            p.run_engine(nc, "sync", eng, sems, dma_sems)

        @block.tensor
        def _(eng):
            p.run_engine(nc, "tensor", eng, sems, dma_sems)

        @block.vector
        def _(eng):
            p.run_engine(nc, "vector", eng, sems, dma_sems)

        @block.scalar
        def _(eng):
            p.run_engine(nc, "scalar", eng, sems, dma_sems)

        @block.gpsimd
        def _(eng):
            p.run_engine(nc, "gpsimd", eng, sems, dma_sems)

    for cm in reversed(ctxs):
        cm.__exit__(None, None, None)
    return nc


def kernel(L0_attn_x, L0_attn_wq, L0_attn_wk, L0_attn_wv, L0_attn_w1,
           L0_attn_b1, L0_attn_w2, L0_attn_b2, L0_attn_g1, L0_attn_be1,
           L0_attn_g2, L0_attn_be2, L0_ffn_W1, L0_ffn_b1, L0_ffn_W2,
           L0_ffn_b2):
    x = np.ascontiguousarray(np.asarray(L0_attn_x, dtype=np.float32))
    B, L, C = x.shape

    import os as _os
    nel = B * L * C
    idx = np.arange(nel, dtype=np.uint64)
    o0, o1 = _threefry2x32(np.uint32(0), np.uint32(42),
                           (idx >> np.uint64(32)).astype(np.uint32),
                           (idx & np.uint64(0xffffffff)).astype(np.uint32))
    if _os.environ.get("BASS_SIM_FP64") == "1":
        b64 = (o0.astype(np.uint64) << np.uint64(32)) | o1.astype(np.uint64)
        f = ((b64 >> np.uint64(12)) |
             np.uint64(0x3FF0000000000000)).view(np.float64) - 1.0
        keep = f < 0.8
    else:
        b32 = o0 ^ o1
        f = ((b32 >> np.uint32(9)) |
             np.uint32(0x3f800000)).view(np.float32) - np.float32(1.0)
        keep = f < np.float32(0.8)
    keep = keep.reshape(B, L, C)
    dmask_full = keep.astype(np.float32) * np.float32(1.25)

    def cols(v, nj):
        return np.ascontiguousarray(
            np.asarray(v, np.float32).reshape(nj, 128).T)

    common = {
        "wq": np.ascontiguousarray(np.asarray(L0_attn_wq, np.float32)),
        "wk": np.ascontiguousarray(np.asarray(L0_attn_wk, np.float32)),
        "wv": np.ascontiguousarray(np.asarray(L0_attn_wv, np.float32)),
        "w1a": np.ascontiguousarray(np.asarray(L0_attn_w1, np.float32)),
        "w2a": np.ascontiguousarray(np.asarray(L0_attn_w2, np.float32)),
        "W1f": np.ascontiguousarray(np.asarray(L0_ffn_W1, np.float32)),
        "W2f": np.ascontiguousarray(np.asarray(L0_ffn_W2, np.float32)),
        "b1a_c": cols(L0_attn_b1, FJ),
        "b2a_r": np.ascontiguousarray(
            np.asarray(L0_attn_b2, np.float32).reshape(1, D)),
        "b1f_c": cols(L0_ffn_b1, FJ),
        "b2f_r": np.ascontiguousarray(
            np.asarray(L0_ffn_b2, np.float32).reshape(1, D)),
        "g1_r": np.ascontiguousarray(
            np.asarray(L0_attn_g1, np.float32).reshape(1, D)),
        "be1_r": np.ascontiguousarray(
            np.asarray(L0_attn_be1, np.float32).reshape(1, D)),
        "g2_r": np.ascontiguousarray(
            np.asarray(L0_attn_g2, np.float32).reshape(1, D)),
        "be2_r": np.ascontiguousarray(
            np.asarray(L0_attn_be2, np.float32).reshape(1, D)),
        "eye": np.eye(128, dtype=np.float32),
    }

    in_maps = []
    for c in range(8):
        n, half = c // 2, c % 2
        own = x[n, half * 2048:(half + 1) * 2048]
        other = x[n, (1 - half) * 2048:(2 - half) * 2048]
        m = dict(common)
        m["xb"] = np.ascontiguousarray(np.concatenate([own, other], axis=0))
        m["dmask"] = np.ascontiguousarray(
            dmask_full[n, half * 2048:(half + 1) * 2048].astype(
                ml_dtypes.bfloat16))
        in_maps.append(m)

    nc = build_nc()
    res = run_bass_kernel_spmd(nc, in_maps, core_ids=list(range(8)))
    outp = np.empty((B, L, C), dtype=np.float32)
    for c in range(8):
        n, half = c // 2, c % 2
        outp[n, half * 2048:(half + 1) * 2048] = res.results[c]["out"]
    return outp


# version 23
# speedup vs baseline: 1.0048x; 1.0048x over previous
import numpy as np
import ml_dtypes
import concourse.bass as bass
import concourse.mybir as mybir
from concourse.bass_utils import run_bass_kernel_spmd

F32 = mybir.dt.float32
BF16 = mybir.dt.bfloat16
I32 = mybir.dt.int32
ALU = mybir.AluOpType
AF = mybir.ActivationFunctionType

D = 1024
DJ = 8
H = 8
DFF = 4096
FJ = 32
S = 4096
SC = S // 128
TOWN = 2048
TT = 256
NTILE = TOWN // TT
TC = TT // 128
EPS_LN = 1e-5
EPS_ATTN = 1e-6
KVW = 132

ENGINES = ("sync", "tensor", "vector", "scalar", "gpsimd")
NDMASEM = 4


class Tile:
    __slots__ = ("name", "writes", "readers")

    def __init__(self, name):
        self.name = name
        self.writes = []
        self.readers = []


class Op:
    __slots__ = ("idx", "engine", "build", "deps", "is_dma", "needs_inc",
                 "semval", "seq", "slot", "virtual")

    def __init__(self, idx, engine, build, deps, is_dma):
        self.virtual = False
        self.idx = idx
        self.engine = engine
        self.build = build
        self.deps = deps
        self.is_dma = is_dma
        self.needs_inc = False
        self.semval = None
        self.seq = None
        self.slot = 0


class Prog:
    label = ""

    def __init__(self):
        self.ops = []
        self.labels = []
        self.last_by_engine = {e: None for e in ENGINES}
        self._cap = None

    def emit(self, engine, build, reads=(), writes=(), extra_deps=(),
             is_dma=False):
        if self._cap is not None:
            self._cap.append((engine, build, tuple(reads), tuple(writes),
                              tuple(extra_deps), is_dma))
            return None
        deps = set(extra_deps)
        for t in reads:
            deps.update(t.writes)
        for t in writes:
            deps.update(t.writes)
            deps.update(t.readers)
        idx = len(self.ops)
        op = Op(idx, engine, build, sorted(deps), is_dma)
        self.ops.append(op)
        self.labels.append(Prog.label)
        self.last_by_engine[engine] = idx
        for t in reads:
            if idx not in t.readers:
                t.readers.append(idx)
        for t in writes:
            t.writes = [idx]
            t.readers = []
        return idx

    def barrier(self):
        lasts = [i for i in self.last_by_engine.values() if i is not None]
        for e in ENGINES:
            deps = [i for i in lasts if self.ops[i].engine != e]
            idx = self.emit(e, lambda nc: None, extra_deps=deps)
            self.ops[idx].virtual = True

    def finalize(self):
        dma_rr = {e: 0 for e in ENGINES}
        chains = {}
        for op in self.ops:
            if op.is_dma:
                op.slot = dma_rr[op.engine] % NDMASEM
                dma_rr[op.engine] += 1
                chains.setdefault((op.engine, op.slot), []).append(op.idx)
        consumers = {}
        for op in self.ops:
            for d in op.deps:
                consumers.setdefault(d, []).append(op.idx)
        for lst in chains.values():
            for i in range(1, len(lst)):
                cur = self.ops[lst[i]]
                prev = [c for c in consumers.get(lst[i - 1], ())
                        if c < cur.idx]
                if prev:
                    cur.deps = sorted(set(cur.deps) | set(prev))
        self.slot_chain = chains
        memo = {}

        def expand(i):
            if i in memo:
                return memo[i]
            op = self.ops[i]
            if not op.virtual:
                memo[i] = (i,)
                return memo[i]
            r = []
            for d in op.deps:
                r.extend(expand(d))
            memo[i] = tuple(sorted(set(r)))
            return memo[i]

        for op in self.ops:
            if any(self.ops[d].virtual for d in op.deps):
                nd = set()
                for d in op.deps:
                    nd.update(expand(d))
                op.deps = sorted(nd)
        for op in self.ops:
            for d in op.deps:
                dep = self.ops[d]
                if dep.is_dma or dep.engine != op.engine:
                    dep.needs_inc = True
        cnt = {e: 0 for e in ENGINES}
        dma_cnt = {e: [0] * NDMASEM for e in ENGINES}
        seq = {e: 0 for e in ENGINES}
        for op in self.ops:
            op.seq = seq[op.engine]
            seq[op.engine] += 1
            if op.is_dma:
                dma_cnt[op.engine][op.slot] += 1
                op.semval = dma_cnt[op.engine][op.slot] * 16
                op.needs_inc = True
            elif op.needs_inc:
                cnt[op.engine] += 1
                op.semval = cnt[op.engine]
        self.by_engine = {e: [op for op in self.ops if op.engine == e]
                          for e in ENGINES}

    def run_engine(self, nc, engine, eng, sems, dma_sems):
        waited = {}
        drained_upto = -1
        for op in self.by_engine[engine]:
            need_drain = False
            for d in op.deps:
                dep = self.ops[d]
                if dep.engine == engine and not dep.is_dma:
                    if engine != "tensor" and dep.seq > drained_upto:
                        need_drain = True
                else:
                    key = (dep.engine, dep.is_dma, dep.slot)
                    val = dep.semval
                    assert val is not None, (op.idx, d, dep.engine, engine)
                    if dep.is_dma:
                        ch = self.slot_chain[(dep.engine, dep.slot)]
                        import bisect
                        pos = bisect.bisect_left(ch, op.idx) - 1
                        val = self.ops[ch[pos]].semval
                    if waited.get(key, -1) < val:
                        if dep.is_dma:
                            sem = dma_sems[dep.engine][dep.slot]
                        else:
                            sem = sems[dep.engine]
                        eng.wait_ge(sem, val)
                        waited[key] = val
            if need_drain:
                eng.drain()
                drained_upto = op.seq - 1
            ins = op.build(nc)
            if ins is not None:
                if op.is_dma:
                    ins.then_inc(dma_sems[engine][op.slot], 16)
                elif op.needs_inc:
                    ins.then_inc(sems[engine], 1)



def _threefry2x32(k0, k1, x0, x1):
    def rotl(x, r):
        return ((x << np.uint32(r)) | (x >> np.uint32(32 - r))).astype(np.uint32)
    k0 = np.asarray(k0, np.uint32)
    k1 = np.asarray(k1, np.uint32)
    x0 = np.asarray(x0, np.uint32).copy()
    x1 = np.asarray(x1, np.uint32).copy()
    ks2 = (k0 ^ k1 ^ np.uint32(0x1BD11BDA)).astype(np.uint32)
    x0 = (x0 + k0).astype(np.uint32)
    x1 = (x1 + k1).astype(np.uint32)
    ks = [k0, k1, ks2]
    rot = ((13, 15, 26, 6), (17, 29, 16, 24))
    for i in range(5):
        for r in rot[i % 2]:
            x0 = (x0 + x1).astype(np.uint32)
            x1 = rotl(x1, r) ^ x0
        x0 = (x0 + ks[(i + 1) % 3]).astype(np.uint32)
        x1 = (x1 + ks[(i + 2) % 3] + np.uint32(i + 1)).astype(np.uint32)
    return x0, x1

def build_nc():
    nc = bass.Bass("TRN2", target_bir_lowering=False, debug=False)

    xb = nc.dram_tensor("xb", [S, D], F32, kind="ExternalInput").ap()
    wq = nc.dram_tensor("wq", [D, D], F32, kind="ExternalInput").ap()
    wk = nc.dram_tensor("wk", [D, D], F32, kind="ExternalInput").ap()
    wv = nc.dram_tensor("wv", [D, D], F32, kind="ExternalInput").ap()
    w1a = nc.dram_tensor("w1a", [DFF, D], F32, kind="ExternalInput").ap()
    w2a = nc.dram_tensor("w2a", [D, DFF], F32, kind="ExternalInput").ap()
    W1f = nc.dram_tensor("W1f", [DFF, D], F32, kind="ExternalInput").ap()
    W2f = nc.dram_tensor("W2f", [D, DFF], F32, kind="ExternalInput").ap()
    b1a_c = nc.dram_tensor("b1a_c", [128, FJ], F32, kind="ExternalInput").ap()
    b2a_r = nc.dram_tensor("b2a_r", [1, D], F32, kind="ExternalInput").ap()
    b1f_c = nc.dram_tensor("b1f_c", [128, FJ], F32, kind="ExternalInput").ap()
    b2f_r = nc.dram_tensor("b2f_r", [1, D], F32, kind="ExternalInput").ap()
    g1_r = nc.dram_tensor("g1_r", [1, D], F32, kind="ExternalInput").ap()
    be1_r = nc.dram_tensor("be1_r", [1, D], F32, kind="ExternalInput").ap()
    g2_r = nc.dram_tensor("g2_r", [1, D], F32, kind="ExternalInput").ap()
    be2_r = nc.dram_tensor("be2_r", [1, D], F32, kind="ExternalInput").ap()
    dmask = nc.dram_tensor("dmask", [TOWN, D], BF16, kind="ExternalInput").ap()
    eye_in = nc.dram_tensor("eye", [128, 128], F32, kind="ExternalInput").ap()
    out = nc.dram_tensor("out", [TOWN, D], F32, kind="ExternalOutput").ap()

    s_wq_hi = nc.dram_tensor("s_wq_hi", [128, DJ * D], BF16, kind="Internal").ap()
    s_wq_lo = nc.dram_tensor("s_wq_lo", [128, DJ * D], BF16, kind="Internal").ap()
    s_w1a_hi = nc.dram_tensor("s_w1a_hi", [128, FJ * D], BF16, kind="Internal").ap()
    s_w1a_lo = nc.dram_tensor("s_w1a_lo", [128, FJ * D], BF16, kind="Internal").ap()
    s_w2a_hi = nc.dram_tensor("s_w2a_hi", [128, FJ * D], BF16, kind="Internal").ap()
    s_w2a_lo = nc.dram_tensor("s_w2a_lo", [128, FJ * D], BF16, kind="Internal").ap()
    s_W1q = nc.dram_tensor("s_W1q", [128, FJ * D], BF16, kind="Internal").ap()
    s_W2q = nc.dram_tensor("s_W2q", [128, FJ * D], BF16, kind="Internal").ap()

    p = Prog()
    ctxs = []
    tiles = {}
    scr_t = {nm: Tile(nm) for nm in
             ("s_wq_hi", "s_wq_lo", "s_w1a_hi", "s_w1a_lo", "s_w2a_hi",
              "s_w2a_lo", "s_W1q", "s_W2q")}

    def sb(name, shape, dtype):
        cm = nc.sbuf_tensor(name, shape, dtype)
        h = cm.__enter__()
        ctxs.append(cm)
        t = Tile(name)
        tiles[name] = (t, h.ap())
        return t, h.ap()

    def psum(name):
        cm = nc.psum_tensor(name, [128, 512], F32)
        h = cm.__enter__()
        ctxs.append(cm)
        t = Tile(name)
        return t, h.ap()

    def semaphore(name):
        cm = nc.semaphore(name)
        s = cm.__enter__()
        ctxs.append(cm)
        return s

    sems = {e: semaphore("sem_" + e) for e in ENGINES}
    dma_sems = {e: [semaphore(f"dma_{e}{i}") for i in range(NDMASEM)]
                for e in ENGINES}

    eye_t, eye = sb("eyet", [128, 128], F32)
    ones1_t, ones1 = sb("ones1", [1, 128], F32)
    b1a_t, b1a = sb("b1a_t", [128, FJ], F32)
    b1f_t, b1f = sb("b1f_t", [128, FJ], F32)
    b1fq_t, b1fq = sb("b1fq_t", [128, FJ], F32)
    b2a_t, b2a = sb("b2a_row", [1, D], F32)
    b2f_t, b2f = sb("b2f_row", [1, D], F32)
    b2fq_t, b2fq = sb("b2fq_row", [1, D], F32)
    grow = {}
    for nm in ("g1", "be1", "g2", "be2"):
        grow[nm] = sb(nm + "row", [1, D], F32)
    gbc = {}
    for nm in ("g1", "be1"):
        gbc[nm] = sb(nm + "bc", [128, D], F32)
    QW = 1024
    qI = [sb(f"qI{i}", [128, QW], I32) for i in range(4)]
    qF = [sb(f"qF{i}", [128, QW], F32) for i in range(3)]
    kvh_t, kvh = sb("kv_hi", [128, H * KVW], BF16)
    kvl_t, kvl = sb("kv_lo", [128, H * KVW], BF16)
    z_tt, z_ap = sb("z_t", [128, 1], F32)
    pb = [psum(f"pb{i}") for i in range(8)]

    def E(engine, fn, reads=(), writes=(), extra=(), is_dma=False):
        return p.emit(engine, fn, reads=reads, writes=writes,
                      extra_deps=extra, is_dma=is_dma)

    fill_q = []

    def drain_fill(n=10 ** 9):
        while fill_q and n > 0:
            a = fill_q.pop(0)
            cap, p._cap = p._cap, None
            p.emit(a[0], a[1], reads=a[2], writes=a[3], extra_deps=a[4],
                   is_dma=a[5])
            p._cap = cap
            n -= 1

    def vts(dst, dt_, src, st_, s1, s2, op0, op1=None, sc=()):
        if op1 is None:
            fn = lambda nc, d=dst, s=src, a=s1, o0=op0: \
                nc.vector.tensor_scalar(d, s, a, None, op0=o0)
        else:
            fn = lambda nc, d=dst, s=src, a=s1, b=s2, o0=op0, o1=op1: \
                nc.vector.tensor_scalar(d, s, a, b, op0=o0, op1=o1)
        return E("vector", fn, reads=[st_] + list(sc), writes=[dt_])

    def vtt(dst, dt_, a, at_, b, bt_, op):
        fn = lambda nc, d=dst, x=a, y=b, o=op: \
            nc.vector.tensor_tensor(d, x, y, op=o)
        return E("vector", fn, reads=[at_, bt_], writes=[dt_])

    def vstt(dst, dt_, in0, t0, scalar, in1, t1, op0, op1, sc=()):
        fn = lambda nc, d=dst, x=in0, s=scalar, y=in1, o0=op0, o1=op1: \
            nc.vector.scalar_tensor_tensor(d, x, s, y, op0=o0, op1=o1)
        return E("vector", fn, reads=[t0, t1] + list(sc), writes=[dt_])

    def vcp(dst, dt_, src, st_):
        fn = lambda nc, d=dst, s=src: nc.vector.tensor_copy(d, s)
        return E("vector", fn, reads=[st_], writes=[dt_])

    def sact(dst, dt_, src, st_, func, bias=None, scale=1.0, bt_=None):
        if bias is None:
            fn = lambda nc, d=dst, s=src, f=func, sc_=scale: \
                nc.scalar.activation(d, s, f, scale=sc_)
        else:
            fn = lambda nc, d=dst, s=src, f=func, b=bias, sc_=scale: \
                nc.scalar.activation(d, s, f, bias=b, scale=sc_)
        rd = [st_] + ([bt_] if bt_ is not None else [])
        return E("scalar", fn, reads=rd, writes=[dt_])

    def dmain(queue, dst, dt_, src):
        fn = lambda nc, d=dst, s=src: \
            getattr(nc, queue).dma_start(out=d, in_=s)
        return E(queue, fn, writes=[dt_], is_dma=True)

    def dmaout(queue, dst, src, st_):
        fn = lambda nc, d=dst, s=src: \
            getattr(nc, queue).dma_start(out=d, in_=s)
        return E(queue, fn, reads=[st_], is_dma=True)

    def mm(lhs, lt_, rhs, rt_, dst, dt_, start, stop):
        fn = lambda nc, l=lhs, r=rhs, d=dst, s0=start, s1=stop: \
            nc.tensor.matmul(d, l, r, start=s0, stop=s1)
        return E("tensor", fn, reads=[lt_, rt_], writes=[dt_])

    def transp(dst, dt_, src, st_):
        fn = lambda nc, d=dst, s=src: nc.tensor.transpose(d, s, eye[:, :])
        return E("tensor", fn, reads=[st_, eye_t], writes=[dt_])

    def memset(t_, ap_, val):
        fn = lambda nc, a=ap_, v=val: nc.gpsimd.memset(a, v)
        return E("gpsimd", fn, writes=[t_])

    def quant_chain(src, st_, dst, dt_, W, Mb, emin, P=128):
        I1 = qI[0][1][0:P, 0:W]; i1 = qI[0][0]
        I2 = qI[1][1][0:P, 0:W]; i2 = qI[1][0]
        I3 = qI[2][1][0:P, 0:W]; i3 = qI[2][0]
        I4 = qI[3][1][0:P, 0:W]; i4 = qI[3][0]
        F1 = qF[0][1][0:P, 0:W]; f1 = qF[0][0]
        F2 = qF[1][1][0:P, 0:W]; f2 = qF[1][0]
        F3 = qF[2][1][0:P, 0:W]; f3 = qF[2][0]
        bits = src.bitcast(I32)
        vts(I1, i1, bits, st_, 23, 255, ALU.logical_shift_right,
            ALU.bitwise_and)
        vts(I2, i2, I1, i1, 127 + emin, -1, ALU.max, ALU.mult)
        vts(I3, i3, I2, i2, Mb - 1 + 254, 8388608, ALU.add, ALU.mult)
        vts(I1, i1, bits, st_, -2147483648, None, ALU.bitwise_and)
        vts(I4, i4, I2, i2, -1, -Mb, ALU.mult, ALU.add)
        vstt(I4, i4, I4, i4, 8388608, I1, i1, ALU.mult, ALU.add)
        vtt(F1, f1, src, st_, I3.bitcast(F32), i3, ALU.mult)
        vstt(F2, f2, F1, f1, -1.0, F1, f1, ALU.mult, ALU.max)
        vcp(I3, i3, F2, f2)
        vcp(F3, f3, I3, i3)
        vtt(F1, f1, F2, f2, F3, f3, ALU.subtract)
        vts(F2, f2, F1, f1, -0.25, -1.0, ALU.is_ge, ALU.add)
        vstt(F2, f2, F1, f1, 0.25, F2, f2, ALU.is_gt, ALU.add)
        vstt(F2, f2, F1, f1, 0.75, F2, f2, ALU.is_ge, ALU.add)
        vstt(F3, f3, F3, f3, 2.0, F2, f2, ALU.mult, ALU.add)
        vtt(dst, dt_, F3, f3, I4.bitcast(F32), i4, ALU.mult)

    def hq_chain(src, st_, dst, dt_, W):
        I1 = qI[0][1][:, 0:W]; i1 = qI[0][0]
        I2 = qI[1][1][:, 0:W]; i2 = qI[1][0]
        I3 = qI[2][1][:, 0:W]; i3 = qI[2][0]
        I4 = qI[3][1][:, 0:W]; i4 = qI[3][0]
        F1 = qF[0][1][:, 0:W]; f1 = qF[0][0]
        F2 = qF[1][1][:, 0:W]; f2 = qF[1][0]
        F3 = qF[2][1][:, 0:W]; f3 = qF[2][0]
        bits = src.bitcast(I32)
        vts(I1, i1, bits, st_, 23, 255, ALU.logical_shift_right,
            ALU.bitwise_and)
        vts(I2, i2, I1, i1, 113, -1, ALU.max, ALU.mult)
        vts(I3, i3, I1, i1, 113, 8, ALU.min, ALU.add)
        vts(I3, i3, I3, i3, 8388608, None, ALU.mult)
        vts(I1, i1, I2, i2, 2 + 254, 8388608, ALU.add, ALU.mult)
        vts(I4, i4, I2, i2, -1, -2, ALU.mult, ALU.add)
        vts(I4, i4, I4, i4, 8388608, None, ALU.mult)
        vtt(F1, f1, src, st_, I1.bitcast(F32), i1, ALU.mult)
        vtt(F2, f2, F1, f1, I3.bitcast(F32), i3, ALU.subtract)
        vtt(F3, f3, F1, f1, I3.bitcast(F32), i3, ALU.add)
        vts(F1, f1, F2, f2, 0.5, None, ALU.is_gt)
        for j in range(1, 8):
            if j % 2 == 0:
                vstt(F1, f1, F2, f2, j + 0.5, F1, f1, ALU.is_gt, ALU.add)
            else:
                vstt(F1, f1, F3, f3, j + 0.5, F1, f1, ALU.is_ge, ALU.add)
        vtt(dst, dt_, F1, f1, I4.bitcast(F32), i4, ALU.mult)

    NLN = 4
    lnp = [sb(f"lnp{i}", [128, 48], F32) for i in range(NLN)]
    lnpi = [sb(f"lnpi{i}", [128, 2], I32) for i in range(NLN)]
    _ln_rr = [0]

    def ln_multi(jobs):
        nch = len(jobs)
        assert nch <= 2
        sl = _ln_rr[0] % NLN
        _ln_rr[0] += 1
        pt_, pp = lnp[sl]
        it_, ip = lnpi[sl]
        for c, (xap, xt_, _, _) in enumerate(jobs):
            E("vector", lambda nc, d=pp[:, 16 * c:16 * c + 6],
              s=xap[:, 0:512]: nc.vector.bn_stats(d, s),
              reads=[xt_], writes=[pt_])
            E("vector", lambda nc, d=pp[:, 16 * c + 6:16 * c + 12],
              s=xap[:, 512:1024]: nc.vector.bn_stats(d, s),
              reads=[xt_], writes=[pt_])
            E("vector", lambda nc, d=pp[:, 16 * c + 12:16 * c + 14],
              s=pp[:, 16 * c:16 * c + 12]: nc.vector.bn_aggr(d, s),
              reads=[pt_], writes=[pt_])
        VA = pp[:, 13:13 + 16 * nch].rearrange("q (c e) -> q c e",
                                               e=16)[:, :, 0:1]
        T1 = pp[:, 40:40 + nch]
        R = pp[:, 42:42 + nch]
        S = pp[:, 44:44 + nch]
        E("vector", lambda nc, d=T1, s=VA: nc.vector.tensor_scalar(
            d, s, EPS_LN, None, op0=ALU.add), reads=[pt_], writes=[pt_])
        E("vector", lambda nc, d=ip[:, 0:nch], s=T1.bitcast(I32):
          nc.vector.tensor_scalar(d, s, 1, None,
                                  op0=ALU.logical_shift_right),
          reads=[pt_], writes=[it_])
        E("vector", lambda nc, d=ip[:, 0:nch], s=ip[:, 0:nch]:
          nc.vector.tensor_scalar(d, s, -1, 0x5f3759df, op0=ALU.mult,
                                  op1=ALU.add), reads=[it_], writes=[it_])
        E("vector", lambda nc, d=R, s=ip[:, 0:nch].bitcast(F32):
          nc.vector.tensor_copy(d, s), reads=[it_], writes=[pt_])
        for _ in range(2):
            E("vector", lambda nc, d=S, a=R, b=R:
              nc.vector.tensor_tensor(d, a, b, op=ALU.mult),
              reads=[pt_], writes=[pt_])
            E("vector", lambda nc, d=S, a=S, b=T1:
              nc.vector.tensor_tensor(d, a, b, op=ALU.mult),
              reads=[pt_], writes=[pt_])
            E("vector", lambda nc, d=S, s=S: nc.vector.tensor_scalar(
                  d, s, -0.5, 1.5, op0=ALU.mult, op1=ALU.add),
              reads=[pt_], writes=[pt_])
            E("vector", lambda nc, d=R, a=R, b=S:
              nc.vector.tensor_tensor(d, a, b, op=ALU.mult),
              reads=[pt_], writes=[pt_])
        NB = pp[:, 46:46 + nch]
        for c in range(nch):
            E("vector", lambda nc, d=NB[:, c:c + 1],
              m_=pp[:, 16 * c + 12:16 * c + 13], r_=R[:, c:c + 1]:
              nc.vector.scalar_tensor_tensor(d, m_, -1.0, r_, op0=ALU.mult,
                                             op1=ALU.mult),
              reads=[pt_], writes=[pt_])
        for c, (xap, xt_, dstap, dt_) in enumerate(jobs):
            E("scalar", lambda nc, d=dstap, s=xap, b=NB[:, c:c + 1],
              r_=R[:, c:c + 1]:
              nc.scalar.activation(d, s, AF.Identity, bias=b, scale=r_),
              reads=[xt_, pt_], writes=[dt_])

    def ln_norm(xap, xt_, dstap, dt_):
        ln_multi([(xap, xt_, dstap, dt_)])

    dmain("sync", eye[:, :], eye_t, eye_in[:, :])
    dmain("sync", b1a[:, :], b1a_t, b1a_c[:, :])
    dmain("sync", b1f[:, :], b1f_t, b1f_c[:, :])
    dmain("sync", b2a[0:1, :], b2a_t, b2a_r[:, :])
    dmain("sync", b2f[0:1, :], b2f_t, b2f_r[:, :])
    dmain("sync", grow["g1"][1][0:1, :], grow["g1"][0], g1_r[:, :])
    dmain("sync", grow["be1"][1][0:1, :], grow["be1"][0], be1_r[:, :])
    dmain("sync", grow["g2"][1][0:1, :], grow["g2"][0], g2_r[:, :])
    dmain("sync", grow["be2"][1][0:1, :], grow["be2"][0], be2_r[:, :])
    memset(ones1_t, ones1[0:1, :], 1.0)
    for nm in ("g1", "be1"):
        row_t, row_ap = grow[nm]
        bc_t, bc_ap = gbc[nm]
        for og in range(2):
            bt_, bank = pb[5]
            mm(ones1[0:1, 0:128], ones1_t,
               row_ap[0:1, og * 512:og * 512 + 512], row_t,
               bank[:, :], bt_, start=True, stop=True)
            E("scalar", lambda nc, d=bc_ap[:, og * 512:og * 512 + 512],
              s=bank[:, :]: nc.scalar.copy(d, s),
              reads=[bt_], writes=[bc_t])
    quant_chain(b1f[:, 0:FJ], b1f_t, b1fq[:, 0:FJ], b1fq_t, FJ, 2, -14)
    quant_chain(b2f[0:1, 0:D], b2f_t, b2fq[0:1, 0:D], b2fq_t, D, 2, -14, P=1)

    scope1 = []
    def _enter1(name, shape, dtype):
        cm = nc.sbuf_tensor(name, shape, dtype)
        h = cm.__enter__(); scope1.append(cm); return h
    wtmp_h = [_enter1(f"wtmp{i}", [128, D], F32) for i in range(2)]
    pthi_h = [_enter1(f"pthi{i}", [128, D], BF16) for i in range(2)]
    ptlo_h = [_enter1(f"ptlo{i}", [128, D], BF16) for i in range(2)]
    ptf_h = [_enter1(f"ptf{i}", [128, D], F32) for i in range(2)]
    wkhi_h = _enter1("wkT_hi", [128, DJ * D], BF16)
    wklo_h = _enter1("wkT_lo", [128, DJ * D], BF16)
    wvhi_h = _enter1("wvT_hi", [128, DJ * D], BF16)
    wvlo_h = _enter1("wvT_lo", [128, DJ * D], BF16)
    p1xs_h = [_enter1(f"p1xs{i}", [128, D], F32) for i in range(2)]
    p1xh_h = [_enter1(f"p1xh{i}", [128, D], F32) for i in range(2)]
    p1x2hi_h = [_enter1(f"p1x2hi{i}", [128, D], BF16) for i in range(2)]
    p1x2lo_h = [_enter1(f"p1x2lo{i}", [128, D], BF16) for i in range(2)]
    p1khi_h = [_enter1(f"p1khi{i}", [128, D], BF16) for i in range(2)]
    p1klo_h = [_enter1(f"p1klo{i}", [128, D], BF16) for i in range(2)]
    p1vhi_h = [_enter1(f"p1vhi{i}", [128, H * KVW], BF16) for i in range(2)]
    p1vlo_h = [_enter1(f"p1vlo{i}", [128, H * KVW], BF16) for i in range(2)]
    p1tmp_h = [_enter1("p1tmp0", [128, 512], F32)] * 2
    p1tmpk_h = _enter1("p1tmpk", [128, 512], F32)
    if True:
        wtmpP = [(Tile(f"wtmp{i}"), wtmp_h[i].ap()) for i in range(2)]
        pthiP = [(Tile(f"pthi{i}"), pthi_h[i].ap()) for i in range(2)]
        ptloP = [(Tile(f"ptlo{i}"), ptlo_h[i].ap()) for i in range(2)]
        ptfP = [(Tile(f"ptf{i}"), ptf_h[i].ap()) for i in range(2)]
        wkhi_t, wkhi = Tile("wkhi"), wkhi_h.ap()
        wklo_t, wklo = Tile("wklo"), wklo_h.ap()
        wvhi_t, wvhi = Tile("wvhi"), wvhi_h.ap()
        wvlo_t, wvlo = Tile("wvlo"), wvlo_h.ap()

        def transpose_blocks(src, st_, nsub, sink, banks=(6, 7)):
            for g0 in range(0, nsub, 4):
                bt_, bank = pb[banks[(g0 // 4) % len(banks)]]
                gn = min(4, nsub - g0)
                for gg in range(gn):
                    jj = g0 + gg
                    transp(bank[:, gg * 128:(gg + 1) * 128], bt_,
                           src[:, jj * 128:(jj + 1) * 128], st_)
                sink(g0, gn, bank, bt_)

        _prep_rr = [0]

        def prep_split_one(src_dram, oc, strided_in, dst_hi, dst_lo,
                           dst_hi_t, dst_lo_t, res_hi=None, res_hi_t=None,
                           res_lo=None, res_lo_t=None):
            pi = _prep_rr[0] % 2
            _prep_rr[0] += 1
            wtmp_t, wtmp = wtmpP[pi]
            pthi_t, pthi = pthiP[pi]
            ptlo_t, ptlo = ptloP[pi]
            if strided_in:
                src_ap = src_dram[:, oc * 128:(oc + 1) * 128]
                dmain("scalar",
                      wtmp[:, 0:D].rearrange("q (s c) -> q s c", c=128),
                      wtmp_t,
                      src_ap.rearrange("(s q) c -> q s c", q=128))
            else:
                dmain("scalar", wtmp[:, 0:D], wtmp_t,
                      src_dram[oc * 128:(oc + 1) * 128, :])

            def sink(g0, gn, bank, bt_):
                w = gn * 128
                vcp(pthi[:, g0 * 128:g0 * 128 + w], pthi_t,
                    bank[:, 0:w], bt_)
                vstt(ptlo[:, g0 * 128:g0 * 128 + w], ptlo_t,
                     pthi[:, g0 * 128:g0 * 128 + w], pthi_t, -1.0,
                     bank[:, 0:w], bt_, ALU.mult, ALU.add)
            transpose_blocks(wtmp[:, 0:D], wtmp_t, DJ, sink, banks=(6,))
            if dst_hi is not None:
                E("gpsimd", lambda nc, d=dst_hi[:, oc * D:(oc + 1) * D],
                  s=pthi[:, 0:D]: nc.gpsimd.dma_start(out=d, in_=s),
                  reads=[pthi_t], writes=[dst_hi_t], is_dma=True)
                E("gpsimd", lambda nc, d=dst_lo[:, oc * D:(oc + 1) * D],
                  s=ptlo[:, 0:D]: nc.gpsimd.dma_start(out=d, in_=s),
                  reads=[ptlo_t], writes=[dst_lo_t], is_dma=True)
            else:
                dh = res_hi[:, :].rearrange("q (j d) -> q j d", d=D)[
                    :, :, oc * 128:(oc + 1) * 128]
                dl = res_lo[:, :].rearrange("q (j d) -> q j d", d=D)[
                    :, :, oc * 128:(oc + 1) * 128]
                sh = pthi[:, 0:D].rearrange("q (j r) -> q j r", r=128)
                sl = ptlo[:, 0:D].rearrange("q (j r) -> q j r", r=128)
                E("vector", lambda nc, d=dh, s=sh: nc.vector.tensor_copy(
                    d, s), reads=[pthi_t], writes=[res_hi_t])
                E("vector", lambda nc, d=dl, s=sl: nc.vector.tensor_copy(
                    d, s), reads=[ptlo_t], writes=[res_lo_t])

        def prep_quant_one(src_dram, oc, strided_in, dst, dst_t):
            pi = _prep_rr[0] % 2
            _prep_rr[0] += 1
            wtmp_t, wtmp = wtmpP[pi]
            pthi_t, pthi = pthiP[pi]
            ptf_t, ptf = ptfP[pi]
            if strided_in:
                src_ap = src_dram[:, oc * 128:(oc + 1) * 128]
                dmain("scalar",
                      wtmp[:, 0:D].rearrange("q (s c) -> q s c", c=128),
                      wtmp_t,
                      src_ap.rearrange("(s q) c -> q s c", q=128))
            else:
                dmain("scalar", wtmp[:, 0:D], wtmp_t,
                      src_dram[oc * 128:(oc + 1) * 128, :])
            quant_chain(wtmp[:, 0:D], wtmp_t, ptf[:, 0:D], ptf_t, D, 2, -14)

            def qsink(g0, gn, bank, bt_):
                w = gn * 128
                vcp(pthi[:, g0 * 128:g0 * 128 + w], pthi_t,
                    bank[:, 0:w], bt_)
            transpose_blocks(ptf[:, 0:D], ptf_t, DJ, qsink, banks=(6,))
            E("gpsimd", lambda nc, d=dst[:, oc * D:(oc + 1) * D],
              s=pthi[:, 0:D]: nc.gpsimd.dma_start(out=d, in_=s),
              reads=[pthi_t], writes=[dst_t], is_dma=True)

        for oc in range(DJ):
            prep_split_one(wk, oc, False, None, None, None, None,
                           wkhi, wkhi_t, wklo, wklo_t)
        for oc in range(DJ):
            prep_split_one(wv, oc, False, None, None, None, None,
                           wvhi, wvhi_t, wvlo, wvlo_t)
        prep_units = []
        for oc in range(DJ):
            prep_units.append(lambda oc=oc: prep_split_one(
                wq, oc, False, s_wq_hi, s_wq_lo,
                scr_t["s_wq_hi"], scr_t["s_wq_lo"]))
        for oc in range(FJ):
            prep_units.append(lambda oc=oc: prep_split_one(
                w1a, oc, False, s_w1a_hi, s_w1a_lo,
                scr_t["s_w1a_hi"], scr_t["s_w1a_lo"]))
        for oc in range(FJ):
            prep_units.append(lambda oc=oc: prep_split_one(
                w2a, oc, True, s_w2a_hi, s_w2a_lo,
                scr_t["s_w2a_hi"], scr_t["s_w2a_lo"]))
        for oc in range(FJ):
            prep_units.append(lambda oc=oc: prep_quant_one(
                W1f, oc, False, s_W1q, scr_t["s_W1q"]))
        for oc in range(FJ):
            prep_units.append(lambda oc=oc: prep_quant_one(
                W2f, oc, True, s_W2q, scr_t["s_W2q"]))
        for u in prep_units[:DJ]:
            u()
        prep_units = prep_units[DJ:]

        p1tmp_shared = (Tile("p1tmp0"), p1tmp_h[0].ap())
        p1tmpk_shared = (Tile("p1tmpk"), p1tmpk_h.ap())
        P1 = [dict(
            tmp=p1tmp_shared,
            tmpk=p1tmpk_shared,
            xs=(Tile(f"p1xs{i}"), p1xs_h[i].ap()),
            xh=(Tile(f"p1xh{i}"), p1xh_h[i].ap()),
            x2hi=(Tile(f"p1x2hi{i}"), p1x2hi_h[i].ap()),
            x2lo=(Tile(f"p1x2lo{i}"), p1x2lo_h[i].ap()),
            khi=(Tile(f"p1khi{i}"), p1khi_h[i].ap()),
            klo=(Tile(f"p1klo{i}"), p1klo_h[i].ap()),
            vhi=(Tile(f"p1vhi{i}"), p1vhi_h[i].ap()),
            vlo=(Tile(f"p1vlo{i}"), p1vlo_h[i].ap()),
        ) for i in range(2)]
        for i in range(2):
            memset(P1[i]["vhi"][0], P1[i]["vhi"][1][:, :], 0.0)
            memset(P1[i]["vlo"][0], P1[i]["vlo"][1][:, :], 0.0)
            for h in range(H):
                memset(P1[i]["vhi"][0],
                       P1[i]["vhi"][1][:, h * KVW + 128:h * KVW + 129], 1.0)

        kv_started = [False, False, False]
        nprep = len(prep_units)

        def p1_ln(si):
            par = P1[si % 2]
            p1xs_t, p1xs = par["xs"]
            p1xh_t, p1xh = par["xh"]
            dmain("sync", p1xs[:, :], p1xs_t, xb[si * 128:(si + 1) * 128, :])
            ln_norm(p1xs, p1xs_t, p1xh[:, :], p1xh_t)
            E("vector", lambda nc, d=p1xh[:, :], a=p1xh[:, :],
              b=gbc["g1"][1][:, :]: nc.vector.tensor_tensor(
                  d, a, b, op=ALU.mult),
              reads=[p1xh_t, gbc["g1"][0]], writes=[p1xh_t])
            E("vector", lambda nc, d=p1xh[:, :], a=p1xh[:, :],
              b=gbc["be1"][1][:, :]: nc.vector.tensor_tensor(
                  d, a, b, op=ALU.add),
              reads=[p1xh_t, gbc["be1"][0]], writes=[p1xh_t])

        def p1_tr(si):
            par = P1[si % 2]
            p1xh_t, p1xh = par["xh"]
            p1x2hi_t, p1x2hi = par["x2hi"]
            p1x2lo_t, p1x2lo = par["x2lo"]

            def split_sink_round(g0, bank, bt_):
                E("scalar", lambda nc, d=p1x2hi[:, g0 * 128:g0 * 128 + 512],
                  s=bank[:, :]: nc.scalar.copy(d, s),
                  reads=[bt_], writes=[p1x2hi_t])
                E("vector", lambda nc, d=p1x2lo[:, g0 * 128:g0 * 128 + 512],
                  a=p1x2hi[:, g0 * 128:g0 * 128 + 512], b=bank[:, :]:
                  nc.vector.scalar_tensor_tensor(
                      d, a, -1.0, b, op0=ALU.mult, op1=ALU.add),
                  reads=[bt_, p1x2hi_t], writes=[p1x2lo_t])
            for g0 in range(0, DJ, 4):
                bt_, bank = pb[7 if (g0 // 4) % 2 == 0 else 6]
                for gg in range(4):
                    jj = g0 + gg
                    transp(bank[:, gg * 128:(gg + 1) * 128], bt_,
                           p1xh[:, jj * 128:(jj + 1) * 128], p1xh_t)
                split_sink_round(g0, bank, bt_)

        def p1_kv(si):
            par = P1[si % 2]
            p1x2hi_t, p1x2hi = par["x2hi"]
            p1x2lo_t, p1x2lo = par["x2lo"]
            p1khi_t, p1khi = par["khi"]
            p1klo_t, p1klo = par["klo"]
            p1vhi_t, p1vhi = par["vhi"]
            p1vlo_t, p1vlo = par["vlo"]
            p1tmp_t, p1tmp = par["tmp"]
            p1tmpk_t, p1tmpk = par["tmpk"]
            for (whi_ap, whi_t2, wlo_ap, wlo_t2, isk) in (
                    (wkhi, wkhi_t, wklo, wklo_t, True),
                    (wvhi, wvhi_t, wvlo, wvlo_t, False)):
                for og in range(2):
                    if isk:
                        bt_, bank = pb[3 + og]
                    else:
                        bt_, bank = pb[5 if og == 0 else 3]
                    first = True
                    for dj in range(DJ):
                        lh = p1x2hi[:, dj * 128:(dj + 1) * 128]
                        ll = p1x2lo[:, dj * 128:(dj + 1) * 128]
                        rh = whi_ap[:, dj * D + og * 512:
                                    dj * D + og * 512 + 512]
                        rl = wlo_ap[:, dj * D + og * 512:
                                    dj * D + og * 512 + 512]
                        mm(lh, p1x2hi_t, rh, whi_t2, bank[:, :], bt_,
                           start=first, stop=False)
                        first = False
                        mm(lh, p1x2hi_t, rl, wlo_t2, bank[:, :], bt_,
                           start=False, stop=False)
                        mm(ll, p1x2lo_t, rh, whi_t2, bank[:, :], bt_,
                           start=False, stop=(dj == DJ - 1))
                    if isk:
                        sact(p1tmp[:, :], p1tmp_t, bank[:, :], bt_, AF.Relu,
                             scale=-1.0)
                        sact(p1tmp[:, :], p1tmp_t, p1tmp[:, :], p1tmp_t,
                             AF.Exp, scale=-1.0)
                        E("vector", lambda nc, d=p1tmpk[:, :],
                          s=bank[:, :], e2=p1tmp[:, :]:
                          nc.vector.scalar_tensor_tensor(
                              d, s, 0.0, e2, op0=ALU.max, op1=ALU.add),
                          reads=[bt_, p1tmp_t], writes=[p1tmpk_t])
                        E("scalar", lambda nc,
                          d=p1khi[:, og * 512:og * 512 + 512],
                          s=p1tmpk[:, :]: nc.scalar.copy(d, s),
                          reads=[p1tmpk_t], writes=[p1khi_t])
                        E("vector", lambda nc,
                          d=p1klo[:, og * 512:og * 512 + 512],
                          a=p1khi[:, og * 512:og * 512 + 512],
                          b=p1tmpk[:, :]: nc.vector.scalar_tensor_tensor(
                              d, a, -1.0, b, op0=ALU.mult, op1=ALU.add),
                          reads=[p1tmpk_t, p1khi_t], writes=[p1klo_t])
                    else:
                        h0 = og * 4
                        vdst = p1vhi[:, h0 * KVW:(h0 + 4) * KVW].rearrange(
                            "q (h w) -> q h w", w=KVW)[:, :, 0:128]
                        ldst = p1vlo[:, h0 * KVW:(h0 + 4) * KVW].rearrange(
                            "q (h w) -> q h w", w=KVW)[:, :, 0:128]
                        psrc = bank[:, :].rearrange("q (h w) -> q h w", w=128)
                        E("scalar", lambda nc, d=vdst, s=psrc:
                          nc.scalar.copy(d, s), reads=[bt_], writes=[p1vhi_t])
                        E("vector", lambda nc, d=ldst, a=vdst, b=psrc:
                          nc.vector.scalar_tensor_tensor(
                              d, a, -1.0, b, op0=ALU.mult, op1=ALU.add),
                          reads=[bt_, p1vhi_t], writes=[p1vlo_t])
            for h in range(H):
                bt_, bank = pb[h // 3]
                off = (h % 3) * KVW
                dst = bank[:, off:off + KVW]
                lh = p1khi[:, h * 128:(h + 1) * 128]
                ll = p1klo[:, h * 128:(h + 1) * 128]
                rh = p1vhi[:, h * KVW:(h + 1) * KVW]
                rl = p1vlo[:, h * KVW:(h + 1) * KVW]
                st = not kv_started[h // 3]
                kv_started[h // 3] = True
                last = (si == SC - 1) and (h % 3 == 2 or h == 7)
                mm(lh, p1khi_t, rh, p1vhi_t, dst, bt_, start=st, stop=False)
                mm(lh, p1khi_t, rl, p1vlo_t, dst, bt_, start=False,
                   stop=False)
                mm(ll, p1klo_t, rh, p1vhi_t, dst, bt_, start=False,
                   stop=last)

        p1_ln(0)
        p1_tr(0)
        for si in range(SC):
            lo_u = (si * nprep) // SC
            hi_u = ((si + 1) * nprep) // SC
            for u in prep_units[lo_u:hi_u]:
                u()
            if si + 1 < SC:
                p1_ln(si + 1)
            p1_kv(si)
            if si + 1 < SC:
                p1_tr(si + 1)
        for h in range(H):
            bt_, bank = pb[h // 3]
            off = (h % 3) * KVW
            E("scalar", lambda nc, d=kvh[:, h * KVW:(h + 1) * KVW],
              s=bank[:, off:off + KVW]: nc.scalar.copy(d, s),
              reads=[bt_], writes=[kvh_t])
            E("vector", lambda nc, d=kvl[:, h * KVW:(h + 1) * KVW],
              a=kvh[:, h * KVW:(h + 1) * KVW], b=bank[:, off:off + KVW]:
              nc.vector.scalar_tensor_tensor(
                  d, a, -1.0, b, op0=ALU.mult, op1=ALU.add),
              reads=[bt_, kvh_t], writes=[kvl_t])

    for cm in reversed(scope1):
        cm.__exit__(None, None, None)
    p.barrier()

    scope2 = []
    def _enter2(name, shape, dtype):
        cm = nc.sbuf_tensor(name, shape, dtype)
        h = cm.__enter__(); scope2.append(cm); return h
    xt2_h = [_enter2(f"xt2_{i}", [128, TC * D], F32) for i in range(2)]
    xorig_h = [_enter2("xorig0", [128, TC * D], F32)] * 2
    xh2_h = [_enter2(f"xh2_{i}", [128, D], F32) for i in range(2)]
    x2hi2_h = _enter2("x2hi2", [128, DJ * TT], BF16)
    x2lo2_h = _enter2("x2lo2", [128, DJ * TT], BF16)
    wbufhi_h = [_enter2(f"wbuf_hi{i}", [128, 2 * D], BF16) for i in range(2)]
    wbuflo_h = [_enter2(f"wbuf_lo{i}", [128, 2 * D], BF16) for i in range(2)]
    w2blo_h = [_enter2(f"w2b_lo{i}", [128, 2 * D], BF16) for i in range(2)]
    Qhi_h = _enter2("Qhi", [128, H * TT], BF16)
    Qlo_h = _enter2("Qlo", [128, H * TT], BF16)
    qtmp_h = [_enter2(f"qtmp{i}", [128, TT], F32) for i in range(2)]
    qtmp2_h = [_enter2(f"qtmp2_{i}", [128, TT], F32) for i in range(2)]
    hgrp_h = [_enter2(f"hgrp{i}", [128, 4 * TT], F32) for i in range(2)]
    hgrphi_h = [_enter2(f"hgrp_hi{i}", [128, 4 * TT], BF16) for i in range(2)]
    hgrplo_h = [_enter2(f"hgrp_lo{i}", [128, 4 * TT], BF16) for i in range(2)]
    y1g_h = [_enter2(f"y1g{i}", [128, 4 * TT], F32) for i in range(2)]
    hqg_h = [_enter2(f"hqg{i}", [128, 4 * TT], BF16) for i in range(2)]
    xq2_h = _enter2("xq2", [128, DJ * TT], BF16)
    y2f_h = _enter2("y2f", [128, TC * D], F32)
    dmt_h = [_enter2(f"dmt{i}", [128, TC * D], BF16) for i in range(2)]
    gb2_h = {nm: _enter2(nm + "bc", [128, D], F32) for nm in ("g2", "be2")}
    if True:
        xt2P = [(Tile(f"xt2_{i}"), xt2_h[i].ap()) for i in range(2)]
        _xo = (Tile("xorig0"), xorig_h[0].ap())
        xorigP = [_xo, _xo]
        xh2P = [(Tile(f"xh2_{i}"), xh2_h[i].ap()) for i in range(2)]
        x2f2_t, x2f2 = None, None
        x2hi2_t, x2hi2 = Tile("x2hi2"), x2hi2_h.ap()
        x2lo2_t, x2lo2 = Tile("x2lo2"), x2lo2_h.ap()
        wbufhiP = [(Tile(f"wbufhi{i}"), wbufhi_h[i].ap()) for i in range(2)]
        wbufloP = [(Tile(f"wbuflo{i}"), wbuflo_h[i].ap()) for i in range(2)]
        w2bloP = [(Tile(f"w2blo{i}"), w2blo_h[i].ap()) for i in range(2)]
        Qhi_t, Qhi = Tile("Qhi"), Qhi_h.ap()
        Qlo_t, Qlo = Tile("Qlo"), Qlo_h.ap()
        w2bhiP = [(Qhi_t, Qhi), (Qlo_t, Qlo)]
        qtmpP = [(Tile(f"qtmp{i}"), qtmp_h[i].ap()) for i in range(2)]
        qtmp2P = [(Tile(f"qtmp2_{i}"), qtmp2_h[i].ap()) for i in range(2)]
        hgrpP = [(Tile(f"hgrp{i}"), hgrp_h[i].ap()) for i in range(2)]
        hgrphiP = [(Tile(f"hgrphi{i}"), hgrphi_h[i].ap()) for i in range(2)]
        hgrploP = [(Tile(f"hgrplo{i}"), hgrplo_h[i].ap()) for i in range(2)]
        y1gP = [(Tile(f"y1g{i}"), y1g_h[i].ap()) for i in range(2)]
        hqgP = [(Tile(f"hqg{i}"), hqg_h[i].ap()) for i in range(2)]
        xq2_t, xq2 = Tile("xq2"), xq2_h.ap()
        y2f_t, y2f = Tile("y2f"), y2f_h.ap()
        x2f2_t, x2f2 = y2f_t, y2f
        dmtP = [(Tile(f"dmt{i}"), dmt_h[i].ap()) for i in range(2)]
        for nm in ("g2", "be2"):
            gbc[nm] = (Tile(nm + "bc"), gb2_h[nm].ap())
            row_t, row_ap = grow[nm]
            bc_t, bc_ap = gbc[nm]
            for og in range(2):
                bt_, bank = pb[5]
                mm(ones1[0:1, 0:128], ones1_t,
                   row_ap[0:1, og * 512:og * 512 + 512], row_t,
                   bank[:, :], bt_, start=True, stop=True)
                E("scalar", lambda nc, d=bc_ap[:, og * 512:og * 512 + 512],
                  s=bank[:, :]: nc.scalar.copy(d, s),
                  reads=[bt_], writes=[bc_t])

        def transpose_fold2(src_t, src_ap, gname, bname):
            ln_multi([(src_ap[:, c * D:(c + 1) * D], src_t,
                       xh2P[c % 2][1][:, :], xh2P[c % 2][0])
                      for c in range(TC)])
            for c in range(TC):
                xh2_t, xh2 = xh2P[c % 2]
                E("vector", lambda nc, d=xh2[:, :], a=xh2[:, :],
                  b=gbc[gname][1][:, :]: nc.vector.tensor_tensor(
                      d, a, b, op=ALU.mult),
                  reads=[xh2_t, gbc[gname][0]], writes=[xh2_t])
                E("vector", lambda nc, d=xh2[:, :], a=xh2[:, :],
                  b=gbc[bname][1][:, :]: nc.vector.tensor_tensor(
                      d, a, b, op=ALU.add),
                  reads=[xh2_t, gbc[bname][0]], writes=[xh2_t])
                for g0 in range(0, DJ, 4):
                    bt_, bank = pb[6 + (g0 // 4) % 2]
                    for gg in range(4):
                        jj = g0 + gg
                        transp(bank[:, gg * 128:(gg + 1) * 128], bt_,
                               xh2[:, jj * 128:(jj + 1) * 128], xh2_t)
                    dsth = x2hi2[:, :].rearrange(
                        "q (j r) -> q j r", r=TT)[:, g0:g0 + 4,
                                                  c * 128:c * 128 + 128]
                    dstl = x2lo2[:, :].rearrange(
                        "q (j r) -> q j r", r=TT)[:, g0:g0 + 4,
                                                  c * 128:c * 128 + 128]
                    psrc = bank[:, :].rearrange("q (j r) -> q j r", r=128)
                    E("scalar", lambda nc, d=dsth, s=psrc:
                      nc.scalar.copy(d, s), reads=[bt_], writes=[x2hi2_t])
                    E("vector", lambda nc, d=dstl, a=dsth, b=psrc:
                      nc.vector.scalar_tensor_tensor(
                          d, a, -1.0, b, op0=ALU.mult, op1=ALU.add),
                      reads=[bt_, x2hi2_t], writes=[x2lo2_t])

        def tile_body(ti):
            tp = ti % 2
            xt2_t, xt2 = xt2P[tp]
            xorig_t, xorig = xorigP[tp]
            dmt_t, dmt = dmtP[tp]
            def head1():
                rows_r = xb[ti * TT:(ti + 1) * TT, :].rearrange("(c q) d -> q c d", q=128)
                xt2_3d = xt2[:, :].rearrange("q (c d) -> q c d", d=D)
                xo_3d = xorig[:, :].rearrange("q (c d) -> q c d", d=D)
                dmain("sync", xt2_3d, xt2_t, rows_r)
                dmain("sync", xo_3d, xorig_t, rows_r)
                dm_r = dmask[ti * TT:(ti + 1) * TT, :].rearrange("(c q) d -> q c d", q=128)
                dmain("sync", dmt[:, :].rearrange("q (c d) -> q c d", d=D), dmt_t, dm_r)

                transpose_fold2(xt2_t, xt2, "g1", "be1")

                for oc in range(H):
                    ws = (oc // 2) % 2
                    wbufhi_t, wbufhi = wbufhiP[ws]
                    wbuflo_t, wbuflo = wbufloP[ws]
                    qtmp_t, qtmp = qtmpP[oc % 2]
                    qtmp2_t, qtmp2 = qtmp2P[oc % 2]
                    if oc % 2 == 0:
                        E("gpsimd", lambda nc, d=wbufhi[:, 0:2 * D],
                          s=s_wq_hi[:, oc * D:(oc + 2) * D]:
                          nc.gpsimd.dma_start(out=d, in_=s),
                          reads=[scr_t["s_wq_hi"]], writes=[wbufhi_t],
                          is_dma=True)
                        E("gpsimd", lambda nc, d=wbuflo[:, 0:2 * D],
                          s=s_wq_lo[:, oc * D:(oc + 2) * D]:
                          nc.gpsimd.dma_start(out=d, in_=s),
                          reads=[scr_t["s_wq_lo"]], writes=[wbuflo_t],
                          is_dma=True)
                    off = (oc % 2) * D
                    bt_, bank = pb[oc % 4]
                    first = True
                    for dj in range(DJ):
                        lh = wbufhi[:, off + dj * 128:off + (dj + 1) * 128]
                        ll = wbuflo[:, off + dj * 128:off + (dj + 1) * 128]
                        rh = x2hi2[:, dj * TT:(dj + 1) * TT]
                        rl = x2lo2[:, dj * TT:(dj + 1) * TT]
                        mm(lh, wbufhi_t, rh, x2hi2_t, bank[:, 0:TT], bt_,
                           start=first, stop=False)
                        first = False
                        mm(lh, wbufhi_t, rl, x2lo2_t, bank[:, 0:TT], bt_,
                           start=False, stop=False)
                        mm(ll, wbuflo_t, rh, x2hi2_t, bank[:, 0:TT], bt_,
                           start=False, stop=(dj == DJ - 1))
                    sact(qtmp[:, :], qtmp_t, bank[:, 0:TT], bt_, AF.Relu,
                         scale=-1.0)
                    sact(qtmp[:, :], qtmp_t, qtmp[:, :], qtmp_t, AF.Exp,
                         scale=-1.0)
                    E("vector", lambda nc, d=qtmp2[:, :], s=bank[:, 0:TT],
                      e2=qtmp[:, :]: nc.vector.scalar_tensor_tensor(
                          d, s, 0.0, e2, op0=ALU.max, op1=ALU.add),
                      reads=[bt_, qtmp_t], writes=[qtmp2_t])
                    E("scalar", lambda nc, d=Qhi[:, oc * TT:(oc + 1) * TT],
                      s=qtmp2[:, :]: nc.scalar.copy(d, s),
                      reads=[qtmp2_t], writes=[Qhi_t])
                    vstt(Qlo[:, oc * TT:(oc + 1) * TT], Qlo_t,
                         Qhi[:, oc * TT:(oc + 1) * TT], Qhi_t, -1.0,
                         qtmp2[:, :], qtmp2_t, ALU.mult, ALU.add)
                    drain_fill(4)

            def head2():
                for h in range(H):
                    for c in range(TC):
                        bt_, bank = pb[6 + (h * TC + c) % 2]
                        lh = Qhi[:, h * TT + c * 128:h * TT + (c + 1) * 128]
                        ll = Qlo[:, h * TT + c * 128:h * TT + (c + 1) * 128]
                        rh = kvh[:, h * KVW:(h + 1) * KVW]
                        rl = kvl[:, h * KVW:(h + 1) * KVW]
                        dst = bank[:, 0:KVW]
                        mm(lh, Qhi_t, rh, kvh_t, dst, bt_, start=True, stop=False)
                        mm(lh, Qhi_t, rl, kvl_t, dst, bt_, start=False,
                           stop=False)
                        mm(ll, Qlo_t, rh, kvh_t, dst, bt_, start=False, stop=True)
                        E("vector", lambda nc, d=z_ap[:, :], s=bank[:, 128:129]:
                          nc.vector.tensor_scalar(d, s, EPS_ATTN, None,
                                                  op0=ALU.add),
                          reads=[bt_], writes=[z_tt])
                        E("vector", lambda nc: nc.vector.reciprocal(
                            z_ap[:, :], z_ap[:, :]), reads=[z_tt], writes=[z_tt])
                        xap = xt2[:, c * D + h * 128:c * D + (h + 1) * 128]
                        E("vector", lambda nc, d=xap, s=bank[:, 0:128],
                          xi=xap: nc.vector.scalar_tensor_tensor(
                              d, s, z_ap[:, 0:1], xi, op0=ALU.mult, op1=ALU.add),
                          reads=[bt_, z_tt, xt2_t], writes=[xt2_t])
                        drain_fill(1)

                transpose_fold2(xt2_t, xt2, "g2", "be2")
            def ffnA():

                def a_w1_block(grp):
                    gp = grp % 2
                    hgrp_t, hgrp = hgrpP[gp]
                    hgrphi_t, hgrphi = hgrphiP[gp]
                    hgrplo_t, hgrplo = hgrploP[gp]
                    for gg in range(4):
                        oc = grp * 4 + gg
                        ws = (oc // 2) % 2
                        off = (oc % 2) * D
                        wbufhi_t, wbufhi = wbufhiP[ws]
                        wbuflo_t, wbuflo = wbufloP[ws]
                        if oc % 2 == 0:
                            E("gpsimd", lambda nc, d=wbufhi[:, 0:2 * D],
                              s=s_w1a_hi[:, oc * D:(oc + 2) * D]:
                              nc.gpsimd.dma_start(out=d, in_=s),
                              reads=[scr_t["s_w1a_hi"]], writes=[wbufhi_t],
                              is_dma=True)
                            E("gpsimd", lambda nc, d=wbuflo[:, 0:2 * D],
                              s=s_w1a_lo[:, oc * D:(oc + 2) * D]:
                              nc.gpsimd.dma_start(out=d, in_=s),
                              reads=[scr_t["s_w1a_lo"]], writes=[wbuflo_t],
                              is_dma=True)
                        bt_, bank = pb[4 + oc % 4]
                        first = True
                        for dj in range(DJ):
                            lh = wbufhi[:, off + dj * 128:off + (dj + 1) * 128]
                            ll = wbuflo[:, off + dj * 128:off + (dj + 1) * 128]
                            rh = x2hi2[:, dj * TT:(dj + 1) * TT]
                            rl = x2lo2[:, dj * TT:(dj + 1) * TT]
                            mm(lh, wbufhi_t, rh, x2hi2_t, bank[:, 0:TT], bt_,
                               start=first, stop=False)
                            first = False
                            mm(lh, wbufhi_t, rl, x2lo2_t, bank[:, 0:TT], bt_,
                               start=False, stop=False)
                            mm(ll, wbuflo_t, rh, x2hi2_t, bank[:, 0:TT], bt_,
                               start=False, stop=(dj == DJ - 1))
                        E("scalar", lambda nc, d=hgrp[:, gg * TT:(gg + 1) * TT],
                          s=bank[:, 0:TT], b=b1a[:, oc:oc + 1]:
                          nc.scalar.activation(d, s, AF.Gelu, bias=b, scale=1.0),
                          reads=[bt_, b1a_t], writes=[hgrp_t])
                        E("scalar", lambda nc,
                          d=hgrphi[:, gg * TT:(gg + 1) * TT],
                          s=hgrp[:, gg * TT:(gg + 1) * TT]:
                          nc.scalar.copy(d, s), reads=[hgrp_t],
                          writes=[hgrphi_t])
                        vstt(hgrplo[:, gg * TT:(gg + 1) * TT], hgrplo_t,
                             hgrphi[:, gg * TT:(gg + 1) * TT], hgrphi_t, -1.0,
                             hgrp[:, gg * TT:(gg + 1) * TT], hgrp_t,
                             ALU.mult, ALU.add)

                def a_w2_block(grp):
                    gp = grp % 2
                    hgrphi_t, hgrphi = hgrphiP[gp]
                    hgrplo_t, hgrplo = hgrploP[gp]
                    for gg in range(4):
                        dffc = grp * 4 + gg
                        ws = (dffc // 2) % 2
                        off = (dffc % 2) * D
                        w2bhi_t, w2bhi = w2bhiP[ws]
                        w2blo_t, w2blo = w2bloP[ws]
                        if dffc % 2 == 0:
                            E("gpsimd", lambda nc, d=w2bhi[:, 0:2 * D],
                              s=s_w2a_hi[:, dffc * D:(dffc + 2) * D]:
                              nc.gpsimd.dma_start(out=d, in_=s),
                              reads=[scr_t["s_w2a_hi"]], writes=[w2bhi_t],
                              is_dma=True)
                            E("gpsimd", lambda nc, d=w2blo[:, 0:2 * D],
                              s=s_w2a_lo[:, dffc * D:(dffc + 2) * D]:
                              nc.gpsimd.dma_start(out=d, in_=s),
                              reads=[scr_t["s_w2a_lo"]], writes=[w2blo_t],
                              is_dma=True)
                        for c in range(TC):
                            lh = hgrphi[:, gg * TT + c * 128:
                                        gg * TT + (c + 1) * 128]
                            ll = hgrplo[:, gg * TT + c * 128:
                                        gg * TT + (c + 1) * 128]
                            for og in range(2):
                                bt_, bank = pb[c * 2 + og]
                                rh = w2bhi[:, off + og * 512:
                                           off + og * 512 + 512]
                                rl = w2blo[:, off + og * 512:
                                           off + og * 512 + 512]
                                mm(lh, hgrphi_t, rh, w2bhi_t, bank[:, :], bt_,
                                   start=(dffc == 0), stop=False)
                                mm(lh, hgrphi_t, rl, w2blo_t, bank[:, :], bt_,
                                   start=False, stop=False)
                                mm(ll, hgrplo_t, rh, w2bhi_t, bank[:, :], bt_,
                                   start=False, stop=False)

                a_w1_block(0)
                drain_fill(4)
                for grp in range(1, FJ // 4):
                    a_w1_block(grp)
                    drain_fill(4)
                    a_w2_block(grp - 1)
                    drain_fill(4)
                a_w2_block(FJ // 4 - 1)
                drain_fill()
                for c in range(TC):
                    for og in range(2):
                        bt_, bank = pb[c * 2 + og]
                        mm(ones1[0:1, 0:128], ones1_t,
                           b2a[0:1, og * 512:og * 512 + 512], b2a_t,
                           bank[:, :], bt_, start=False, stop=True)
                        xap = xt2[:, c * D + og * 512:c * D + og * 512 + 512]
                        E("vector", lambda nc, d=xap, a=xap, b=bank[:, :]:
                          nc.vector.tensor_tensor(d, a, b, op=ALU.add),
                          reads=[bt_, xt2_t], writes=[xt2_t])

                for c in range(TC):
                    xap = xt2[:, c * D:(c + 1) * D]
                    E("vector", lambda nc, d=xap, a=xap,
                      b=xorig[:, c * D:(c + 1) * D]:
                      nc.vector.tensor_tensor(d, a, b, op=ALU.add),
                      reads=[xt2_t, xorig_t], writes=[xt2_t])
                    xh2_t, xh2 = xh2P[c % 2]
                    ln_norm(xap, xt2_t, xh2[:, :], xh2_t)
                    vcp(xap, xt2_t, xh2[:, :], xh2_t)

                for c in range(TC):
                    for g0 in range(0, DJ, 4):
                        bt_, bank = pb[6 + (g0 // 4) % 2]
                        for gg in range(4):
                            jj = g0 + gg
                            transp(bank[:, gg * 128:(gg + 1) * 128], bt_,
                                   xt2[:, c * D + jj * 128:c * D + (jj + 1) * 128],
                                   xt2_t)
                        dstf = x2f2[:, :].rearrange(
                            "q (j r) -> q j r", r=TT)[:, g0:g0 + 4,
                                                      c * 128:c * 128 + 128]
                        psrc = bank[:, :].rearrange("q (j r) -> q j r", r=128)
                        E("scalar", lambda nc, d=dstf, s=psrc:
                          nc.scalar.copy(d, s), reads=[bt_], writes=[x2f2_t])
            def ffnQ():
                for blk in range(DJ * TT // QW):
                    quant_chain(x2f2[:, blk * QW:(blk + 1) * QW], x2f2_t,
                                xq2[:, blk * QW:(blk + 1) * QW], xq2_t,
                                QW, 2, -14)

            def ffnB():
                drain_fill()

                def q_w1_block(grp):
                    gp = grp % 2
                    y1g_t, y1g = y1gP[gp]
                    hqg_t, hqg = hqgP[gp]
                    for gg in range(4):
                        oc = grp * 4 + gg
                        ws = (oc // 2) % 2
                        off = (oc % 2) * D
                        wbufhi_t, wbufhi = wbufhiP[ws]
                        if oc % 2 == 0:
                            E("gpsimd", lambda nc, d=wbufhi[:, 0:2 * D],
                              s=s_W1q[:, oc * D:(oc + 2) * D]:
                              nc.gpsimd.dma_start(out=d, in_=s),
                              reads=[scr_t["s_W1q"]], writes=[wbufhi_t],
                              is_dma=True)
                        bt_, bank = pb[4 + oc % 4]
                        first = True
                        for dj in range(DJ):
                            l_ = wbufhi[:, off + dj * 128:off + (dj + 1) * 128]
                            r_ = xq2[:, dj * TT:(dj + 1) * TT]
                            mm(l_, wbufhi_t, r_, xq2_t, bank[:, 0:TT], bt_,
                               start=first, stop=(dj == DJ - 1))
                            first = False
                        E("scalar", lambda nc, d=y1g[:, gg * TT:(gg + 1) * TT],
                          s=bank[:, 0:TT], b=b1fq[:, oc:oc + 1]:
                          nc.scalar.activation(d, s, AF.Identity, bias=b, scale=1.0),
                          reads=[bt_, b1fq_t], writes=[y1g_t])
                    hq_chain(y1g[:, 0:4 * TT], y1g_t, hqg[:, 0:4 * TT], hqg_t,
                             4 * TT)

                def q_w2_block(grp):
                    gp = grp % 2
                    hqg_t, hqg = hqgP[gp]
                    for gg in range(4):
                        dffc = grp * 4 + gg
                        ws = (dffc // 2) % 2
                        off = (dffc % 2) * D
                        w2blo_t, w2blo = w2bloP[ws]
                        if dffc % 2 == 0:
                            E("gpsimd", lambda nc, d=w2blo[:, 0:2 * D],
                              s=s_W2q[:, dffc * D:(dffc + 2) * D]:
                              nc.gpsimd.dma_start(out=d, in_=s),
                              reads=[scr_t["s_W2q"]], writes=[w2blo_t],
                              is_dma=True)
                        for c in range(TC):
                            l_ = hqg[:, gg * TT + c * 128:gg * TT + (c + 1) * 128]
                            for og in range(2):
                                bt_, bank = pb[c * 2 + og]
                                r_ = w2blo[:, off + og * 512:
                                           off + og * 512 + 512]
                                mm(l_, hqg_t, r_, w2blo_t, bank[:, :], bt_,
                                   start=(dffc == 0), stop=False)

                q_w1_block(0)
                for grp in range(1, FJ // 4):
                    q_w1_block(grp)
                    q_w2_block(grp - 1)
                q_w2_block(FJ // 4 - 1)
                for c in range(TC):
                    for og in range(2):
                        bt_, bank = pb[c * 2 + og]
                        mm(ones1[0:1, 0:128], ones1_t,
                           b2fq[0:1, og * 512:og * 512 + 512], b2fq_t,
                           bank[:, :], bt_, start=False, stop=True)
                        vcp(y2f[:, c * D + og * 512:c * D + og * 512 + 512],
                            y2f_t, bank[:, :], bt_)
            def tail():
                for blk in range(TC * D // QW):
                    quant_chain(y2f[:, blk * QW:(blk + 1) * QW], y2f_t,
                                y2f[:, blk * QW:(blk + 1) * QW], y2f_t,
                                QW, 7, -95)
                for c in range(TC):
                    xap = xt2[:, c * D:(c + 1) * D]
                    E("vector", lambda nc, d=y2f[:, c * D:(c + 1) * D],
                      a=y2f[:, c * D:(c + 1) * D], b=dmt[:, c * D:(c + 1) * D]:
                      nc.vector.tensor_tensor(d, a, b, op=ALU.mult),
                      reads=[y2f_t, dmt_t], writes=[y2f_t])
                    E("vector", lambda nc, d=xap, a=xap,
                      b=y2f[:, c * D:(c + 1) * D]:
                      nc.vector.tensor_tensor(d, a, b, op=ALU.add),
                      reads=[xt2_t, y2f_t], writes=[xt2_t])
                    xh2_t, xh2 = xh2P[c % 2]
                    ln_norm(xap, xt2_t, xh2[:, :], xh2_t)
                    vcp(xap, xt2_t, xh2[:, :], xh2_t)
                orow = out[ti * TT:(ti + 1) * TT, :].rearrange("(c q) d -> q c d", q=128)
                dmaout("gpsimd", orow, xt2[:, :].rearrange("q (c d) -> q c d", d=D), xt2_t)
            return head1, head2, ffnA, ffnQ, ffnB, tail

        stages = [tile_body(ti) for ti in range(NTILE)]
        H1, H2, FA, FQ, FB, TL = 0, 1, 2, 3, 4, 5
        stages[0][H1](); stages[0][H2](); stages[0][FA]()
        for ti in range(NTILE):
            p._cap = []
            stages[ti][FQ]()
            fill_q.extend(p._cap); p._cap = None
            if ti + 1 < NTILE:
                stages[ti + 1][H1]()
                stages[ti + 1][H2]()
            stages[ti][FB]()
            p._cap = []
            stages[ti][TL]()
            fill_q.extend(p._cap); p._cap = None
            if ti + 1 < NTILE:
                stages[ti + 1][FA]()
        drain_fill()


        last_dma = [op.idx for op in p.ops
                    if op.is_dma and op.engine == "gpsimd"]
        fidx = p.emit("gpsimd", lambda nc: None, extra_deps=last_dma[-NTILE:])
        p.ops[fidx].virtual = True

    for cm in reversed(scope2):
        cm.__exit__(None, None, None)
    p.finalize()
    with nc.Block() as block:
        @block.sync
        def _(eng):
            p.run_engine(nc, "sync", eng, sems, dma_sems)

        @block.tensor
        def _(eng):
            p.run_engine(nc, "tensor", eng, sems, dma_sems)

        @block.vector
        def _(eng):
            p.run_engine(nc, "vector", eng, sems, dma_sems)

        @block.scalar
        def _(eng):
            p.run_engine(nc, "scalar", eng, sems, dma_sems)

        @block.gpsimd
        def _(eng):
            p.run_engine(nc, "gpsimd", eng, sems, dma_sems)

    for cm in reversed(ctxs):
        cm.__exit__(None, None, None)
    return nc


def kernel(L0_attn_x, L0_attn_wq, L0_attn_wk, L0_attn_wv, L0_attn_w1,
           L0_attn_b1, L0_attn_w2, L0_attn_b2, L0_attn_g1, L0_attn_be1,
           L0_attn_g2, L0_attn_be2, L0_ffn_W1, L0_ffn_b1, L0_ffn_W2,
           L0_ffn_b2):
    x = np.ascontiguousarray(np.asarray(L0_attn_x, dtype=np.float32))
    B, L, C = x.shape

    import os as _os
    nel = B * L * C
    idx = np.arange(nel, dtype=np.uint64)
    o0, o1 = _threefry2x32(np.uint32(0), np.uint32(42),
                           (idx >> np.uint64(32)).astype(np.uint32),
                           (idx & np.uint64(0xffffffff)).astype(np.uint32))
    if _os.environ.get("BASS_SIM_FP64") == "1":
        b64 = (o0.astype(np.uint64) << np.uint64(32)) | o1.astype(np.uint64)
        f = ((b64 >> np.uint64(12)) |
             np.uint64(0x3FF0000000000000)).view(np.float64) - 1.0
        keep = f < 0.8
    else:
        b32 = o0 ^ o1
        f = ((b32 >> np.uint32(9)) |
             np.uint32(0x3f800000)).view(np.float32) - np.float32(1.0)
        keep = f < np.float32(0.8)
    keep = keep.reshape(B, L, C)
    dmask_full = keep.astype(np.float32) * np.float32(1.25)

    def cols(v, nj):
        return np.ascontiguousarray(
            np.asarray(v, np.float32).reshape(nj, 128).T)

    common = {
        "wq": np.ascontiguousarray(np.asarray(L0_attn_wq, np.float32)),
        "wk": np.ascontiguousarray(np.asarray(L0_attn_wk, np.float32)),
        "wv": np.ascontiguousarray(np.asarray(L0_attn_wv, np.float32)),
        "w1a": np.ascontiguousarray(np.asarray(L0_attn_w1, np.float32)),
        "w2a": np.ascontiguousarray(np.asarray(L0_attn_w2, np.float32)),
        "W1f": np.ascontiguousarray(np.asarray(L0_ffn_W1, np.float32)),
        "W2f": np.ascontiguousarray(np.asarray(L0_ffn_W2, np.float32)),
        "b1a_c": cols(L0_attn_b1, FJ),
        "b2a_r": np.ascontiguousarray(
            np.asarray(L0_attn_b2, np.float32).reshape(1, D)),
        "b1f_c": cols(L0_ffn_b1, FJ),
        "b2f_r": np.ascontiguousarray(
            np.asarray(L0_ffn_b2, np.float32).reshape(1, D)),
        "g1_r": np.ascontiguousarray(
            np.asarray(L0_attn_g1, np.float32).reshape(1, D)),
        "be1_r": np.ascontiguousarray(
            np.asarray(L0_attn_be1, np.float32).reshape(1, D)),
        "g2_r": np.ascontiguousarray(
            np.asarray(L0_attn_g2, np.float32).reshape(1, D)),
        "be2_r": np.ascontiguousarray(
            np.asarray(L0_attn_be2, np.float32).reshape(1, D)),
        "eye": np.eye(128, dtype=np.float32),
    }

    in_maps = []
    for c in range(8):
        n, half = c // 2, c % 2
        own = x[n, half * 2048:(half + 1) * 2048]
        other = x[n, (1 - half) * 2048:(2 - half) * 2048]
        m = dict(common)
        m["xb"] = np.ascontiguousarray(np.concatenate([own, other], axis=0))
        m["dmask"] = np.ascontiguousarray(
            dmask_full[n, half * 2048:(half + 1) * 2048].astype(
                ml_dtypes.bfloat16))
        in_maps.append(m)

    nc = build_nc()
    res = run_bass_kernel_spmd(nc, in_maps, core_ids=list(range(8)))
    outp = np.empty((B, L, C), dtype=np.float32)
    for c in range(8):
        n, half = c // 2, c % 2
        outp[n, half * 2048:(half + 1) * 2048] = res.results[c]["out"]
    return outp


# version 25
# speedup vs baseline: 1.0104x; 1.0056x over previous
import numpy as np
import ml_dtypes
import concourse.bass as bass
import concourse.mybir as mybir
from concourse.bass_utils import run_bass_kernel_spmd

F32 = mybir.dt.float32
BF16 = mybir.dt.bfloat16
I32 = mybir.dt.int32
ALU = mybir.AluOpType
AF = mybir.ActivationFunctionType

D = 1024
DJ = 8
H = 8
DFF = 4096
FJ = 32
S = 4096
SC = S // 128
TOWN = 2048
TT = 256
NTILE = TOWN // TT
TC = TT // 128
EPS_LN = 1e-5
EPS_ATTN = 1e-6
KVW = 132

ENGINES = ("sync", "tensor", "vector", "scalar", "gpsimd")
NDMASEM = 4


class Tile:
    __slots__ = ("name", "writes", "readers")

    def __init__(self, name):
        self.name = name
        self.writes = []
        self.readers = []


class Op:
    __slots__ = ("idx", "engine", "build", "deps", "is_dma", "needs_inc",
                 "semval", "seq", "slot", "virtual")

    def __init__(self, idx, engine, build, deps, is_dma):
        self.virtual = False
        self.idx = idx
        self.engine = engine
        self.build = build
        self.deps = deps
        self.is_dma = is_dma
        self.needs_inc = False
        self.semval = None
        self.seq = None
        self.slot = 0


class Prog:
    label = ""

    def __init__(self):
        self.ops = []
        self.labels = []
        self.last_by_engine = {e: None for e in ENGINES}
        self._cap = None

    def emit(self, engine, build, reads=(), writes=(), extra_deps=(),
             is_dma=False):
        if self._cap is not None:
            self._cap.append((engine, build, tuple(reads), tuple(writes),
                              tuple(extra_deps), is_dma))
            return None
        deps = set(extra_deps)
        for t in reads:
            deps.update(t.writes)
        for t in writes:
            deps.update(t.writes)
            deps.update(t.readers)
        idx = len(self.ops)
        op = Op(idx, engine, build, sorted(deps), is_dma)
        self.ops.append(op)
        self.labels.append(Prog.label)
        self.last_by_engine[engine] = idx
        for t in reads:
            if idx not in t.readers:
                t.readers.append(idx)
        for t in writes:
            t.writes = [idx]
            t.readers = []
        return idx

    def barrier(self):
        lasts = [i for i in self.last_by_engine.values() if i is not None]
        for e in ENGINES:
            deps = [i for i in lasts if self.ops[i].engine != e]
            idx = self.emit(e, lambda nc: None, extra_deps=deps)
            self.ops[idx].virtual = True

    def finalize(self):
        dma_rr = {e: 0 for e in ENGINES}
        chains = {}
        for op in self.ops:
            if op.is_dma:
                op.slot = dma_rr[op.engine] % NDMASEM
                dma_rr[op.engine] += 1
                chains.setdefault((op.engine, op.slot), []).append(op.idx)
        consumers = {}
        for op in self.ops:
            for d in op.deps:
                consumers.setdefault(d, []).append(op.idx)
        for lst in chains.values():
            for i in range(1, len(lst)):
                cur = self.ops[lst[i]]
                prev = [c for c in consumers.get(lst[i - 1], ())
                        if c < cur.idx]
                if prev:
                    cur.deps = sorted(set(cur.deps) | set(prev))
        self.slot_chain = chains
        memo = {}

        def expand(i):
            if i in memo:
                return memo[i]
            op = self.ops[i]
            if not op.virtual:
                memo[i] = (i,)
                return memo[i]
            r = []
            for d in op.deps:
                r.extend(expand(d))
            memo[i] = tuple(sorted(set(r)))
            return memo[i]

        for op in self.ops:
            if any(self.ops[d].virtual for d in op.deps):
                nd = set()
                for d in op.deps:
                    nd.update(expand(d))
                op.deps = sorted(nd)
        for op in self.ops:
            for d in op.deps:
                dep = self.ops[d]
                if dep.is_dma or dep.engine != op.engine:
                    dep.needs_inc = True
        cnt = {e: 0 for e in ENGINES}
        dma_cnt = {e: [0] * NDMASEM for e in ENGINES}
        seq = {e: 0 for e in ENGINES}
        for op in self.ops:
            op.seq = seq[op.engine]
            seq[op.engine] += 1
            if op.is_dma:
                dma_cnt[op.engine][op.slot] += 1
                op.semval = dma_cnt[op.engine][op.slot] * 16
                op.needs_inc = True
            elif op.needs_inc:
                cnt[op.engine] += 1
                op.semval = cnt[op.engine]
        self.by_engine = {e: [op for op in self.ops if op.engine == e]
                          for e in ENGINES}

    def run_engine(self, nc, engine, eng, sems, dma_sems):
        waited = {}
        drained_upto = -1
        for op in self.by_engine[engine]:
            need_drain = False
            for d in op.deps:
                dep = self.ops[d]
                if dep.engine == engine and not dep.is_dma:
                    if engine != "tensor" and dep.seq > drained_upto:
                        need_drain = True
                else:
                    key = (dep.engine, dep.is_dma, dep.slot)
                    val = dep.semval
                    assert val is not None, (op.idx, d, dep.engine, engine)
                    if dep.is_dma:
                        ch = self.slot_chain[(dep.engine, dep.slot)]
                        import bisect
                        pos = bisect.bisect_left(ch, op.idx) - 1
                        val = self.ops[ch[pos]].semval
                    if waited.get(key, -1) < val:
                        if dep.is_dma:
                            sem = dma_sems[dep.engine][dep.slot]
                        else:
                            sem = sems[dep.engine]
                        eng.wait_ge(sem, val)
                        waited[key] = val
            if need_drain:
                eng.drain()
                drained_upto = op.seq - 1
            ins = op.build(nc)
            if ins is not None:
                if op.is_dma:
                    ins.then_inc(dma_sems[engine][op.slot], 16)
                elif op.needs_inc:
                    ins.then_inc(sems[engine], 1)



def _threefry2x32(k0, k1, x0, x1):
    def rotl(x, r):
        return ((x << np.uint32(r)) | (x >> np.uint32(32 - r))).astype(np.uint32)
    k0 = np.asarray(k0, np.uint32)
    k1 = np.asarray(k1, np.uint32)
    x0 = np.asarray(x0, np.uint32).copy()
    x1 = np.asarray(x1, np.uint32).copy()
    ks2 = (k0 ^ k1 ^ np.uint32(0x1BD11BDA)).astype(np.uint32)
    x0 = (x0 + k0).astype(np.uint32)
    x1 = (x1 + k1).astype(np.uint32)
    ks = [k0, k1, ks2]
    rot = ((13, 15, 26, 6), (17, 29, 16, 24))
    for i in range(5):
        for r in rot[i % 2]:
            x0 = (x0 + x1).astype(np.uint32)
            x1 = rotl(x1, r) ^ x0
        x0 = (x0 + ks[(i + 1) % 3]).astype(np.uint32)
        x1 = (x1 + ks[(i + 2) % 3] + np.uint32(i + 1)).astype(np.uint32)
    return x0, x1

def build_nc():
    nc = bass.Bass("TRN2", target_bir_lowering=False, debug=False)

    xb = nc.dram_tensor("xb", [S, D], F32, kind="ExternalInput").ap()
    wq = nc.dram_tensor("wq", [D, D], F32, kind="ExternalInput").ap()
    wk = nc.dram_tensor("wk", [D, D], F32, kind="ExternalInput").ap()
    wv = nc.dram_tensor("wv", [D, D], F32, kind="ExternalInput").ap()
    w1a = nc.dram_tensor("w1a", [DFF, D], F32, kind="ExternalInput").ap()
    w2a = nc.dram_tensor("w2a", [D, DFF], F32, kind="ExternalInput").ap()
    W1f = nc.dram_tensor("W1f", [DFF, D], F32, kind="ExternalInput").ap()
    W2f = nc.dram_tensor("W2f", [D, DFF], F32, kind="ExternalInput").ap()
    b1a_c = nc.dram_tensor("b1a_c", [128, FJ], F32, kind="ExternalInput").ap()
    b2a_r = nc.dram_tensor("b2a_r", [1, D], F32, kind="ExternalInput").ap()
    b1f_c = nc.dram_tensor("b1f_c", [128, FJ], F32, kind="ExternalInput").ap()
    b2f_r = nc.dram_tensor("b2f_r", [1, D], F32, kind="ExternalInput").ap()
    g1_r = nc.dram_tensor("g1_r", [1, D], F32, kind="ExternalInput").ap()
    be1_r = nc.dram_tensor("be1_r", [1, D], F32, kind="ExternalInput").ap()
    g2_r = nc.dram_tensor("g2_r", [1, D], F32, kind="ExternalInput").ap()
    be2_r = nc.dram_tensor("be2_r", [1, D], F32, kind="ExternalInput").ap()
    dmask = nc.dram_tensor("dmask", [TOWN, D], BF16, kind="ExternalInput").ap()
    eye_in = nc.dram_tensor("eye", [128, 128], F32, kind="ExternalInput").ap()
    out = nc.dram_tensor("out", [TOWN, D], F32, kind="ExternalOutput").ap()

    s_wq_hi = nc.dram_tensor("s_wq_hi", [128, DJ * D], BF16, kind="Internal").ap()
    s_wq_lo = nc.dram_tensor("s_wq_lo", [128, DJ * D], BF16, kind="Internal").ap()
    s_w1a_hi = nc.dram_tensor("s_w1a_hi", [128, FJ * D], BF16, kind="Internal").ap()
    s_w1a_lo = nc.dram_tensor("s_w1a_lo", [128, FJ * D], BF16, kind="Internal").ap()
    s_w2a_hi = nc.dram_tensor("s_w2a_hi", [128, FJ * D], BF16, kind="Internal").ap()
    s_w2a_lo = nc.dram_tensor("s_w2a_lo", [128, FJ * D], BF16, kind="Internal").ap()
    s_W1q = nc.dram_tensor("s_W1q", [128, FJ * D], BF16, kind="Internal").ap()
    s_W2q = nc.dram_tensor("s_W2q", [128, FJ * D], BF16, kind="Internal").ap()

    p = Prog()
    ctxs = []
    tiles = {}
    scr_t = {nm: Tile(nm) for nm in
             ("s_wq_hi", "s_wq_lo", "s_w1a_hi", "s_w1a_lo", "s_w2a_hi",
              "s_w2a_lo", "s_W1q", "s_W2q")}

    def sb(name, shape, dtype):
        cm = nc.sbuf_tensor(name, shape, dtype)
        h = cm.__enter__()
        ctxs.append(cm)
        t = Tile(name)
        tiles[name] = (t, h.ap())
        return t, h.ap()

    def psum(name):
        cm = nc.psum_tensor(name, [128, 512], F32)
        h = cm.__enter__()
        ctxs.append(cm)
        t = Tile(name)
        return t, h.ap()

    def semaphore(name):
        cm = nc.semaphore(name)
        s = cm.__enter__()
        ctxs.append(cm)
        return s

    sems = {e: semaphore("sem_" + e) for e in ENGINES}
    dma_sems = {e: [semaphore(f"dma_{e}{i}") for i in range(NDMASEM)]
                for e in ENGINES}

    eye_t, eye = sb("eyet", [128, 128], F32)
    ones1_t, ones1 = sb("ones1", [1, 128], F32)
    b1a_t, b1a = sb("b1a_t", [128, FJ], F32)
    b1f_t, b1f = sb("b1f_t", [128, FJ], F32)
    b1fq_t, b1fq = sb("b1fq_t", [128, FJ], F32)
    b2a_t, b2a = sb("b2a_row", [1, D], F32)
    b2f_t, b2f = sb("b2f_row", [1, D], F32)
    b2fq_t, b2fq = sb("b2fq_row", [1, D], F32)
    grow = {}
    for nm in ("g1", "be1", "g2", "be2"):
        grow[nm] = sb(nm + "row", [1, D], F32)
    gbc = {}
    for nm in ("g1", "be1"):
        gbc[nm] = sb(nm + "bc", [128, D], F32)
    QW = 1024
    qI = [sb(f"qI{i}", [128, QW], I32) for i in range(4)]
    qF = [sb(f"qF{i}", [128, QW], F32) for i in range(3)]
    kvh_t, kvh = sb("kv_hi", [128, H * KVW], BF16)
    kvl_t, kvl = sb("kv_lo", [128, H * KVW], BF16)
    zP = [sb(f"z_t{i}", [128, 1], F32) for i in range(2)]
    pb = [psum(f"pb{i}") for i in range(8)]

    def E(engine, fn, reads=(), writes=(), extra=(), is_dma=False):
        return p.emit(engine, fn, reads=reads, writes=writes,
                      extra_deps=extra, is_dma=is_dma)

    fill_q = []

    def drain_fill(n=10 ** 9):
        while fill_q and n > 0:
            a = fill_q.pop(0)
            cap, p._cap = p._cap, None
            p.emit(a[0], a[1], reads=a[2], writes=a[3], extra_deps=a[4],
                   is_dma=a[5])
            p._cap = cap
            n -= 1

    def vts(dst, dt_, src, st_, s1, s2, op0, op1=None, sc=()):
        if op1 is None:
            fn = lambda nc, d=dst, s=src, a=s1, o0=op0: \
                nc.vector.tensor_scalar(d, s, a, None, op0=o0)
        else:
            fn = lambda nc, d=dst, s=src, a=s1, b=s2, o0=op0, o1=op1: \
                nc.vector.tensor_scalar(d, s, a, b, op0=o0, op1=o1)
        return E("vector", fn, reads=[st_] + list(sc), writes=[dt_])

    def vtt(dst, dt_, a, at_, b, bt_, op):
        fn = lambda nc, d=dst, x=a, y=b, o=op: \
            nc.vector.tensor_tensor(d, x, y, op=o)
        return E("vector", fn, reads=[at_, bt_], writes=[dt_])

    def vstt(dst, dt_, in0, t0, scalar, in1, t1, op0, op1, sc=()):
        fn = lambda nc, d=dst, x=in0, s=scalar, y=in1, o0=op0, o1=op1: \
            nc.vector.scalar_tensor_tensor(d, x, s, y, op0=o0, op1=o1)
        return E("vector", fn, reads=[t0, t1] + list(sc), writes=[dt_])

    def vcp(dst, dt_, src, st_):
        fn = lambda nc, d=dst, s=src: nc.vector.tensor_copy(d, s)
        return E("vector", fn, reads=[st_], writes=[dt_])

    def sact(dst, dt_, src, st_, func, bias=None, scale=1.0, bt_=None):
        if bias is None:
            fn = lambda nc, d=dst, s=src, f=func, sc_=scale: \
                nc.scalar.activation(d, s, f, scale=sc_)
        else:
            fn = lambda nc, d=dst, s=src, f=func, b=bias, sc_=scale: \
                nc.scalar.activation(d, s, f, bias=b, scale=sc_)
        rd = [st_] + ([bt_] if bt_ is not None else [])
        return E("scalar", fn, reads=rd, writes=[dt_])

    def dmain(queue, dst, dt_, src):
        fn = lambda nc, d=dst, s=src: \
            getattr(nc, queue).dma_start(out=d, in_=s)
        return E(queue, fn, writes=[dt_], is_dma=True)

    def dmaout(queue, dst, src, st_):
        fn = lambda nc, d=dst, s=src: \
            getattr(nc, queue).dma_start(out=d, in_=s)
        return E(queue, fn, reads=[st_], is_dma=True)

    def mm(lhs, lt_, rhs, rt_, dst, dt_, start, stop):
        fn = lambda nc, l=lhs, r=rhs, d=dst, s0=start, s1=stop: \
            nc.tensor.matmul(d, l, r, start=s0, stop=s1)
        return E("tensor", fn, reads=[lt_, rt_], writes=[dt_])

    def transp(dst, dt_, src, st_):
        fn = lambda nc, d=dst, s=src: nc.tensor.transpose(d, s, eye[:, :])
        return E("tensor", fn, reads=[st_, eye_t], writes=[dt_])

    def memset(t_, ap_, val):
        fn = lambda nc, a=ap_, v=val: nc.gpsimd.memset(a, v)
        return E("gpsimd", fn, writes=[t_])

    def quant_chain(src, st_, dst, dt_, W, Mb, emin, P=128):
        I1 = qI[0][1][0:P, 0:W]; i1 = qI[0][0]
        I2 = qI[1][1][0:P, 0:W]; i2 = qI[1][0]
        I3 = qI[2][1][0:P, 0:W]; i3 = qI[2][0]
        I4 = qI[3][1][0:P, 0:W]; i4 = qI[3][0]
        F1 = qF[0][1][0:P, 0:W]; f1 = qF[0][0]
        F2 = qF[1][1][0:P, 0:W]; f2 = qF[1][0]
        F3 = qF[2][1][0:P, 0:W]; f3 = qF[2][0]
        bits = src.bitcast(I32)
        vts(I1, i1, bits, st_, 23, 255, ALU.logical_shift_right,
            ALU.bitwise_and)
        vts(I2, i2, I1, i1, 127 + emin, -1, ALU.max, ALU.mult)
        vts(I3, i3, I2, i2, Mb - 1 + 254, 8388608, ALU.add, ALU.mult)
        vts(I1, i1, bits, st_, -2147483648, None, ALU.bitwise_and)
        vts(I4, i4, I2, i2, -1, -Mb, ALU.mult, ALU.add)
        vstt(I4, i4, I4, i4, 8388608, I1, i1, ALU.mult, ALU.add)
        vtt(F1, f1, src, st_, I3.bitcast(F32), i3, ALU.mult)
        vstt(F2, f2, F1, f1, -1.0, F1, f1, ALU.mult, ALU.max)
        vcp(I3, i3, F2, f2)
        vcp(F3, f3, I3, i3)
        vtt(F1, f1, F2, f2, F3, f3, ALU.subtract)
        vts(F2, f2, F1, f1, -0.25, -1.0, ALU.is_ge, ALU.add)
        vstt(F2, f2, F1, f1, 0.25, F2, f2, ALU.is_gt, ALU.add)
        vstt(F2, f2, F1, f1, 0.75, F2, f2, ALU.is_ge, ALU.add)
        vstt(F3, f3, F3, f3, 2.0, F2, f2, ALU.mult, ALU.add)
        vtt(dst, dt_, F3, f3, I4.bitcast(F32), i4, ALU.mult)

    def hq_chain(src, st_, dst, dt_, W):
        I1 = qI[0][1][:, 0:W]; i1 = qI[0][0]
        I2 = qI[1][1][:, 0:W]; i2 = qI[1][0]
        I3 = qI[2][1][:, 0:W]; i3 = qI[2][0]
        I4 = qI[3][1][:, 0:W]; i4 = qI[3][0]
        F1 = qF[0][1][:, 0:W]; f1 = qF[0][0]
        F2 = qF[1][1][:, 0:W]; f2 = qF[1][0]
        F3 = qF[2][1][:, 0:W]; f3 = qF[2][0]
        bits = src.bitcast(I32)
        vts(I1, i1, bits, st_, 23, 255, ALU.logical_shift_right,
            ALU.bitwise_and)
        vts(I2, i2, I1, i1, 113, -1, ALU.max, ALU.mult)
        vts(I3, i3, I1, i1, 113, 8, ALU.min, ALU.add)
        vts(I3, i3, I3, i3, 8388608, None, ALU.mult)
        vts(I1, i1, I2, i2, 2 + 254, 8388608, ALU.add, ALU.mult)
        vts(I4, i4, I2, i2, -1, -2, ALU.mult, ALU.add)
        vts(I4, i4, I4, i4, 8388608, None, ALU.mult)
        vtt(F1, f1, src, st_, I1.bitcast(F32), i1, ALU.mult)
        vtt(F2, f2, F1, f1, I3.bitcast(F32), i3, ALU.subtract)
        vtt(F3, f3, F1, f1, I3.bitcast(F32), i3, ALU.add)
        vts(F1, f1, F2, f2, 0.5, None, ALU.is_gt)
        for j in range(1, 8):
            if j % 2 == 0:
                vstt(F1, f1, F2, f2, j + 0.5, F1, f1, ALU.is_gt, ALU.add)
            else:
                vstt(F1, f1, F3, f3, j + 0.5, F1, f1, ALU.is_ge, ALU.add)
        vtt(dst, dt_, F1, f1, I4.bitcast(F32), i4, ALU.mult)

    NLN = 4
    lnp = [sb(f"lnp{i}", [128, 48], F32) for i in range(NLN)]
    lnpi = [sb(f"lnpi{i}", [128, 2], I32) for i in range(NLN)]
    _ln_rr = [0]

    def ln_multi(jobs):
        nch = len(jobs)
        assert nch <= 2
        sl = _ln_rr[0] % NLN
        _ln_rr[0] += 1
        pt_, pp = lnp[sl]
        it_, ip = lnpi[sl]
        for c, (xap, xt_, _, _) in enumerate(jobs):
            E("vector", lambda nc, d=pp[:, 16 * c:16 * c + 6],
              s=xap[:, 0:512]: nc.vector.bn_stats(d, s),
              reads=[xt_], writes=[pt_])
            E("vector", lambda nc, d=pp[:, 16 * c + 6:16 * c + 12],
              s=xap[:, 512:1024]: nc.vector.bn_stats(d, s),
              reads=[xt_], writes=[pt_])
            E("vector", lambda nc, d=pp[:, 16 * c + 12:16 * c + 14],
              s=pp[:, 16 * c:16 * c + 12]: nc.vector.bn_aggr(d, s),
              reads=[pt_], writes=[pt_])
        VA = pp[:, 13:13 + 16 * nch].rearrange("q (c e) -> q c e",
                                               e=16)[:, :, 0:1]
        T1 = pp[:, 40:40 + nch]
        R = pp[:, 42:42 + nch]
        S = pp[:, 44:44 + nch]
        E("vector", lambda nc, d=T1, s=VA: nc.vector.tensor_scalar(
            d, s, EPS_LN, None, op0=ALU.add), reads=[pt_], writes=[pt_])
        E("vector", lambda nc, d=ip[:, 0:nch], s=T1.bitcast(I32):
          nc.vector.tensor_scalar(d, s, 1, None,
                                  op0=ALU.logical_shift_right),
          reads=[pt_], writes=[it_])
        E("vector", lambda nc, d=ip[:, 0:nch], s=ip[:, 0:nch]:
          nc.vector.tensor_scalar(d, s, -1, 0x5f3759df, op0=ALU.mult,
                                  op1=ALU.add), reads=[it_], writes=[it_])
        E("vector", lambda nc, d=R, s=ip[:, 0:nch].bitcast(F32):
          nc.vector.tensor_copy(d, s), reads=[it_], writes=[pt_])
        for _ in range(2):
            E("vector", lambda nc, d=S, a=R, b=R:
              nc.vector.tensor_tensor(d, a, b, op=ALU.mult),
              reads=[pt_], writes=[pt_])
            E("vector", lambda nc, d=S, a=S, b=T1:
              nc.vector.tensor_tensor(d, a, b, op=ALU.mult),
              reads=[pt_], writes=[pt_])
            E("vector", lambda nc, d=S, s=S: nc.vector.tensor_scalar(
                  d, s, -0.5, 1.5, op0=ALU.mult, op1=ALU.add),
              reads=[pt_], writes=[pt_])
            E("vector", lambda nc, d=R, a=R, b=S:
              nc.vector.tensor_tensor(d, a, b, op=ALU.mult),
              reads=[pt_], writes=[pt_])
        for c, (xap, xt_, dstap, dt_) in enumerate(jobs):
            E("vector", lambda nc, d=dstap, s=xap,
              m_=pp[:, 16 * c + 12:16 * c + 13], r_=R[:, c:c + 1]:
              nc.vector.tensor_scalar(d, s, m_, r_, op0=ALU.subtract,
                                      op1=ALU.mult),
              reads=[xt_, pt_], writes=[dt_])

    def ln_norm(xap, xt_, dstap, dt_):
        ln_multi([(xap, xt_, dstap, dt_)])

    dmain("sync", eye[:, :], eye_t, eye_in[:, :])
    dmain("sync", b1a[:, :], b1a_t, b1a_c[:, :])
    dmain("sync", b1f[:, :], b1f_t, b1f_c[:, :])
    dmain("sync", b2a[0:1, :], b2a_t, b2a_r[:, :])
    dmain("sync", b2f[0:1, :], b2f_t, b2f_r[:, :])
    dmain("sync", grow["g1"][1][0:1, :], grow["g1"][0], g1_r[:, :])
    dmain("sync", grow["be1"][1][0:1, :], grow["be1"][0], be1_r[:, :])
    dmain("sync", grow["g2"][1][0:1, :], grow["g2"][0], g2_r[:, :])
    dmain("sync", grow["be2"][1][0:1, :], grow["be2"][0], be2_r[:, :])
    memset(ones1_t, ones1[0:1, :], 1.0)
    for nm in ("g1", "be1"):
        row_t, row_ap = grow[nm]
        bc_t, bc_ap = gbc[nm]
        for og in range(2):
            bt_, bank = pb[5]
            mm(ones1[0:1, 0:128], ones1_t,
               row_ap[0:1, og * 512:og * 512 + 512], row_t,
               bank[:, :], bt_, start=True, stop=True)
            E("scalar", lambda nc, d=bc_ap[:, og * 512:og * 512 + 512],
              s=bank[:, :]: nc.scalar.copy(d, s),
              reads=[bt_], writes=[bc_t])
    quant_chain(b1f[:, 0:FJ], b1f_t, b1fq[:, 0:FJ], b1fq_t, FJ, 2, -14)
    quant_chain(b2f[0:1, 0:D], b2f_t, b2fq[0:1, 0:D], b2fq_t, D, 2, -14, P=1)

    scope1 = []
    def _enter1(name, shape, dtype):
        cm = nc.sbuf_tensor(name, shape, dtype)
        h = cm.__enter__(); scope1.append(cm); return h
    wtmp_h = [_enter1(f"wtmp{i}", [128, D], F32) for i in range(2)]
    pthi_h = [_enter1(f"pthi{i}", [128, D], BF16) for i in range(2)]
    ptlo_h = [_enter1(f"ptlo{i}", [128, D], BF16) for i in range(2)]
    ptf_h = [_enter1(f"ptf{i}", [128, D], F32) for i in range(2)]
    wkhi_h = _enter1("wkT_hi", [128, DJ * D], BF16)
    wklo_h = _enter1("wkT_lo", [128, DJ * D], BF16)
    wvhi_h = _enter1("wvT_hi", [128, DJ * D], BF16)
    wvlo_h = _enter1("wvT_lo", [128, DJ * D], BF16)
    p1xs_h = [_enter1(f"p1xs{i}", [128, D], F32) for i in range(2)]
    p1xh_h = [_enter1(f"p1xh{i}", [128, D], F32) for i in range(2)]
    p1x2hi_h = [_enter1(f"p1x2hi{i}", [128, D], BF16) for i in range(2)]
    p1x2lo_h = [_enter1(f"p1x2lo{i}", [128, D], BF16) for i in range(2)]
    p1khi_h = [_enter1(f"p1khi{i}", [128, D], BF16) for i in range(2)]
    p1klo_h = [_enter1(f"p1klo{i}", [128, D], BF16) for i in range(2)]
    p1vhi_h = [_enter1(f"p1vhi{i}", [128, H * KVW], BF16) for i in range(2)]
    p1vlo_h = [_enter1(f"p1vlo{i}", [128, H * KVW], BF16) for i in range(2)]
    p1tmp_h = [_enter1("p1tmp0", [128, 512], F32)] * 2
    p1tmpk_h = _enter1("p1tmpk", [128, 512], F32)
    if True:
        wtmpP = [(Tile(f"wtmp{i}"), wtmp_h[i].ap()) for i in range(2)]
        pthiP = [(Tile(f"pthi{i}"), pthi_h[i].ap()) for i in range(2)]
        ptloP = [(Tile(f"ptlo{i}"), ptlo_h[i].ap()) for i in range(2)]
        ptfP = [(Tile(f"ptf{i}"), ptf_h[i].ap()) for i in range(2)]
        wkhi_t, wkhi = Tile("wkhi"), wkhi_h.ap()
        wklo_t, wklo = Tile("wklo"), wklo_h.ap()
        wvhi_t, wvhi = Tile("wvhi"), wvhi_h.ap()
        wvlo_t, wvlo = Tile("wvlo"), wvlo_h.ap()

        def transpose_blocks(src, st_, nsub, sink, banks=(6, 7)):
            for g0 in range(0, nsub, 4):
                bt_, bank = pb[banks[(g0 // 4) % len(banks)]]
                gn = min(4, nsub - g0)
                for gg in range(gn):
                    jj = g0 + gg
                    transp(bank[:, gg * 128:(gg + 1) * 128], bt_,
                           src[:, jj * 128:(jj + 1) * 128], st_)
                sink(g0, gn, bank, bt_)

        _prep_rr = [0]

        def prep_split_one(src_dram, oc, strided_in, dst_hi, dst_lo,
                           dst_hi_t, dst_lo_t, res_hi=None, res_hi_t=None,
                           res_lo=None, res_lo_t=None):
            pi = _prep_rr[0] % 2
            _prep_rr[0] += 1
            wtmp_t, wtmp = wtmpP[pi]
            pthi_t, pthi = pthiP[pi]
            ptlo_t, ptlo = ptloP[pi]
            if strided_in:
                src_ap = src_dram[:, oc * 128:(oc + 1) * 128]
                dmain("scalar",
                      wtmp[:, 0:D].rearrange("q (s c) -> q s c", c=128),
                      wtmp_t,
                      src_ap.rearrange("(s q) c -> q s c", q=128))
            else:
                dmain("scalar", wtmp[:, 0:D], wtmp_t,
                      src_dram[oc * 128:(oc + 1) * 128, :])

            def sink(g0, gn, bank, bt_):
                w = gn * 128
                vcp(pthi[:, g0 * 128:g0 * 128 + w], pthi_t,
                    bank[:, 0:w], bt_)
                vstt(ptlo[:, g0 * 128:g0 * 128 + w], ptlo_t,
                     pthi[:, g0 * 128:g0 * 128 + w], pthi_t, -1.0,
                     bank[:, 0:w], bt_, ALU.mult, ALU.add)
            transpose_blocks(wtmp[:, 0:D], wtmp_t, DJ, sink, banks=(6,))
            if dst_hi is not None:
                E("gpsimd", lambda nc, d=dst_hi[:, oc * D:(oc + 1) * D],
                  s=pthi[:, 0:D]: nc.gpsimd.dma_start(out=d, in_=s),
                  reads=[pthi_t], writes=[dst_hi_t], is_dma=True)
                E("gpsimd", lambda nc, d=dst_lo[:, oc * D:(oc + 1) * D],
                  s=ptlo[:, 0:D]: nc.gpsimd.dma_start(out=d, in_=s),
                  reads=[ptlo_t], writes=[dst_lo_t], is_dma=True)
            else:
                dh = res_hi[:, :].rearrange("q (j d) -> q j d", d=D)[
                    :, :, oc * 128:(oc + 1) * 128]
                dl = res_lo[:, :].rearrange("q (j d) -> q j d", d=D)[
                    :, :, oc * 128:(oc + 1) * 128]
                sh = pthi[:, 0:D].rearrange("q (j r) -> q j r", r=128)
                sl = ptlo[:, 0:D].rearrange("q (j r) -> q j r", r=128)
                E("vector", lambda nc, d=dh, s=sh: nc.vector.tensor_copy(
                    d, s), reads=[pthi_t], writes=[res_hi_t])
                E("vector", lambda nc, d=dl, s=sl: nc.vector.tensor_copy(
                    d, s), reads=[ptlo_t], writes=[res_lo_t])

        def prep_quant_one(src_dram, oc, strided_in, dst, dst_t):
            pi = _prep_rr[0] % 2
            _prep_rr[0] += 1
            wtmp_t, wtmp = wtmpP[pi]
            pthi_t, pthi = pthiP[pi]
            ptf_t, ptf = ptfP[pi]
            if strided_in:
                src_ap = src_dram[:, oc * 128:(oc + 1) * 128]
                dmain("scalar",
                      wtmp[:, 0:D].rearrange("q (s c) -> q s c", c=128),
                      wtmp_t,
                      src_ap.rearrange("(s q) c -> q s c", q=128))
            else:
                dmain("scalar", wtmp[:, 0:D], wtmp_t,
                      src_dram[oc * 128:(oc + 1) * 128, :])
            quant_chain(wtmp[:, 0:D], wtmp_t, ptf[:, 0:D], ptf_t, D, 2, -14)

            def qsink(g0, gn, bank, bt_):
                w = gn * 128
                vcp(pthi[:, g0 * 128:g0 * 128 + w], pthi_t,
                    bank[:, 0:w], bt_)
            transpose_blocks(ptf[:, 0:D], ptf_t, DJ, qsink, banks=(6,))
            E("gpsimd", lambda nc, d=dst[:, oc * D:(oc + 1) * D],
              s=pthi[:, 0:D]: nc.gpsimd.dma_start(out=d, in_=s),
              reads=[pthi_t], writes=[dst_t], is_dma=True)

        for oc in range(DJ):
            prep_split_one(wk, oc, False, None, None, None, None,
                           wkhi, wkhi_t, wklo, wklo_t)
        for oc in range(DJ):
            prep_split_one(wv, oc, False, None, None, None, None,
                           wvhi, wvhi_t, wvlo, wvlo_t)
        prep_units = []
        for oc in range(DJ):
            prep_units.append(lambda oc=oc: prep_split_one(
                wq, oc, False, s_wq_hi, s_wq_lo,
                scr_t["s_wq_hi"], scr_t["s_wq_lo"]))
        for oc in range(FJ):
            prep_units.append(lambda oc=oc: prep_split_one(
                w1a, oc, False, s_w1a_hi, s_w1a_lo,
                scr_t["s_w1a_hi"], scr_t["s_w1a_lo"]))
        for oc in range(FJ):
            prep_units.append(lambda oc=oc: prep_split_one(
                w2a, oc, True, s_w2a_hi, s_w2a_lo,
                scr_t["s_w2a_hi"], scr_t["s_w2a_lo"]))
        for oc in range(FJ):
            prep_units.append(lambda oc=oc: prep_quant_one(
                W1f, oc, False, s_W1q, scr_t["s_W1q"]))
        for oc in range(FJ):
            prep_units.append(lambda oc=oc: prep_quant_one(
                W2f, oc, True, s_W2q, scr_t["s_W2q"]))
        for u in prep_units[:DJ]:
            u()
        prep_units = prep_units[DJ:]

        p1tmp_shared = (Tile("p1tmp0"), p1tmp_h[0].ap())
        p1tmpk_shared = (Tile("p1tmpk"), p1tmpk_h.ap())
        P1 = [dict(
            tmp=p1tmp_shared,
            tmpk=p1tmpk_shared,
            xs=(Tile(f"p1xs{i}"), p1xs_h[i].ap()),
            xh=(Tile(f"p1xh{i}"), p1xh_h[i].ap()),
            x2hi=(Tile(f"p1x2hi{i}"), p1x2hi_h[i].ap()),
            x2lo=(Tile(f"p1x2lo{i}"), p1x2lo_h[i].ap()),
            khi=(Tile(f"p1khi{i}"), p1khi_h[i].ap()),
            klo=(Tile(f"p1klo{i}"), p1klo_h[i].ap()),
            vhi=(Tile(f"p1vhi{i}"), p1vhi_h[i].ap()),
            vlo=(Tile(f"p1vlo{i}"), p1vlo_h[i].ap()),
        ) for i in range(2)]
        for i in range(2):
            memset(P1[i]["vhi"][0], P1[i]["vhi"][1][:, :], 0.0)
            memset(P1[i]["vlo"][0], P1[i]["vlo"][1][:, :], 0.0)
            for h in range(H):
                memset(P1[i]["vhi"][0],
                       P1[i]["vhi"][1][:, h * KVW + 128:h * KVW + 129], 1.0)

        kv_started = [False, False, False]
        nprep = len(prep_units)

        def p1_ln(si):
            par = P1[si % 2]
            p1xs_t, p1xs = par["xs"]
            p1xh_t, p1xh = par["xh"]
            dmain("sync", p1xs[:, :], p1xs_t, xb[si * 128:(si + 1) * 128, :])
            ln_norm(p1xs, p1xs_t, p1xh[:, :], p1xh_t)
            E("vector", lambda nc, d=p1xh[:, :], a=p1xh[:, :],
              b=gbc["g1"][1][:, :]: nc.vector.tensor_tensor(
                  d, a, b, op=ALU.mult),
              reads=[p1xh_t, gbc["g1"][0]], writes=[p1xh_t])
            E("vector", lambda nc, d=p1xh[:, :], a=p1xh[:, :],
              b=gbc["be1"][1][:, :]: nc.vector.tensor_tensor(
                  d, a, b, op=ALU.add),
              reads=[p1xh_t, gbc["be1"][0]], writes=[p1xh_t])

        def p1_tr(si):
            par = P1[si % 2]
            p1xh_t, p1xh = par["xh"]
            p1x2hi_t, p1x2hi = par["x2hi"]
            p1x2lo_t, p1x2lo = par["x2lo"]

            def split_sink_round(g0, bank, bt_):
                E("scalar", lambda nc, d=p1x2hi[:, g0 * 128:g0 * 128 + 512],
                  s=bank[:, :]: nc.scalar.copy(d, s),
                  reads=[bt_], writes=[p1x2hi_t])
                E("vector", lambda nc, d=p1x2lo[:, g0 * 128:g0 * 128 + 512],
                  a=p1x2hi[:, g0 * 128:g0 * 128 + 512], b=bank[:, :]:
                  nc.vector.scalar_tensor_tensor(
                      d, a, -1.0, b, op0=ALU.mult, op1=ALU.add),
                  reads=[bt_, p1x2hi_t], writes=[p1x2lo_t])
            for g0 in range(0, DJ, 4):
                bt_, bank = pb[7 if (g0 // 4) % 2 == 0 else 6]
                for gg in range(4):
                    jj = g0 + gg
                    transp(bank[:, gg * 128:(gg + 1) * 128], bt_,
                           p1xh[:, jj * 128:(jj + 1) * 128], p1xh_t)
                split_sink_round(g0, bank, bt_)

        def p1_kv(si):
            par = P1[si % 2]
            p1x2hi_t, p1x2hi = par["x2hi"]
            p1x2lo_t, p1x2lo = par["x2lo"]
            p1khi_t, p1khi = par["khi"]
            p1klo_t, p1klo = par["klo"]
            p1vhi_t, p1vhi = par["vhi"]
            p1vlo_t, p1vlo = par["vlo"]
            p1tmp_t, p1tmp = par["tmp"]
            p1tmpk_t, p1tmpk = par["tmpk"]
            for (whi_ap, whi_t2, wlo_ap, wlo_t2, isk) in (
                    (wkhi, wkhi_t, wklo, wklo_t, True),
                    (wvhi, wvhi_t, wvlo, wvlo_t, False)):
                for og in range(2):
                    if isk:
                        bt_, bank = pb[3 + og]
                    else:
                        bt_, bank = pb[5 if og == 0 else 3]
                    first = True
                    for dj in range(DJ):
                        lh = p1x2hi[:, dj * 128:(dj + 1) * 128]
                        ll = p1x2lo[:, dj * 128:(dj + 1) * 128]
                        rh = whi_ap[:, dj * D + og * 512:
                                    dj * D + og * 512 + 512]
                        rl = wlo_ap[:, dj * D + og * 512:
                                    dj * D + og * 512 + 512]
                        mm(lh, p1x2hi_t, rh, whi_t2, bank[:, :], bt_,
                           start=first, stop=False)
                        first = False
                        mm(lh, p1x2hi_t, rl, wlo_t2, bank[:, :], bt_,
                           start=False, stop=False)
                        mm(ll, p1x2lo_t, rh, whi_t2, bank[:, :], bt_,
                           start=False, stop=(dj == DJ - 1))
                    if isk:
                        sact(p1tmp[:, :], p1tmp_t, bank[:, :], bt_, AF.Relu,
                             scale=-1.0)
                        sact(p1tmp[:, :], p1tmp_t, p1tmp[:, :], p1tmp_t,
                             AF.Exp, scale=-1.0)
                        E("vector", lambda nc, d=p1tmpk[:, :],
                          s=bank[:, :], e2=p1tmp[:, :]:
                          nc.vector.scalar_tensor_tensor(
                              d, s, 0.0, e2, op0=ALU.max, op1=ALU.add),
                          reads=[bt_, p1tmp_t], writes=[p1tmpk_t])
                        E("scalar", lambda nc,
                          d=p1khi[:, og * 512:og * 512 + 512],
                          s=p1tmpk[:, :]: nc.scalar.copy(d, s),
                          reads=[p1tmpk_t], writes=[p1khi_t])
                        E("vector", lambda nc,
                          d=p1klo[:, og * 512:og * 512 + 512],
                          a=p1khi[:, og * 512:og * 512 + 512],
                          b=p1tmpk[:, :]: nc.vector.scalar_tensor_tensor(
                              d, a, -1.0, b, op0=ALU.mult, op1=ALU.add),
                          reads=[p1tmpk_t, p1khi_t], writes=[p1klo_t])
                    else:
                        h0 = og * 4
                        vdst = p1vhi[:, h0 * KVW:(h0 + 4) * KVW].rearrange(
                            "q (h w) -> q h w", w=KVW)[:, :, 0:128]
                        ldst = p1vlo[:, h0 * KVW:(h0 + 4) * KVW].rearrange(
                            "q (h w) -> q h w", w=KVW)[:, :, 0:128]
                        psrc = bank[:, :].rearrange("q (h w) -> q h w", w=128)
                        E("scalar", lambda nc, d=vdst, s=psrc:
                          nc.scalar.copy(d, s), reads=[bt_], writes=[p1vhi_t])
                        E("vector", lambda nc, d=ldst, a=vdst, b=psrc:
                          nc.vector.scalar_tensor_tensor(
                              d, a, -1.0, b, op0=ALU.mult, op1=ALU.add),
                          reads=[bt_, p1vhi_t], writes=[p1vlo_t])
            for h in range(H):
                bt_, bank = pb[h // 3]
                off = (h % 3) * KVW
                dst = bank[:, off:off + KVW]
                lh = p1khi[:, h * 128:(h + 1) * 128]
                ll = p1klo[:, h * 128:(h + 1) * 128]
                rh = p1vhi[:, h * KVW:(h + 1) * KVW]
                rl = p1vlo[:, h * KVW:(h + 1) * KVW]
                st = not kv_started[h // 3]
                kv_started[h // 3] = True
                last = (si == SC - 1) and (h % 3 == 2 or h == 7)
                mm(lh, p1khi_t, rh, p1vhi_t, dst, bt_, start=st, stop=False)
                mm(lh, p1khi_t, rl, p1vlo_t, dst, bt_, start=False,
                   stop=False)
                mm(ll, p1klo_t, rh, p1vhi_t, dst, bt_, start=False,
                   stop=last)

        p1_ln(0)
        p1_tr(0)
        for si in range(SC):
            lo_u = (si * nprep) // SC
            hi_u = ((si + 1) * nprep) // SC
            for u in prep_units[lo_u:hi_u]:
                u()
            if si + 1 < SC:
                p1_ln(si + 1)
            p1_kv(si)
            if si + 1 < SC:
                p1_tr(si + 1)
        for h in range(H):
            bt_, bank = pb[h // 3]
            off = (h % 3) * KVW
            E("scalar", lambda nc, d=kvh[:, h * KVW:(h + 1) * KVW],
              s=bank[:, off:off + KVW]: nc.scalar.copy(d, s),
              reads=[bt_], writes=[kvh_t])
            E("vector", lambda nc, d=kvl[:, h * KVW:(h + 1) * KVW],
              a=kvh[:, h * KVW:(h + 1) * KVW], b=bank[:, off:off + KVW]:
              nc.vector.scalar_tensor_tensor(
                  d, a, -1.0, b, op0=ALU.mult, op1=ALU.add),
              reads=[bt_, kvh_t], writes=[kvl_t])

    for cm in reversed(scope1):
        cm.__exit__(None, None, None)
    p.barrier()

    scope2 = []
    def _enter2(name, shape, dtype):
        cm = nc.sbuf_tensor(name, shape, dtype)
        h = cm.__enter__(); scope2.append(cm); return h
    xt2_h = [_enter2(f"xt2_{i}", [128, TC * D], F32) for i in range(2)]
    xorig_h = [_enter2("xorig0", [128, TC * D], F32)] * 2
    xh2_h = [_enter2(f"xh2_{i}", [128, D], F32) for i in range(2)]
    x2hi2_h = _enter2("x2hi2", [128, DJ * TT], BF16)
    x2lo2_h = _enter2("x2lo2", [128, DJ * TT], BF16)
    wbufhi_h = [_enter2(f"wbuf_hi{i}", [128, 2 * D], BF16) for i in range(2)]
    wbuflo_h = [_enter2(f"wbuf_lo{i}", [128, 2 * D], BF16) for i in range(2)]
    w2blo_h = [_enter2(f"w2b_lo{i}", [128, 2 * D], BF16) for i in range(2)]
    Qhi_h = _enter2("Qhi", [128, H * TT], BF16)
    Qlo_h = _enter2("Qlo", [128, H * TT], BF16)
    qtmp_h = [_enter2(f"qtmp{i}", [128, TT], F32) for i in range(2)]
    qtmp2_h = [_enter2(f"qtmp2_{i}", [128, TT], F32) for i in range(2)]
    hgrp_h = [_enter2(f"hgrp{i}", [128, 4 * TT], F32) for i in range(2)]
    hgrphi_h = [_enter2(f"hgrp_hi{i}", [128, 4 * TT], BF16) for i in range(2)]
    hgrplo_h = [_enter2(f"hgrp_lo{i}", [128, 4 * TT], BF16) for i in range(2)]
    y1g_h = [_enter2(f"y1g{i}", [128, 4 * TT], F32) for i in range(2)]
    hqg_h = [_enter2(f"hqg{i}", [128, 4 * TT], BF16) for i in range(2)]
    xq2_h = _enter2("xq2", [128, DJ * TT], BF16)
    y2f_h = _enter2("y2f", [128, TC * D], F32)
    dmt_h = [_enter2(f"dmt{i}", [128, TC * D], BF16) for i in range(2)]
    gb2_h = {nm: _enter2(nm + "bc", [128, D], F32) for nm in ("g2", "be2")}
    if True:
        xt2P = [(Tile(f"xt2_{i}"), xt2_h[i].ap()) for i in range(2)]
        _xo = (Tile("xorig0"), xorig_h[0].ap())
        xorigP = [_xo, _xo]
        xh2P = [(Tile(f"xh2_{i}"), xh2_h[i].ap()) for i in range(2)]
        x2f2_t, x2f2 = None, None
        x2hi2_t, x2hi2 = Tile("x2hi2"), x2hi2_h.ap()
        x2lo2_t, x2lo2 = Tile("x2lo2"), x2lo2_h.ap()
        wbufhiP = [(Tile(f"wbufhi{i}"), wbufhi_h[i].ap()) for i in range(2)]
        wbufloP = [(Tile(f"wbuflo{i}"), wbuflo_h[i].ap()) for i in range(2)]
        w2bloP = [(Tile(f"w2blo{i}"), w2blo_h[i].ap()) for i in range(2)]
        Qhi_t, Qhi = Tile("Qhi"), Qhi_h.ap()
        Qlo_t, Qlo = Tile("Qlo"), Qlo_h.ap()
        w2bhiP = [(Qhi_t, Qhi), (Qlo_t, Qlo)]
        qtmpP = [(Tile(f"qtmp{i}"), qtmp_h[i].ap()) for i in range(2)]
        qtmp2P = [(Tile(f"qtmp2_{i}"), qtmp2_h[i].ap()) for i in range(2)]
        hgrpP = [(Tile(f"hgrp{i}"), hgrp_h[i].ap()) for i in range(2)]
        hgrphiP = [(Tile(f"hgrphi{i}"), hgrphi_h[i].ap()) for i in range(2)]
        hgrploP = [(Tile(f"hgrplo{i}"), hgrplo_h[i].ap()) for i in range(2)]
        y1gP = [(Tile(f"y1g{i}"), y1g_h[i].ap()) for i in range(2)]
        hqgP = [(Tile(f"hqg{i}"), hqg_h[i].ap()) for i in range(2)]
        xq2_t, xq2 = Tile("xq2"), xq2_h.ap()
        y2f_t, y2f = Tile("y2f"), y2f_h.ap()
        x2f2_t, x2f2 = y2f_t, y2f
        dmtP = [(Tile(f"dmt{i}"), dmt_h[i].ap()) for i in range(2)]
        for nm in ("g2", "be2"):
            gbc[nm] = (Tile(nm + "bc"), gb2_h[nm].ap())
            row_t, row_ap = grow[nm]
            bc_t, bc_ap = gbc[nm]
            for og in range(2):
                bt_, bank = pb[5]
                mm(ones1[0:1, 0:128], ones1_t,
                   row_ap[0:1, og * 512:og * 512 + 512], row_t,
                   bank[:, :], bt_, start=True, stop=True)
                E("scalar", lambda nc, d=bc_ap[:, og * 512:og * 512 + 512],
                  s=bank[:, :]: nc.scalar.copy(d, s),
                  reads=[bt_], writes=[bc_t])

        def transpose_fold2(src_t, src_ap, gname, bname):
            ln_multi([(src_ap[:, c * D:(c + 1) * D], src_t,
                       xh2P[c % 2][1][:, :], xh2P[c % 2][0])
                      for c in range(TC)])
            for c in range(TC):
                xh2_t, xh2 = xh2P[c % 2]
                E("vector", lambda nc, d=xh2[:, :], a=xh2[:, :],
                  b=gbc[gname][1][:, :]: nc.vector.tensor_tensor(
                      d, a, b, op=ALU.mult),
                  reads=[xh2_t, gbc[gname][0]], writes=[xh2_t])
                E("vector", lambda nc, d=xh2[:, :], a=xh2[:, :],
                  b=gbc[bname][1][:, :]: nc.vector.tensor_tensor(
                      d, a, b, op=ALU.add),
                  reads=[xh2_t, gbc[bname][0]], writes=[xh2_t])
                for g0 in range(0, DJ, 4):
                    bt_, bank = pb[6 + (g0 // 4) % 2]
                    for gg in range(4):
                        jj = g0 + gg
                        transp(bank[:, gg * 128:(gg + 1) * 128], bt_,
                               xh2[:, jj * 128:(jj + 1) * 128], xh2_t)
                    dsth = x2hi2[:, :].rearrange(
                        "q (j r) -> q j r", r=TT)[:, g0:g0 + 4,
                                                  c * 128:c * 128 + 128]
                    dstl = x2lo2[:, :].rearrange(
                        "q (j r) -> q j r", r=TT)[:, g0:g0 + 4,
                                                  c * 128:c * 128 + 128]
                    psrc = bank[:, :].rearrange("q (j r) -> q j r", r=128)
                    E("scalar", lambda nc, d=dsth, s=psrc:
                      nc.scalar.copy(d, s), reads=[bt_], writes=[x2hi2_t])
                    E("vector", lambda nc, d=dstl, a=dsth, b=psrc:
                      nc.vector.scalar_tensor_tensor(
                          d, a, -1.0, b, op0=ALU.mult, op1=ALU.add),
                      reads=[bt_, x2hi2_t], writes=[x2lo2_t])

        def tile_body(ti):
            tp = ti % 2
            xt2_t, xt2 = xt2P[tp]
            xorig_t, xorig = xorigP[tp]
            dmt_t, dmt = dmtP[tp]
            def head1():
                rows_r = xb[ti * TT:(ti + 1) * TT, :].rearrange("(c q) d -> q c d", q=128)
                xt2_3d = xt2[:, :].rearrange("q (c d) -> q c d", d=D)
                xo_3d = xorig[:, :].rearrange("q (c d) -> q c d", d=D)
                dmain("sync", xt2_3d, xt2_t, rows_r)
                dmain("sync", xo_3d, xorig_t, rows_r)
                dm_r = dmask[ti * TT:(ti + 1) * TT, :].rearrange("(c q) d -> q c d", q=128)
                dmain("sync", dmt[:, :].rearrange("q (c d) -> q c d", d=D), dmt_t, dm_r)

                transpose_fold2(xt2_t, xt2, "g1", "be1")

                for oc in range(H):
                    ws = (oc // 2) % 2
                    wbufhi_t, wbufhi = wbufhiP[ws]
                    wbuflo_t, wbuflo = wbufloP[ws]
                    qtmp_t, qtmp = qtmpP[oc % 2]
                    qtmp2_t, qtmp2 = qtmp2P[oc % 2]
                    if oc % 2 == 0:
                        E("gpsimd", lambda nc, d=wbufhi[:, 0:2 * D],
                          s=s_wq_hi[:, oc * D:(oc + 2) * D]:
                          nc.gpsimd.dma_start(out=d, in_=s),
                          reads=[scr_t["s_wq_hi"]], writes=[wbufhi_t],
                          is_dma=True)
                        E("gpsimd", lambda nc, d=wbuflo[:, 0:2 * D],
                          s=s_wq_lo[:, oc * D:(oc + 2) * D]:
                          nc.gpsimd.dma_start(out=d, in_=s),
                          reads=[scr_t["s_wq_lo"]], writes=[wbuflo_t],
                          is_dma=True)
                    off = (oc % 2) * D
                    bt_, bank = pb[oc % 4]
                    first = True
                    for dj in range(DJ):
                        lh = wbufhi[:, off + dj * 128:off + (dj + 1) * 128]
                        ll = wbuflo[:, off + dj * 128:off + (dj + 1) * 128]
                        rh = x2hi2[:, dj * TT:(dj + 1) * TT]
                        rl = x2lo2[:, dj * TT:(dj + 1) * TT]
                        mm(lh, wbufhi_t, rh, x2hi2_t, bank[:, 0:TT], bt_,
                           start=first, stop=False)
                        first = False
                        mm(lh, wbufhi_t, rl, x2lo2_t, bank[:, 0:TT], bt_,
                           start=False, stop=False)
                        mm(ll, wbuflo_t, rh, x2hi2_t, bank[:, 0:TT], bt_,
                           start=False, stop=(dj == DJ - 1))
                    sact(qtmp[:, :], qtmp_t, bank[:, 0:TT], bt_, AF.Relu,
                         scale=-1.0)
                    sact(qtmp[:, :], qtmp_t, qtmp[:, :], qtmp_t, AF.Exp,
                         scale=-1.0)
                    E("vector", lambda nc, d=qtmp2[:, :], s=bank[:, 0:TT],
                      e2=qtmp[:, :]: nc.vector.scalar_tensor_tensor(
                          d, s, 0.0, e2, op0=ALU.max, op1=ALU.add),
                      reads=[bt_, qtmp_t], writes=[qtmp2_t])
                    E("scalar", lambda nc, d=Qhi[:, oc * TT:(oc + 1) * TT],
                      s=qtmp2[:, :]: nc.scalar.copy(d, s),
                      reads=[qtmp2_t], writes=[Qhi_t])
                    vstt(Qlo[:, oc * TT:(oc + 1) * TT], Qlo_t,
                         Qhi[:, oc * TT:(oc + 1) * TT], Qhi_t, -1.0,
                         qtmp2[:, :], qtmp2_t, ALU.mult, ALU.add)
                    drain_fill(4)

            def head2():
                for h in range(H):
                    for c in range(TC):
                        bt_, bank = pb[6 + (h * TC + c) % 2]
                        z_tt, z_ap = zP[(h * TC + c) % 2]
                        lh = Qhi[:, h * TT + c * 128:h * TT + (c + 1) * 128]
                        ll = Qlo[:, h * TT + c * 128:h * TT + (c + 1) * 128]
                        rh = kvh[:, h * KVW:(h + 1) * KVW]
                        rl = kvl[:, h * KVW:(h + 1) * KVW]
                        dst = bank[:, 0:KVW]
                        mm(lh, Qhi_t, rh, kvh_t, dst, bt_, start=True, stop=False)
                        mm(lh, Qhi_t, rl, kvl_t, dst, bt_, start=False,
                           stop=False)
                        mm(ll, Qlo_t, rh, kvh_t, dst, bt_, start=False, stop=True)
                        E("vector", lambda nc, d=z_ap[:, :], s=bank[:, 128:129]:
                          nc.vector.tensor_scalar(d, s, EPS_ATTN, None,
                                                  op0=ALU.add),
                          reads=[bt_], writes=[z_tt])
                        E("vector", lambda nc, z=z_ap: nc.vector.reciprocal(
                            z[:, :], z[:, :]), reads=[z_tt], writes=[z_tt])
                        xap = xt2[:, c * D + h * 128:c * D + (h + 1) * 128]
                        E("vector", lambda nc, d=xap, s=bank[:, 0:128],
                          xi=xap, z=z_ap[:, 0:1]:
                          nc.vector.scalar_tensor_tensor(
                              d, s, z, xi, op0=ALU.mult, op1=ALU.add),
                          reads=[bt_, z_tt, xt2_t], writes=[xt2_t])
                        drain_fill(1)

                transpose_fold2(xt2_t, xt2, "g2", "be2")
            def ffnA():

                def a_w1_block(grp):
                    gp = grp % 2
                    hgrp_t, hgrp = hgrpP[gp]
                    hgrphi_t, hgrphi = hgrphiP[gp]
                    hgrplo_t, hgrplo = hgrploP[gp]
                    for gg in range(4):
                        oc = grp * 4 + gg
                        ws = (oc // 2) % 2
                        off = (oc % 2) * D
                        wbufhi_t, wbufhi = wbufhiP[ws]
                        wbuflo_t, wbuflo = wbufloP[ws]
                        if oc % 2 == 0:
                            E("gpsimd", lambda nc, d=wbufhi[:, 0:2 * D],
                              s=s_w1a_hi[:, oc * D:(oc + 2) * D]:
                              nc.gpsimd.dma_start(out=d, in_=s),
                              reads=[scr_t["s_w1a_hi"]], writes=[wbufhi_t],
                              is_dma=True)
                            E("gpsimd", lambda nc, d=wbuflo[:, 0:2 * D],
                              s=s_w1a_lo[:, oc * D:(oc + 2) * D]:
                              nc.gpsimd.dma_start(out=d, in_=s),
                              reads=[scr_t["s_w1a_lo"]], writes=[wbuflo_t],
                              is_dma=True)
                        bt_, bank = pb[4 + oc % 4]
                        first = True
                        for dj in range(DJ):
                            lh = wbufhi[:, off + dj * 128:off + (dj + 1) * 128]
                            ll = wbuflo[:, off + dj * 128:off + (dj + 1) * 128]
                            rh = x2hi2[:, dj * TT:(dj + 1) * TT]
                            rl = x2lo2[:, dj * TT:(dj + 1) * TT]
                            mm(lh, wbufhi_t, rh, x2hi2_t, bank[:, 0:TT], bt_,
                               start=first, stop=False)
                            first = False
                            mm(lh, wbufhi_t, rl, x2lo2_t, bank[:, 0:TT], bt_,
                               start=False, stop=False)
                            mm(ll, wbuflo_t, rh, x2hi2_t, bank[:, 0:TT], bt_,
                               start=False, stop=(dj == DJ - 1))
                        E("scalar", lambda nc, d=hgrp[:, gg * TT:(gg + 1) * TT],
                          s=bank[:, 0:TT], b=b1a[:, oc:oc + 1]:
                          nc.scalar.activation(d, s, AF.Gelu, bias=b, scale=1.0),
                          reads=[bt_, b1a_t], writes=[hgrp_t])
                        E("scalar", lambda nc,
                          d=hgrphi[:, gg * TT:(gg + 1) * TT],
                          s=hgrp[:, gg * TT:(gg + 1) * TT]:
                          nc.scalar.copy(d, s), reads=[hgrp_t],
                          writes=[hgrphi_t])
                        vstt(hgrplo[:, gg * TT:(gg + 1) * TT], hgrplo_t,
                             hgrphi[:, gg * TT:(gg + 1) * TT], hgrphi_t, -1.0,
                             hgrp[:, gg * TT:(gg + 1) * TT], hgrp_t,
                             ALU.mult, ALU.add)

                def a_w2_block(grp):
                    gp = grp % 2
                    hgrphi_t, hgrphi = hgrphiP[gp]
                    hgrplo_t, hgrplo = hgrploP[gp]
                    for gg in range(4):
                        dffc = grp * 4 + gg
                        ws = (dffc // 2) % 2
                        off = (dffc % 2) * D
                        w2bhi_t, w2bhi = w2bhiP[ws]
                        w2blo_t, w2blo = w2bloP[ws]
                        if dffc % 2 == 0:
                            E("gpsimd", lambda nc, d=w2bhi[:, 0:2 * D],
                              s=s_w2a_hi[:, dffc * D:(dffc + 2) * D]:
                              nc.gpsimd.dma_start(out=d, in_=s),
                              reads=[scr_t["s_w2a_hi"]], writes=[w2bhi_t],
                              is_dma=True)
                            E("gpsimd", lambda nc, d=w2blo[:, 0:2 * D],
                              s=s_w2a_lo[:, dffc * D:(dffc + 2) * D]:
                              nc.gpsimd.dma_start(out=d, in_=s),
                              reads=[scr_t["s_w2a_lo"]], writes=[w2blo_t],
                              is_dma=True)
                        for c in range(TC):
                            lh = hgrphi[:, gg * TT + c * 128:
                                        gg * TT + (c + 1) * 128]
                            ll = hgrplo[:, gg * TT + c * 128:
                                        gg * TT + (c + 1) * 128]
                            for og in range(2):
                                bt_, bank = pb[c * 2 + og]
                                rh = w2bhi[:, off + og * 512:
                                           off + og * 512 + 512]
                                rl = w2blo[:, off + og * 512:
                                           off + og * 512 + 512]
                                mm(lh, hgrphi_t, rh, w2bhi_t, bank[:, :], bt_,
                                   start=(dffc == 0), stop=False)
                                mm(lh, hgrphi_t, rl, w2blo_t, bank[:, :], bt_,
                                   start=False, stop=False)
                                mm(ll, hgrplo_t, rh, w2bhi_t, bank[:, :], bt_,
                                   start=False, stop=False)

                a_w1_block(0)
                drain_fill(4)
                for grp in range(1, FJ // 4):
                    a_w1_block(grp)
                    drain_fill(4)
                    a_w2_block(grp - 1)
                    drain_fill(4)
                a_w2_block(FJ // 4 - 1)
                drain_fill()
                for c in range(TC):
                    for og in range(2):
                        bt_, bank = pb[c * 2 + og]
                        mm(ones1[0:1, 0:128], ones1_t,
                           b2a[0:1, og * 512:og * 512 + 512], b2a_t,
                           bank[:, :], bt_, start=False, stop=True)
                        xap = xt2[:, c * D + og * 512:c * D + og * 512 + 512]
                        E("vector", lambda nc, d=xap, a=xap, b=bank[:, :]:
                          nc.vector.tensor_tensor(d, a, b, op=ALU.add),
                          reads=[bt_, xt2_t], writes=[xt2_t])

                for c in range(TC):
                    xap = xt2[:, c * D:(c + 1) * D]
                    E("vector", lambda nc, d=xap, a=xap,
                      b=xorig[:, c * D:(c + 1) * D]:
                      nc.vector.tensor_tensor(d, a, b, op=ALU.add),
                      reads=[xt2_t, xorig_t], writes=[xt2_t])
                    xh2_t, xh2 = xh2P[c % 2]
                    ln_norm(xap, xt2_t, xh2[:, :], xh2_t)
                    vcp(xap, xt2_t, xh2[:, :], xh2_t)

                for c in range(TC):
                    for g0 in range(0, DJ, 4):
                        bt_, bank = pb[6 + (g0 // 4) % 2]
                        for gg in range(4):
                            jj = g0 + gg
                            transp(bank[:, gg * 128:(gg + 1) * 128], bt_,
                                   xt2[:, c * D + jj * 128:c * D + (jj + 1) * 128],
                                   xt2_t)
                        dstf = x2f2[:, :].rearrange(
                            "q (j r) -> q j r", r=TT)[:, g0:g0 + 4,
                                                      c * 128:c * 128 + 128]
                        psrc = bank[:, :].rearrange("q (j r) -> q j r", r=128)
                        E("scalar", lambda nc, d=dstf, s=psrc:
                          nc.scalar.copy(d, s), reads=[bt_], writes=[x2f2_t])
            def ffnQ():
                for blk in range(DJ * TT // QW):
                    quant_chain(x2f2[:, blk * QW:(blk + 1) * QW], x2f2_t,
                                xq2[:, blk * QW:(blk + 1) * QW], xq2_t,
                                QW, 2, -14)

            def ffnB():
                drain_fill()

                def q_w1_block(grp):
                    gp = grp % 2
                    y1g_t, y1g = y1gP[gp]
                    hqg_t, hqg = hqgP[gp]
                    for gg in range(4):
                        oc = grp * 4 + gg
                        ws = (oc // 2) % 2
                        off = (oc % 2) * D
                        wbufhi_t, wbufhi = wbufhiP[ws]
                        if oc % 2 == 0:
                            E("gpsimd", lambda nc, d=wbufhi[:, 0:2 * D],
                              s=s_W1q[:, oc * D:(oc + 2) * D]:
                              nc.gpsimd.dma_start(out=d, in_=s),
                              reads=[scr_t["s_W1q"]], writes=[wbufhi_t],
                              is_dma=True)
                        bt_, bank = pb[4 + oc % 4]
                        first = True
                        for dj in range(DJ):
                            l_ = wbufhi[:, off + dj * 128:off + (dj + 1) * 128]
                            r_ = xq2[:, dj * TT:(dj + 1) * TT]
                            mm(l_, wbufhi_t, r_, xq2_t, bank[:, 0:TT], bt_,
                               start=first, stop=(dj == DJ - 1))
                            first = False
                        E("scalar", lambda nc, d=y1g[:, gg * TT:(gg + 1) * TT],
                          s=bank[:, 0:TT], b=b1fq[:, oc:oc + 1]:
                          nc.scalar.activation(d, s, AF.Identity, bias=b, scale=1.0),
                          reads=[bt_, b1fq_t], writes=[y1g_t])
                    hq_chain(y1g[:, 0:4 * TT], y1g_t, hqg[:, 0:4 * TT], hqg_t,
                             4 * TT)

                def q_w2_block(grp):
                    gp = grp % 2
                    hqg_t, hqg = hqgP[gp]
                    for gg in range(4):
                        dffc = grp * 4 + gg
                        ws = (dffc // 2) % 2
                        off = (dffc % 2) * D
                        w2blo_t, w2blo = w2bloP[ws]
                        if dffc % 2 == 0:
                            E("gpsimd", lambda nc, d=w2blo[:, 0:2 * D],
                              s=s_W2q[:, dffc * D:(dffc + 2) * D]:
                              nc.gpsimd.dma_start(out=d, in_=s),
                              reads=[scr_t["s_W2q"]], writes=[w2blo_t],
                              is_dma=True)
                        for c in range(TC):
                            l_ = hqg[:, gg * TT + c * 128:gg * TT + (c + 1) * 128]
                            for og in range(2):
                                bt_, bank = pb[c * 2 + og]
                                r_ = w2blo[:, off + og * 512:
                                           off + og * 512 + 512]
                                mm(l_, hqg_t, r_, w2blo_t, bank[:, :], bt_,
                                   start=(dffc == 0), stop=False)

                q_w1_block(0)
                for grp in range(1, FJ // 4):
                    q_w1_block(grp)
                    q_w2_block(grp - 1)
                q_w2_block(FJ // 4 - 1)
                for c in range(TC):
                    for og in range(2):
                        bt_, bank = pb[c * 2 + og]
                        mm(ones1[0:1, 0:128], ones1_t,
                           b2fq[0:1, og * 512:og * 512 + 512], b2fq_t,
                           bank[:, :], bt_, start=False, stop=True)
                        vcp(y2f[:, c * D + og * 512:c * D + og * 512 + 512],
                            y2f_t, bank[:, :], bt_)
            def tail():
                for blk in range(TC * D // QW):
                    quant_chain(y2f[:, blk * QW:(blk + 1) * QW], y2f_t,
                                y2f[:, blk * QW:(blk + 1) * QW], y2f_t,
                                QW, 7, -95)
                for c in range(TC):
                    xap = xt2[:, c * D:(c + 1) * D]
                    E("vector", lambda nc, d=y2f[:, c * D:(c + 1) * D],
                      a=y2f[:, c * D:(c + 1) * D], b=dmt[:, c * D:(c + 1) * D]:
                      nc.vector.tensor_tensor(d, a, b, op=ALU.mult),
                      reads=[y2f_t, dmt_t], writes=[y2f_t])
                    E("vector", lambda nc, d=xap, a=xap,
                      b=y2f[:, c * D:(c + 1) * D]:
                      nc.vector.tensor_tensor(d, a, b, op=ALU.add),
                      reads=[xt2_t, y2f_t], writes=[xt2_t])
                    xh2_t, xh2 = xh2P[c % 2]
                    ln_norm(xap, xt2_t, xh2[:, :], xh2_t)
                    vcp(xap, xt2_t, xh2[:, :], xh2_t)
                orow = out[ti * TT:(ti + 1) * TT, :].rearrange("(c q) d -> q c d", q=128)
                dmaout("gpsimd", orow, xt2[:, :].rearrange("q (c d) -> q c d", d=D), xt2_t)
            return head1, head2, ffnA, ffnQ, ffnB, tail

        stages = [tile_body(ti) for ti in range(NTILE)]
        H1, H2, FA, FQ, FB, TL = 0, 1, 2, 3, 4, 5
        stages[0][H1](); stages[0][H2](); stages[0][FA]()
        for ti in range(NTILE):
            p._cap = []
            stages[ti][FQ]()
            fill_q.extend(p._cap); p._cap = None
            if ti + 1 < NTILE:
                stages[ti + 1][H1]()
                stages[ti + 1][H2]()
            stages[ti][FB]()
            p._cap = []
            stages[ti][TL]()
            fill_q.extend(p._cap); p._cap = None
            if ti + 1 < NTILE:
                stages[ti + 1][FA]()
        drain_fill()


        last_dma = [op.idx for op in p.ops
                    if op.is_dma and op.engine == "gpsimd"]
        fidx = p.emit("gpsimd", lambda nc: None, extra_deps=last_dma[-NTILE:])
        p.ops[fidx].virtual = True

    for cm in reversed(scope2):
        cm.__exit__(None, None, None)
    p.finalize()
    with nc.Block() as block:
        @block.sync
        def _(eng):
            p.run_engine(nc, "sync", eng, sems, dma_sems)

        @block.tensor
        def _(eng):
            p.run_engine(nc, "tensor", eng, sems, dma_sems)

        @block.vector
        def _(eng):
            p.run_engine(nc, "vector", eng, sems, dma_sems)

        @block.scalar
        def _(eng):
            p.run_engine(nc, "scalar", eng, sems, dma_sems)

        @block.gpsimd
        def _(eng):
            p.run_engine(nc, "gpsimd", eng, sems, dma_sems)

    for cm in reversed(ctxs):
        cm.__exit__(None, None, None)
    return nc


def kernel(L0_attn_x, L0_attn_wq, L0_attn_wk, L0_attn_wv, L0_attn_w1,
           L0_attn_b1, L0_attn_w2, L0_attn_b2, L0_attn_g1, L0_attn_be1,
           L0_attn_g2, L0_attn_be2, L0_ffn_W1, L0_ffn_b1, L0_ffn_W2,
           L0_ffn_b2):
    x = np.ascontiguousarray(np.asarray(L0_attn_x, dtype=np.float32))
    B, L, C = x.shape

    import os as _os
    nel = B * L * C
    idx = np.arange(nel, dtype=np.uint64)
    o0, o1 = _threefry2x32(np.uint32(0), np.uint32(42),
                           (idx >> np.uint64(32)).astype(np.uint32),
                           (idx & np.uint64(0xffffffff)).astype(np.uint32))
    if _os.environ.get("BASS_SIM_FP64") == "1":
        b64 = (o0.astype(np.uint64) << np.uint64(32)) | o1.astype(np.uint64)
        f = ((b64 >> np.uint64(12)) |
             np.uint64(0x3FF0000000000000)).view(np.float64) - 1.0
        keep = f < 0.8
    else:
        b32 = o0 ^ o1
        f = ((b32 >> np.uint32(9)) |
             np.uint32(0x3f800000)).view(np.float32) - np.float32(1.0)
        keep = f < np.float32(0.8)
    keep = keep.reshape(B, L, C)
    dmask_full = keep.astype(np.float32) * np.float32(1.25)

    def cols(v, nj):
        return np.ascontiguousarray(
            np.asarray(v, np.float32).reshape(nj, 128).T)

    common = {
        "wq": np.ascontiguousarray(np.asarray(L0_attn_wq, np.float32)),
        "wk": np.ascontiguousarray(np.asarray(L0_attn_wk, np.float32)),
        "wv": np.ascontiguousarray(np.asarray(L0_attn_wv, np.float32)),
        "w1a": np.ascontiguousarray(np.asarray(L0_attn_w1, np.float32)),
        "w2a": np.ascontiguousarray(np.asarray(L0_attn_w2, np.float32)),
        "W1f": np.ascontiguousarray(np.asarray(L0_ffn_W1, np.float32)),
        "W2f": np.ascontiguousarray(np.asarray(L0_ffn_W2, np.float32)),
        "b1a_c": cols(L0_attn_b1, FJ),
        "b2a_r": np.ascontiguousarray(
            np.asarray(L0_attn_b2, np.float32).reshape(1, D)),
        "b1f_c": cols(L0_ffn_b1, FJ),
        "b2f_r": np.ascontiguousarray(
            np.asarray(L0_ffn_b2, np.float32).reshape(1, D)),
        "g1_r": np.ascontiguousarray(
            np.asarray(L0_attn_g1, np.float32).reshape(1, D)),
        "be1_r": np.ascontiguousarray(
            np.asarray(L0_attn_be1, np.float32).reshape(1, D)),
        "g2_r": np.ascontiguousarray(
            np.asarray(L0_attn_g2, np.float32).reshape(1, D)),
        "be2_r": np.ascontiguousarray(
            np.asarray(L0_attn_be2, np.float32).reshape(1, D)),
        "eye": np.eye(128, dtype=np.float32),
    }

    in_maps = []
    for c in range(8):
        n, half = c // 2, c % 2
        own = x[n, half * 2048:(half + 1) * 2048]
        other = x[n, (1 - half) * 2048:(2 - half) * 2048]
        m = dict(common)
        m["xb"] = np.ascontiguousarray(np.concatenate([own, other], axis=0))
        m["dmask"] = np.ascontiguousarray(
            dmask_full[n, half * 2048:(half + 1) * 2048].astype(
                ml_dtypes.bfloat16))
        in_maps.append(m)

    nc = build_nc()
    res = run_bass_kernel_spmd(nc, in_maps, core_ids=list(range(8)))
    outp = np.empty((B, L, C), dtype=np.float32)
    for c in range(8):
        n, half = c // 2, c % 2
        outp[n, half * 2048:(half + 1) * 2048] = res.results[c]["out"]
    return outp


# version 26 (final)
# speedup vs baseline: 1.0141x; 1.0036x over previous
import numpy as np
import ml_dtypes
import concourse.bass as bass
import concourse.mybir as mybir
from concourse.bass_utils import run_bass_kernel_spmd

F32 = mybir.dt.float32
BF16 = mybir.dt.bfloat16
I32 = mybir.dt.int32
ALU = mybir.AluOpType
AF = mybir.ActivationFunctionType

D = 1024
DJ = 8
H = 8
DFF = 4096
FJ = 32
S = 4096
SC = S // 128
TOWN = 2048
TT = 256
NTILE = TOWN // TT
TC = TT // 128
EPS_LN = 1e-5
EPS_ATTN = 1e-6
KVW = 132

ENGINES = ("sync", "tensor", "vector", "scalar", "gpsimd")
NDMASEM = 4


class Tile:
    __slots__ = ("name", "writes", "readers")

    def __init__(self, name):
        self.name = name
        self.writes = []
        self.readers = []


class Op:
    __slots__ = ("idx", "engine", "build", "deps", "is_dma", "needs_inc",
                 "semval", "seq", "slot", "virtual")

    def __init__(self, idx, engine, build, deps, is_dma):
        self.virtual = False
        self.idx = idx
        self.engine = engine
        self.build = build
        self.deps = deps
        self.is_dma = is_dma
        self.needs_inc = False
        self.semval = None
        self.seq = None
        self.slot = 0


class Prog:
    label = ""

    def __init__(self):
        self.ops = []
        self.labels = []
        self.last_by_engine = {e: None for e in ENGINES}
        self._cap = None

    def emit(self, engine, build, reads=(), writes=(), extra_deps=(),
             is_dma=False):
        if self._cap is not None:
            self._cap.append((engine, build, tuple(reads), tuple(writes),
                              tuple(extra_deps), is_dma))
            return None
        deps = set(extra_deps)
        for t in reads:
            deps.update(t.writes)
        for t in writes:
            deps.update(t.writes)
            deps.update(t.readers)
        idx = len(self.ops)
        op = Op(idx, engine, build, sorted(deps), is_dma)
        self.ops.append(op)
        self.labels.append(Prog.label)
        self.last_by_engine[engine] = idx
        for t in reads:
            if idx not in t.readers:
                t.readers.append(idx)
        for t in writes:
            t.writes = [idx]
            t.readers = []
        return idx

    def barrier(self):
        lasts = [i for i in self.last_by_engine.values() if i is not None]
        for e in ENGINES:
            deps = [i for i in lasts if self.ops[i].engine != e]
            idx = self.emit(e, lambda nc: None, extra_deps=deps)
            self.ops[idx].virtual = True

    def finalize(self):
        dma_rr = {e: 0 for e in ENGINES}
        chains = {}
        for op in self.ops:
            if op.is_dma:
                op.slot = dma_rr[op.engine] % NDMASEM
                dma_rr[op.engine] += 1
                chains.setdefault((op.engine, op.slot), []).append(op.idx)
        consumers = {}
        for op in self.ops:
            for d in op.deps:
                consumers.setdefault(d, []).append(op.idx)
        for lst in chains.values():
            for i in range(1, len(lst)):
                cur = self.ops[lst[i]]
                prev = [c for c in consumers.get(lst[i - 1], ())
                        if c < cur.idx]
                if prev:
                    cur.deps = sorted(set(cur.deps) | set(prev))
        self.slot_chain = chains
        memo = {}

        def expand(i):
            if i in memo:
                return memo[i]
            op = self.ops[i]
            if not op.virtual:
                memo[i] = (i,)
                return memo[i]
            r = []
            for d in op.deps:
                r.extend(expand(d))
            memo[i] = tuple(sorted(set(r)))
            return memo[i]

        for op in self.ops:
            if any(self.ops[d].virtual for d in op.deps):
                nd = set()
                for d in op.deps:
                    nd.update(expand(d))
                op.deps = sorted(nd)
        for op in self.ops:
            for d in op.deps:
                dep = self.ops[d]
                if dep.is_dma or dep.engine != op.engine:
                    dep.needs_inc = True
        cnt = {e: 0 for e in ENGINES}
        dma_cnt = {e: [0] * NDMASEM for e in ENGINES}
        seq = {e: 0 for e in ENGINES}
        for op in self.ops:
            op.seq = seq[op.engine]
            seq[op.engine] += 1
            if op.is_dma:
                dma_cnt[op.engine][op.slot] += 1
                op.semval = dma_cnt[op.engine][op.slot] * 16
                op.needs_inc = True
            elif op.needs_inc:
                cnt[op.engine] += 1
                op.semval = cnt[op.engine]
        self.by_engine = {e: [op for op in self.ops if op.engine == e]
                          for e in ENGINES}

    def run_engine(self, nc, engine, eng, sems, dma_sems):
        waited = {}
        drained_upto = -1
        for op in self.by_engine[engine]:
            need_drain = False
            for d in op.deps:
                dep = self.ops[d]
                if dep.engine == engine and not dep.is_dma:
                    if engine != "tensor" and dep.seq > drained_upto:
                        need_drain = True
                else:
                    key = (dep.engine, dep.is_dma, dep.slot)
                    val = dep.semval
                    assert val is not None, (op.idx, d, dep.engine, engine)
                    if dep.is_dma:
                        ch = self.slot_chain[(dep.engine, dep.slot)]
                        import bisect
                        pos = bisect.bisect_left(ch, op.idx) - 1
                        val = self.ops[ch[pos]].semval
                    if waited.get(key, -1) < val:
                        if dep.is_dma:
                            sem = dma_sems[dep.engine][dep.slot]
                        else:
                            sem = sems[dep.engine]
                        eng.wait_ge(sem, val)
                        waited[key] = val
            if need_drain:
                eng.drain()
                drained_upto = op.seq - 1
            ins = op.build(nc)
            if ins is not None:
                if op.is_dma:
                    ins.then_inc(dma_sems[engine][op.slot], 16)
                elif op.needs_inc:
                    ins.then_inc(sems[engine], 1)



def _threefry2x32(k0, k1, x0, x1):
    def rotl(x, r):
        return ((x << np.uint32(r)) | (x >> np.uint32(32 - r))).astype(np.uint32)
    k0 = np.asarray(k0, np.uint32)
    k1 = np.asarray(k1, np.uint32)
    x0 = np.asarray(x0, np.uint32).copy()
    x1 = np.asarray(x1, np.uint32).copy()
    ks2 = (k0 ^ k1 ^ np.uint32(0x1BD11BDA)).astype(np.uint32)
    x0 = (x0 + k0).astype(np.uint32)
    x1 = (x1 + k1).astype(np.uint32)
    ks = [k0, k1, ks2]
    rot = ((13, 15, 26, 6), (17, 29, 16, 24))
    for i in range(5):
        for r in rot[i % 2]:
            x0 = (x0 + x1).astype(np.uint32)
            x1 = rotl(x1, r) ^ x0
        x0 = (x0 + ks[(i + 1) % 3]).astype(np.uint32)
        x1 = (x1 + ks[(i + 2) % 3] + np.uint32(i + 1)).astype(np.uint32)
    return x0, x1

def build_nc():
    nc = bass.Bass("TRN2", target_bir_lowering=False, debug=False)

    xb = nc.dram_tensor("xb", [S, D], F32, kind="ExternalInput").ap()
    wq = nc.dram_tensor("wq", [D, D], F32, kind="ExternalInput").ap()
    wk = nc.dram_tensor("wk", [D, D], F32, kind="ExternalInput").ap()
    wv = nc.dram_tensor("wv", [D, D], F32, kind="ExternalInput").ap()
    w1a = nc.dram_tensor("w1a", [DFF, D], F32, kind="ExternalInput").ap()
    w2a = nc.dram_tensor("w2a", [D, DFF], F32, kind="ExternalInput").ap()
    W1f = nc.dram_tensor("W1f", [DFF, D], F32, kind="ExternalInput").ap()
    W2f = nc.dram_tensor("W2f", [D, DFF], F32, kind="ExternalInput").ap()
    b1a_c = nc.dram_tensor("b1a_c", [128, FJ], F32, kind="ExternalInput").ap()
    b2a_r = nc.dram_tensor("b2a_r", [1, D], F32, kind="ExternalInput").ap()
    b1f_c = nc.dram_tensor("b1f_c", [128, FJ], F32, kind="ExternalInput").ap()
    b2f_r = nc.dram_tensor("b2f_r", [1, D], F32, kind="ExternalInput").ap()
    g1_r = nc.dram_tensor("g1_r", [1, D], F32, kind="ExternalInput").ap()
    be1_r = nc.dram_tensor("be1_r", [1, D], F32, kind="ExternalInput").ap()
    g2_r = nc.dram_tensor("g2_r", [1, D], F32, kind="ExternalInput").ap()
    be2_r = nc.dram_tensor("be2_r", [1, D], F32, kind="ExternalInput").ap()
    dmask = nc.dram_tensor("dmask", [TOWN, D], BF16, kind="ExternalInput").ap()
    eye_in = nc.dram_tensor("eye", [128, 128], F32, kind="ExternalInput").ap()
    out = nc.dram_tensor("out", [TOWN, D], F32, kind="ExternalOutput").ap()

    s_wq_hi = nc.dram_tensor("s_wq_hi", [128, DJ * D], BF16, kind="Internal").ap()
    s_wq_lo = nc.dram_tensor("s_wq_lo", [128, DJ * D], BF16, kind="Internal").ap()
    s_w1a_hi = nc.dram_tensor("s_w1a_hi", [128, FJ * D], BF16, kind="Internal").ap()
    s_w1a_lo = nc.dram_tensor("s_w1a_lo", [128, FJ * D], BF16, kind="Internal").ap()
    s_w2a_hi = nc.dram_tensor("s_w2a_hi", [128, FJ * D], BF16, kind="Internal").ap()
    s_w2a_lo = nc.dram_tensor("s_w2a_lo", [128, FJ * D], BF16, kind="Internal").ap()
    s_W1q = nc.dram_tensor("s_W1q", [128, FJ * D], BF16, kind="Internal").ap()
    s_W2q = nc.dram_tensor("s_W2q", [128, FJ * D], BF16, kind="Internal").ap()

    p = Prog()
    ctxs = []
    tiles = {}
    scr_t = {nm: Tile(nm) for nm in
             ("s_wq_hi", "s_wq_lo", "s_w1a_hi", "s_w1a_lo", "s_w2a_hi",
              "s_w2a_lo", "s_W1q", "s_W2q")}

    def sb(name, shape, dtype):
        cm = nc.sbuf_tensor(name, shape, dtype)
        h = cm.__enter__()
        ctxs.append(cm)
        t = Tile(name)
        tiles[name] = (t, h.ap())
        return t, h.ap()

    def psum(name):
        cm = nc.psum_tensor(name, [128, 512], F32)
        h = cm.__enter__()
        ctxs.append(cm)
        t = Tile(name)
        return t, h.ap()

    def semaphore(name):
        cm = nc.semaphore(name)
        s = cm.__enter__()
        ctxs.append(cm)
        return s

    sems = {e: semaphore("sem_" + e) for e in ENGINES}
    dma_sems = {e: [semaphore(f"dma_{e}{i}") for i in range(NDMASEM)]
                for e in ENGINES}

    eye_t, eye = sb("eyet", [128, 128], F32)
    ones1_t, ones1 = sb("ones1", [1, 128], F32)
    b1a_t, b1a = sb("b1a_t", [128, FJ], F32)
    b1f_t, b1f = sb("b1f_t", [128, FJ], F32)
    b1fq_t, b1fq = sb("b1fq_t", [128, FJ], F32)
    b2a_t, b2a = sb("b2a_row", [1, D], F32)
    b2f_t, b2f = sb("b2f_row", [1, D], F32)
    b2fq_t, b2fq = sb("b2fq_row", [1, D], F32)
    grow = {}
    for nm in ("g1", "be1", "g2", "be2"):
        grow[nm] = sb(nm + "row", [1, D], F32)
    gbc = {}
    for nm in ("g1", "be1"):
        gbc[nm] = sb(nm + "bc", [128, D], F32)
    QW = 1024
    qI = [sb(f"qI{i}", [128, QW], I32) for i in range(4)]
    qF = [sb(f"qF{i}", [128, QW], F32) for i in range(3)]
    kvh_t, kvh = sb("kv_hi", [128, H * KVW], BF16)
    kvl_t, kvl = sb("kv_lo", [128, H * KVW], BF16)
    z_tt, z_ap = sb("z_t", [128, 1], F32)
    pb = [psum(f"pb{i}") for i in range(8)]

    def E(engine, fn, reads=(), writes=(), extra=(), is_dma=False):
        return p.emit(engine, fn, reads=reads, writes=writes,
                      extra_deps=extra, is_dma=is_dma)

    fill_q = []

    def drain_fill(n=10 ** 9):
        while fill_q and n > 0:
            a = fill_q.pop(0)
            cap, p._cap = p._cap, None
            p.emit(a[0], a[1], reads=a[2], writes=a[3], extra_deps=a[4],
                   is_dma=a[5])
            p._cap = cap
            n -= 1

    def vts(dst, dt_, src, st_, s1, s2, op0, op1=None, sc=()):
        if op1 is None:
            fn = lambda nc, d=dst, s=src, a=s1, o0=op0: \
                nc.vector.tensor_scalar(d, s, a, None, op0=o0)
        else:
            fn = lambda nc, d=dst, s=src, a=s1, b=s2, o0=op0, o1=op1: \
                nc.vector.tensor_scalar(d, s, a, b, op0=o0, op1=o1)
        return E("vector", fn, reads=[st_] + list(sc), writes=[dt_])

    def vtt(dst, dt_, a, at_, b, bt_, op):
        fn = lambda nc, d=dst, x=a, y=b, o=op: \
            nc.vector.tensor_tensor(d, x, y, op=o)
        return E("vector", fn, reads=[at_, bt_], writes=[dt_])

    def vstt(dst, dt_, in0, t0, scalar, in1, t1, op0, op1, sc=()):
        fn = lambda nc, d=dst, x=in0, s=scalar, y=in1, o0=op0, o1=op1: \
            nc.vector.scalar_tensor_tensor(d, x, s, y, op0=o0, op1=o1)
        return E("vector", fn, reads=[t0, t1] + list(sc), writes=[dt_])

    def vcp(dst, dt_, src, st_):
        fn = lambda nc, d=dst, s=src: nc.vector.tensor_copy(d, s)
        return E("vector", fn, reads=[st_], writes=[dt_])

    def sact(dst, dt_, src, st_, func, bias=None, scale=1.0, bt_=None):
        if bias is None:
            fn = lambda nc, d=dst, s=src, f=func, sc_=scale: \
                nc.scalar.activation(d, s, f, scale=sc_)
        else:
            fn = lambda nc, d=dst, s=src, f=func, b=bias, sc_=scale: \
                nc.scalar.activation(d, s, f, bias=b, scale=sc_)
        rd = [st_] + ([bt_] if bt_ is not None else [])
        return E("scalar", fn, reads=rd, writes=[dt_])

    def dmain(queue, dst, dt_, src):
        fn = lambda nc, d=dst, s=src: \
            getattr(nc, queue).dma_start(out=d, in_=s)
        return E(queue, fn, writes=[dt_], is_dma=True)

    def dmaout(queue, dst, src, st_):
        fn = lambda nc, d=dst, s=src: \
            getattr(nc, queue).dma_start(out=d, in_=s)
        return E(queue, fn, reads=[st_], is_dma=True)

    def mm(lhs, lt_, rhs, rt_, dst, dt_, start, stop):
        fn = lambda nc, l=lhs, r=rhs, d=dst, s0=start, s1=stop: \
            nc.tensor.matmul(d, l, r, start=s0, stop=s1)
        return E("tensor", fn, reads=[lt_, rt_], writes=[dt_])

    def transp(dst, dt_, src, st_):
        fn = lambda nc, d=dst, s=src: nc.tensor.transpose(d, s, eye[:, :])
        return E("tensor", fn, reads=[st_, eye_t], writes=[dt_])

    def memset(t_, ap_, val):
        fn = lambda nc, a=ap_, v=val: nc.gpsimd.memset(a, v)
        return E("gpsimd", fn, writes=[t_])

    def quant_chain(src, st_, dst, dt_, W, Mb, emin, P=128):
        I1 = qI[0][1][0:P, 0:W]; i1 = qI[0][0]
        I2 = qI[1][1][0:P, 0:W]; i2 = qI[1][0]
        I3 = qI[2][1][0:P, 0:W]; i3 = qI[2][0]
        I4 = qI[3][1][0:P, 0:W]; i4 = qI[3][0]
        F1 = qF[0][1][0:P, 0:W]; f1 = qF[0][0]
        F2 = qF[1][1][0:P, 0:W]; f2 = qF[1][0]
        F3 = qF[2][1][0:P, 0:W]; f3 = qF[2][0]
        bits = src.bitcast(I32)
        vts(I1, i1, bits, st_, 23, 255, ALU.logical_shift_right,
            ALU.bitwise_and)
        vts(I2, i2, I1, i1, 127 + emin, -1, ALU.max, ALU.mult)
        vts(I3, i3, I2, i2, Mb - 1 + 254, 8388608, ALU.add, ALU.mult)
        vts(I1, i1, bits, st_, -2147483648, None, ALU.bitwise_and)
        vts(I4, i4, I2, i2, -1, -Mb, ALU.mult, ALU.add)
        vstt(I4, i4, I4, i4, 8388608, I1, i1, ALU.mult, ALU.add)
        vtt(F1, f1, src, st_, I3.bitcast(F32), i3, ALU.mult)
        vstt(F2, f2, F1, f1, -1.0, F1, f1, ALU.mult, ALU.max)
        vcp(I3, i3, F2, f2)
        vcp(F3, f3, I3, i3)
        vtt(F1, f1, F2, f2, F3, f3, ALU.subtract)
        vts(F2, f2, F1, f1, -0.25, -1.0, ALU.is_ge, ALU.add)
        vstt(F2, f2, F1, f1, 0.25, F2, f2, ALU.is_gt, ALU.add)
        vstt(F2, f2, F1, f1, 0.75, F2, f2, ALU.is_ge, ALU.add)
        vstt(F3, f3, F3, f3, 2.0, F2, f2, ALU.mult, ALU.add)
        vtt(dst, dt_, F3, f3, I4.bitcast(F32), i4, ALU.mult)

    def hq_chain(src, st_, dst, dt_, W):
        I1 = qI[0][1][:, 0:W]; i1 = qI[0][0]
        I2 = qI[1][1][:, 0:W]; i2 = qI[1][0]
        I3 = qI[2][1][:, 0:W]; i3 = qI[2][0]
        I4 = qI[3][1][:, 0:W]; i4 = qI[3][0]
        F1 = qF[0][1][:, 0:W]; f1 = qF[0][0]
        F2 = qF[1][1][:, 0:W]; f2 = qF[1][0]
        F3 = qF[2][1][:, 0:W]; f3 = qF[2][0]
        bits = src.bitcast(I32)
        vts(I1, i1, bits, st_, 23, 255, ALU.logical_shift_right,
            ALU.bitwise_and)
        vts(I2, i2, I1, i1, 113, -1, ALU.max, ALU.mult)
        vts(I3, i3, I1, i1, 113, 8, ALU.min, ALU.add)
        vts(I3, i3, I3, i3, 8388608, None, ALU.mult)
        vts(I1, i1, I2, i2, 2 + 254, 8388608, ALU.add, ALU.mult)
        vts(I4, i4, I2, i2, -1, -2, ALU.mult, ALU.add)
        vts(I4, i4, I4, i4, 8388608, None, ALU.mult)
        vtt(F1, f1, src, st_, I1.bitcast(F32), i1, ALU.mult)
        vtt(F2, f2, F1, f1, I3.bitcast(F32), i3, ALU.subtract)
        vtt(F3, f3, F1, f1, I3.bitcast(F32), i3, ALU.add)
        vts(F1, f1, F2, f2, 0.5, None, ALU.is_gt)
        for j in range(1, 8):
            if j % 2 == 0:
                vstt(F1, f1, F2, f2, j + 0.5, F1, f1, ALU.is_gt, ALU.add)
            else:
                vstt(F1, f1, F3, f3, j + 0.5, F1, f1, ALU.is_ge, ALU.add)
        vtt(dst, dt_, F1, f1, I4.bitcast(F32), i4, ALU.mult)

    NLN = 4
    lnp = [sb(f"lnp{i}", [128, 48], F32) for i in range(NLN)]
    lnpi = [sb(f"lnpi{i}", [128, 2], I32) for i in range(NLN)]
    _ln_rr = [0]

    def ln_multi(jobs):
        nch = len(jobs)
        assert nch <= 2
        sl = _ln_rr[0] % NLN
        _ln_rr[0] += 1
        pt_, pp = lnp[sl]
        it_, ip = lnpi[sl]
        for c, (xap, xt_, _, _) in enumerate(jobs):
            E("vector", lambda nc, d=pp[:, 16 * c:16 * c + 6],
              s=xap[:, 0:512]: nc.vector.bn_stats(d, s),
              reads=[xt_], writes=[pt_])
            E("vector", lambda nc, d=pp[:, 16 * c + 6:16 * c + 12],
              s=xap[:, 512:1024]: nc.vector.bn_stats(d, s),
              reads=[xt_], writes=[pt_])
            E("vector", lambda nc, d=pp[:, 16 * c + 12:16 * c + 14],
              s=pp[:, 16 * c:16 * c + 12]: nc.vector.bn_aggr(d, s),
              reads=[pt_], writes=[pt_])
        VA = pp[:, 13:13 + 16 * nch].rearrange("q (c e) -> q c e",
                                               e=16)[:, :, 0:1]
        T1 = pp[:, 40:40 + nch]
        R = pp[:, 42:42 + nch]
        S = pp[:, 44:44 + nch]
        E("vector", lambda nc, d=T1, s=VA: nc.vector.tensor_scalar(
            d, s, EPS_LN, None, op0=ALU.add), reads=[pt_], writes=[pt_])
        E("vector", lambda nc, d=ip[:, 0:nch], s=T1.bitcast(I32):
          nc.vector.tensor_scalar(d, s, 1, None,
                                  op0=ALU.logical_shift_right),
          reads=[pt_], writes=[it_])
        E("vector", lambda nc, d=ip[:, 0:nch], s=ip[:, 0:nch]:
          nc.vector.tensor_scalar(d, s, -1, 0x5f3759df, op0=ALU.mult,
                                  op1=ALU.add), reads=[it_], writes=[it_])
        E("vector", lambda nc, d=R, s=ip[:, 0:nch].bitcast(F32):
          nc.vector.tensor_copy(d, s), reads=[it_], writes=[pt_])
        for _ in range(2):
            E("vector", lambda nc, d=S, a=R, b=R:
              nc.vector.tensor_tensor(d, a, b, op=ALU.mult),
              reads=[pt_], writes=[pt_])
            E("vector", lambda nc, d=S, a=S, b=T1:
              nc.vector.tensor_tensor(d, a, b, op=ALU.mult),
              reads=[pt_], writes=[pt_])
            E("vector", lambda nc, d=S, s=S: nc.vector.tensor_scalar(
                  d, s, -0.5, 1.5, op0=ALU.mult, op1=ALU.add),
              reads=[pt_], writes=[pt_])
            E("vector", lambda nc, d=R, a=R, b=S:
              nc.vector.tensor_tensor(d, a, b, op=ALU.mult),
              reads=[pt_], writes=[pt_])
        for c, (xap, xt_, dstap, dt_) in enumerate(jobs):
            E("vector", lambda nc, d=dstap, s=xap,
              m_=pp[:, 16 * c + 12:16 * c + 13], r_=R[:, c:c + 1]:
              nc.vector.tensor_scalar(d, s, m_, r_, op0=ALU.subtract,
                                      op1=ALU.mult),
              reads=[xt_, pt_], writes=[dt_])

    def ln_norm(xap, xt_, dstap, dt_):
        ln_multi([(xap, xt_, dstap, dt_)])

    dmain("sync", eye[:, :], eye_t, eye_in[:, :])
    dmain("sync", b1a[:, :], b1a_t, b1a_c[:, :])
    dmain("sync", b1f[:, :], b1f_t, b1f_c[:, :])
    dmain("sync", b2a[0:1, :], b2a_t, b2a_r[:, :])
    dmain("sync", b2f[0:1, :], b2f_t, b2f_r[:, :])
    dmain("sync", grow["g1"][1][0:1, :], grow["g1"][0], g1_r[:, :])
    dmain("sync", grow["be1"][1][0:1, :], grow["be1"][0], be1_r[:, :])
    dmain("sync", grow["g2"][1][0:1, :], grow["g2"][0], g2_r[:, :])
    dmain("sync", grow["be2"][1][0:1, :], grow["be2"][0], be2_r[:, :])
    memset(ones1_t, ones1[0:1, :], 1.0)
    for nm in ("g1", "be1"):
        row_t, row_ap = grow[nm]
        bc_t, bc_ap = gbc[nm]
        for og in range(2):
            bt_, bank = pb[5]
            mm(ones1[0:1, 0:128], ones1_t,
               row_ap[0:1, og * 512:og * 512 + 512], row_t,
               bank[:, :], bt_, start=True, stop=True)
            E("scalar", lambda nc, d=bc_ap[:, og * 512:og * 512 + 512],
              s=bank[:, :]: nc.scalar.copy(d, s),
              reads=[bt_], writes=[bc_t])
    quant_chain(b1f[:, 0:FJ], b1f_t, b1fq[:, 0:FJ], b1fq_t, FJ, 2, -14)
    quant_chain(b2f[0:1, 0:D], b2f_t, b2fq[0:1, 0:D], b2fq_t, D, 2, -14, P=1)

    scope1 = []
    def _enter1(name, shape, dtype):
        cm = nc.sbuf_tensor(name, shape, dtype)
        h = cm.__enter__(); scope1.append(cm); return h
    wtmp_h = [_enter1(f"wtmp{i}", [128, D], F32) for i in range(2)]
    pthi_h = [_enter1(f"pthi{i}", [128, D], BF16) for i in range(2)]
    ptlo_h = [_enter1(f"ptlo{i}", [128, D], BF16) for i in range(2)]
    ptf_h = [_enter1(f"ptf{i}", [128, D], F32) for i in range(2)]
    wkhi_h = _enter1("wkT_hi", [128, DJ * D], BF16)
    wklo_h = _enter1("wkT_lo", [128, DJ * D], BF16)
    wvhi_h = _enter1("wvT_hi", [128, DJ * D], BF16)
    wvlo_h = _enter1("wvT_lo", [128, DJ * D], BF16)
    p1xs_h = [_enter1(f"p1xs{i}", [128, D], F32) for i in range(2)]
    p1xh_h = [_enter1(f"p1xh{i}", [128, D], F32) for i in range(2)]
    p1x2hi_h = [_enter1(f"p1x2hi{i}", [128, D], BF16) for i in range(2)]
    p1x2lo_h = [_enter1(f"p1x2lo{i}", [128, D], BF16) for i in range(2)]
    p1khi_h = [_enter1(f"p1khi{i}", [128, D], BF16) for i in range(2)]
    p1klo_h = [_enter1(f"p1klo{i}", [128, D], BF16) for i in range(2)]
    p1vhi_h = [_enter1(f"p1vhi{i}", [128, H * KVW], BF16) for i in range(2)]
    p1vlo_h = [_enter1(f"p1vlo{i}", [128, H * KVW], BF16) for i in range(2)]
    p1tmp_h = [_enter1("p1tmp0", [128, 512], F32)] * 2
    p1tmpk_h = _enter1("p1tmpk", [128, 512], F32)
    if True:
        wtmpP = [(Tile(f"wtmp{i}"), wtmp_h[i].ap()) for i in range(2)]
        pthiP = [(Tile(f"pthi{i}"), pthi_h[i].ap()) for i in range(2)]
        ptloP = [(Tile(f"ptlo{i}"), ptlo_h[i].ap()) for i in range(2)]
        ptfP = [(Tile(f"ptf{i}"), ptf_h[i].ap()) for i in range(2)]
        wkhi_t, wkhi = Tile("wkhi"), wkhi_h.ap()
        wklo_t, wklo = Tile("wklo"), wklo_h.ap()
        wvhi_t, wvhi = Tile("wvhi"), wvhi_h.ap()
        wvlo_t, wvlo = Tile("wvlo"), wvlo_h.ap()

        def transpose_blocks(src, st_, nsub, sink, banks=(6, 7)):
            for g0 in range(0, nsub, 4):
                bt_, bank = pb[banks[(g0 // 4) % len(banks)]]
                gn = min(4, nsub - g0)
                for gg in range(gn):
                    jj = g0 + gg
                    transp(bank[:, gg * 128:(gg + 1) * 128], bt_,
                           src[:, jj * 128:(jj + 1) * 128], st_)
                sink(g0, gn, bank, bt_)

        _prep_rr = [0]

        def prep_split_one(src_dram, oc, strided_in, dst_hi, dst_lo,
                           dst_hi_t, dst_lo_t, res_hi=None, res_hi_t=None,
                           res_lo=None, res_lo_t=None):
            pi = _prep_rr[0] % 2
            _prep_rr[0] += 1
            wtmp_t, wtmp = wtmpP[pi]
            pthi_t, pthi = pthiP[pi]
            ptlo_t, ptlo = ptloP[pi]
            if strided_in:
                src_ap = src_dram[:, oc * 128:(oc + 1) * 128]
                dmain("scalar",
                      wtmp[:, 0:D].rearrange("q (s c) -> q s c", c=128),
                      wtmp_t,
                      src_ap.rearrange("(s q) c -> q s c", q=128))
            else:
                dmain("scalar", wtmp[:, 0:D], wtmp_t,
                      src_dram[oc * 128:(oc + 1) * 128, :])

            def sink(g0, gn, bank, bt_):
                w = gn * 128
                vcp(pthi[:, g0 * 128:g0 * 128 + w], pthi_t,
                    bank[:, 0:w], bt_)
                vstt(ptlo[:, g0 * 128:g0 * 128 + w], ptlo_t,
                     pthi[:, g0 * 128:g0 * 128 + w], pthi_t, -1.0,
                     bank[:, 0:w], bt_, ALU.mult, ALU.add)
            transpose_blocks(wtmp[:, 0:D], wtmp_t, DJ, sink, banks=(6,))
            if dst_hi is not None:
                E("gpsimd", lambda nc, d=dst_hi[:, oc * D:(oc + 1) * D],
                  s=pthi[:, 0:D]: nc.gpsimd.dma_start(out=d, in_=s),
                  reads=[pthi_t], writes=[dst_hi_t], is_dma=True)
                E("gpsimd", lambda nc, d=dst_lo[:, oc * D:(oc + 1) * D],
                  s=ptlo[:, 0:D]: nc.gpsimd.dma_start(out=d, in_=s),
                  reads=[ptlo_t], writes=[dst_lo_t], is_dma=True)
            else:
                dh = res_hi[:, :].rearrange("q (j d) -> q j d", d=D)[
                    :, :, oc * 128:(oc + 1) * 128]
                dl = res_lo[:, :].rearrange("q (j d) -> q j d", d=D)[
                    :, :, oc * 128:(oc + 1) * 128]
                sh = pthi[:, 0:D].rearrange("q (j r) -> q j r", r=128)
                sl = ptlo[:, 0:D].rearrange("q (j r) -> q j r", r=128)
                E("vector", lambda nc, d=dh, s=sh: nc.vector.tensor_copy(
                    d, s), reads=[pthi_t], writes=[res_hi_t])
                E("vector", lambda nc, d=dl, s=sl: nc.vector.tensor_copy(
                    d, s), reads=[ptlo_t], writes=[res_lo_t])

        def prep_quant_one(src_dram, oc, strided_in, dst, dst_t):
            pi = _prep_rr[0] % 2
            _prep_rr[0] += 1
            wtmp_t, wtmp = wtmpP[pi]
            pthi_t, pthi = pthiP[pi]
            ptf_t, ptf = ptfP[pi]
            if strided_in:
                src_ap = src_dram[:, oc * 128:(oc + 1) * 128]
                dmain("scalar",
                      wtmp[:, 0:D].rearrange("q (s c) -> q s c", c=128),
                      wtmp_t,
                      src_ap.rearrange("(s q) c -> q s c", q=128))
            else:
                dmain("scalar", wtmp[:, 0:D], wtmp_t,
                      src_dram[oc * 128:(oc + 1) * 128, :])
            quant_chain(wtmp[:, 0:D], wtmp_t, ptf[:, 0:D], ptf_t, D, 2, -14)

            def qsink(g0, gn, bank, bt_):
                w = gn * 128
                vcp(pthi[:, g0 * 128:g0 * 128 + w], pthi_t,
                    bank[:, 0:w], bt_)
            transpose_blocks(ptf[:, 0:D], ptf_t, DJ, qsink, banks=(6,))
            E("gpsimd", lambda nc, d=dst[:, oc * D:(oc + 1) * D],
              s=pthi[:, 0:D]: nc.gpsimd.dma_start(out=d, in_=s),
              reads=[pthi_t], writes=[dst_t], is_dma=True)

        for oc in range(DJ):
            prep_split_one(wk, oc, False, None, None, None, None,
                           wkhi, wkhi_t, wklo, wklo_t)
        for oc in range(DJ):
            prep_split_one(wv, oc, False, None, None, None, None,
                           wvhi, wvhi_t, wvlo, wvlo_t)
        prep_units = []
        for oc in range(DJ):
            prep_units.append(lambda oc=oc: prep_split_one(
                wq, oc, False, s_wq_hi, s_wq_lo,
                scr_t["s_wq_hi"], scr_t["s_wq_lo"]))
        for oc in range(FJ):
            prep_units.append(lambda oc=oc: prep_split_one(
                w1a, oc, False, s_w1a_hi, s_w1a_lo,
                scr_t["s_w1a_hi"], scr_t["s_w1a_lo"]))
        for oc in range(FJ):
            prep_units.append(lambda oc=oc: prep_split_one(
                w2a, oc, True, s_w2a_hi, s_w2a_lo,
                scr_t["s_w2a_hi"], scr_t["s_w2a_lo"]))
        for oc in range(FJ):
            prep_units.append(lambda oc=oc: prep_quant_one(
                W1f, oc, False, s_W1q, scr_t["s_W1q"]))
        for oc in range(FJ):
            prep_units.append(lambda oc=oc: prep_quant_one(
                W2f, oc, True, s_W2q, scr_t["s_W2q"]))
        for u in prep_units[:DJ]:
            u()
        prep_units = prep_units[DJ:]

        p1tmp_shared = (Tile("p1tmp0"), p1tmp_h[0].ap())
        p1tmpk_shared = (Tile("p1tmpk"), p1tmpk_h.ap())
        P1 = [dict(
            tmp=p1tmp_shared,
            tmpk=p1tmpk_shared,
            xs=(Tile(f"p1xs{i}"), p1xs_h[i].ap()),
            xh=(Tile(f"p1xh{i}"), p1xh_h[i].ap()),
            x2hi=(Tile(f"p1x2hi{i}"), p1x2hi_h[i].ap()),
            x2lo=(Tile(f"p1x2lo{i}"), p1x2lo_h[i].ap()),
            khi=(Tile(f"p1khi{i}"), p1khi_h[i].ap()),
            klo=(Tile(f"p1klo{i}"), p1klo_h[i].ap()),
            vhi=(Tile(f"p1vhi{i}"), p1vhi_h[i].ap()),
            vlo=(Tile(f"p1vlo{i}"), p1vlo_h[i].ap()),
        ) for i in range(2)]
        for i in range(2):
            memset(P1[i]["vhi"][0], P1[i]["vhi"][1][:, :], 0.0)
            memset(P1[i]["vlo"][0], P1[i]["vlo"][1][:, :], 0.0)
            for h in range(H):
                memset(P1[i]["vhi"][0],
                       P1[i]["vhi"][1][:, h * KVW + 128:h * KVW + 129], 1.0)

        kv_started = [False, False, False]
        nprep = len(prep_units)

        def p1_ln(si):
            par = P1[si % 2]
            p1xs_t, p1xs = par["xs"]
            p1xh_t, p1xh = par["xh"]
            dmain("sync", p1xs[:, :], p1xs_t, xb[si * 128:(si + 1) * 128, :])
            ln_norm(p1xs, p1xs_t, p1xh[:, :], p1xh_t)
            E("vector", lambda nc, d=p1xh[:, :], a=p1xh[:, :],
              b=gbc["g1"][1][:, :]: nc.vector.tensor_tensor(
                  d, a, b, op=ALU.mult),
              reads=[p1xh_t, gbc["g1"][0]], writes=[p1xh_t])
            E("vector", lambda nc, d=p1xh[:, :], a=p1xh[:, :],
              b=gbc["be1"][1][:, :]: nc.vector.tensor_tensor(
                  d, a, b, op=ALU.add),
              reads=[p1xh_t, gbc["be1"][0]], writes=[p1xh_t])

        def p1_tr(si):
            par = P1[si % 2]
            p1xh_t, p1xh = par["xh"]
            p1x2hi_t, p1x2hi = par["x2hi"]
            p1x2lo_t, p1x2lo = par["x2lo"]

            def split_sink_round(g0, bank, bt_):
                E("scalar", lambda nc, d=p1x2hi[:, g0 * 128:g0 * 128 + 512],
                  s=bank[:, :]: nc.scalar.copy(d, s),
                  reads=[bt_], writes=[p1x2hi_t])
                E("vector", lambda nc, d=p1x2lo[:, g0 * 128:g0 * 128 + 512],
                  a=p1x2hi[:, g0 * 128:g0 * 128 + 512], b=bank[:, :]:
                  nc.vector.scalar_tensor_tensor(
                      d, a, -1.0, b, op0=ALU.mult, op1=ALU.add),
                  reads=[bt_, p1x2hi_t], writes=[p1x2lo_t])
            for g0 in range(0, DJ, 4):
                bt_, bank = pb[7 if (g0 // 4) % 2 == 0 else 6]
                for gg in range(4):
                    jj = g0 + gg
                    transp(bank[:, gg * 128:(gg + 1) * 128], bt_,
                           p1xh[:, jj * 128:(jj + 1) * 128], p1xh_t)
                split_sink_round(g0, bank, bt_)

        def p1_kv(si):
            par = P1[si % 2]
            p1x2hi_t, p1x2hi = par["x2hi"]
            p1x2lo_t, p1x2lo = par["x2lo"]
            p1khi_t, p1khi = par["khi"]
            p1klo_t, p1klo = par["klo"]
            p1vhi_t, p1vhi = par["vhi"]
            p1vlo_t, p1vlo = par["vlo"]
            p1tmp_t, p1tmp = par["tmp"]
            p1tmpk_t, p1tmpk = par["tmpk"]
            for (whi_ap, whi_t2, wlo_ap, wlo_t2, isk) in (
                    (wkhi, wkhi_t, wklo, wklo_t, True),
                    (wvhi, wvhi_t, wvlo, wvlo_t, False)):
                for og in range(2):
                    if isk:
                        bt_, bank = pb[3 + og]
                    else:
                        bt_, bank = pb[5 if og == 0 else 3]
                    first = True
                    for dj in range(DJ):
                        lh = p1x2hi[:, dj * 128:(dj + 1) * 128]
                        ll = p1x2lo[:, dj * 128:(dj + 1) * 128]
                        rh = whi_ap[:, dj * D + og * 512:
                                    dj * D + og * 512 + 512]
                        rl = wlo_ap[:, dj * D + og * 512:
                                    dj * D + og * 512 + 512]
                        mm(lh, p1x2hi_t, rh, whi_t2, bank[:, :], bt_,
                           start=first, stop=False)
                        first = False
                        mm(lh, p1x2hi_t, rl, wlo_t2, bank[:, :], bt_,
                           start=False, stop=False)
                        mm(ll, p1x2lo_t, rh, whi_t2, bank[:, :], bt_,
                           start=False, stop=(dj == DJ - 1))
                    if isk:
                        sact(p1tmp[:, :], p1tmp_t, bank[:, :], bt_, AF.Relu,
                             scale=-1.0)
                        sact(p1tmp[:, :], p1tmp_t, p1tmp[:, :], p1tmp_t,
                             AF.Exp, scale=-1.0)
                        E("vector", lambda nc, d=p1tmpk[:, :],
                          s=bank[:, :], e2=p1tmp[:, :]:
                          nc.vector.scalar_tensor_tensor(
                              d, s, 0.0, e2, op0=ALU.max, op1=ALU.add),
                          reads=[bt_, p1tmp_t], writes=[p1tmpk_t])
                        E("scalar", lambda nc,
                          d=p1khi[:, og * 512:og * 512 + 512],
                          s=p1tmpk[:, :]: nc.scalar.copy(d, s),
                          reads=[p1tmpk_t], writes=[p1khi_t])
                        E("vector", lambda nc,
                          d=p1klo[:, og * 512:og * 512 + 512],
                          a=p1khi[:, og * 512:og * 512 + 512],
                          b=p1tmpk[:, :]: nc.vector.scalar_tensor_tensor(
                              d, a, -1.0, b, op0=ALU.mult, op1=ALU.add),
                          reads=[p1tmpk_t, p1khi_t], writes=[p1klo_t])
                    else:
                        h0 = og * 4
                        vdst = p1vhi[:, h0 * KVW:(h0 + 4) * KVW].rearrange(
                            "q (h w) -> q h w", w=KVW)[:, :, 0:128]
                        ldst = p1vlo[:, h0 * KVW:(h0 + 4) * KVW].rearrange(
                            "q (h w) -> q h w", w=KVW)[:, :, 0:128]
                        psrc = bank[:, :].rearrange("q (h w) -> q h w", w=128)
                        E("scalar", lambda nc, d=vdst, s=psrc:
                          nc.scalar.copy(d, s), reads=[bt_], writes=[p1vhi_t])
                        E("vector", lambda nc, d=ldst, a=vdst, b=psrc:
                          nc.vector.scalar_tensor_tensor(
                              d, a, -1.0, b, op0=ALU.mult, op1=ALU.add),
                          reads=[bt_, p1vhi_t], writes=[p1vlo_t])
            for h in range(H):
                bt_, bank = pb[h // 3]
                off = (h % 3) * KVW
                dst = bank[:, off:off + KVW]
                lh = p1khi[:, h * 128:(h + 1) * 128]
                ll = p1klo[:, h * 128:(h + 1) * 128]
                rh = p1vhi[:, h * KVW:(h + 1) * KVW]
                rl = p1vlo[:, h * KVW:(h + 1) * KVW]
                st = not kv_started[h // 3]
                kv_started[h // 3] = True
                last = (si == SC - 1) and (h % 3 == 2 or h == 7)
                mm(lh, p1khi_t, rh, p1vhi_t, dst, bt_, start=st, stop=False)
                mm(lh, p1khi_t, rl, p1vlo_t, dst, bt_, start=False,
                   stop=False)
                mm(ll, p1klo_t, rh, p1vhi_t, dst, bt_, start=False,
                   stop=last)

        p1_ln(0)
        p1_tr(0)
        for si in range(SC):
            lo_u = (si * nprep) // SC
            hi_u = ((si + 1) * nprep) // SC
            for u in prep_units[lo_u:hi_u]:
                u()
            if si + 1 < SC:
                p1_ln(si + 1)
            p1_kv(si)
            if si + 1 < SC:
                p1_tr(si + 1)
        for h in range(H):
            bt_, bank = pb[h // 3]
            off = (h % 3) * KVW
            E("scalar", lambda nc, d=kvh[:, h * KVW:(h + 1) * KVW],
              s=bank[:, off:off + KVW]: nc.scalar.copy(d, s),
              reads=[bt_], writes=[kvh_t])
            E("vector", lambda nc, d=kvl[:, h * KVW:(h + 1) * KVW],
              a=kvh[:, h * KVW:(h + 1) * KVW], b=bank[:, off:off + KVW]:
              nc.vector.scalar_tensor_tensor(
                  d, a, -1.0, b, op0=ALU.mult, op1=ALU.add),
              reads=[bt_, kvh_t], writes=[kvl_t])

    for cm in reversed(scope1):
        cm.__exit__(None, None, None)
    p.barrier()

    scope2 = []
    def _enter2(name, shape, dtype):
        cm = nc.sbuf_tensor(name, shape, dtype)
        h = cm.__enter__(); scope2.append(cm); return h
    xt2_h = [_enter2(f"xt2_{i}", [128, TC * D], F32) for i in range(2)]
    xorig_h = [_enter2("xorig0", [128, TC * D], F32)] * 2
    xh2_h = [_enter2(f"xh2_{i}", [128, D], F32) for i in range(2)]
    x2hi2_h = _enter2("x2hi2", [128, DJ * TT], BF16)
    x2lo2_h = _enter2("x2lo2", [128, DJ * TT], BF16)
    wbufhi_h = [_enter2(f"wbuf_hi{i}", [128, 2 * D], BF16) for i in range(2)]
    wbuflo_h = [_enter2(f"wbuf_lo{i}", [128, 2 * D], BF16) for i in range(2)]
    w2blo_h = [_enter2(f"w2b_lo{i}", [128, 2 * D], BF16) for i in range(2)]
    Qhi_h = _enter2("Qhi", [128, H * TT], BF16)
    Qlo_h = _enter2("Qlo", [128, H * TT], BF16)
    qtmp_h = [_enter2(f"qtmp{i}", [128, TT], F32) for i in range(2)]
    qtmp2_h = [_enter2(f"qtmp2_{i}", [128, TT], F32) for i in range(2)]
    hgrp_h = [_enter2(f"hgrp{i}", [128, 4 * TT], F32) for i in range(2)]
    hgrphi_h = [_enter2(f"hgrp_hi{i}", [128, 4 * TT], BF16) for i in range(2)]
    hgrplo_h = [_enter2(f"hgrp_lo{i}", [128, 4 * TT], BF16) for i in range(2)]
    y1g_h = [_enter2(f"y1g{i}", [128, 4 * TT], F32) for i in range(2)]
    hqg_h = [_enter2(f"hqg{i}", [128, 4 * TT], BF16) for i in range(2)]
    xq2_h = _enter2("xq2", [128, DJ * TT], BF16)
    y2f_h = _enter2("y2f", [128, TC * D], F32)
    dmt_h = [_enter2(f"dmt{i}", [128, TC * D], BF16) for i in range(2)]
    gb2_h = {nm: _enter2(nm + "bc", [128, D], F32) for nm in ("g2", "be2")}
    if True:
        xt2P = [(Tile(f"xt2_{i}"), xt2_h[i].ap()) for i in range(2)]
        _xo = (Tile("xorig0"), xorig_h[0].ap())
        xorigP = [_xo, _xo]
        xh2P = [(Tile(f"xh2_{i}"), xh2_h[i].ap()) for i in range(2)]
        x2f2_t, x2f2 = None, None
        x2hi2_t, x2hi2 = Tile("x2hi2"), x2hi2_h.ap()
        x2lo2_t, x2lo2 = Tile("x2lo2"), x2lo2_h.ap()
        wbufhiP = [(Tile(f"wbufhi{i}"), wbufhi_h[i].ap()) for i in range(2)]
        wbufloP = [(Tile(f"wbuflo{i}"), wbuflo_h[i].ap()) for i in range(2)]
        w2bloP = [(Tile(f"w2blo{i}"), w2blo_h[i].ap()) for i in range(2)]
        Qhi_t, Qhi = Tile("Qhi"), Qhi_h.ap()
        Qlo_t, Qlo = Tile("Qlo"), Qlo_h.ap()
        w2bhiP = [(Qhi_t, Qhi), (Qlo_t, Qlo)]
        qtmpP = [(Tile(f"qtmp{i}"), qtmp_h[i].ap()) for i in range(2)]
        qtmp2P = [(Tile(f"qtmp2_{i}"), qtmp2_h[i].ap()) for i in range(2)]
        hgrpP = [(Tile(f"hgrp{i}"), hgrp_h[i].ap()) for i in range(2)]
        hgrphiP = [(Tile(f"hgrphi{i}"), hgrphi_h[i].ap()) for i in range(2)]
        hgrploP = [(Tile(f"hgrplo{i}"), hgrplo_h[i].ap()) for i in range(2)]
        y1gP = [(Tile(f"y1g{i}"), y1g_h[i].ap()) for i in range(2)]
        hqgP = [(Tile(f"hqg{i}"), hqg_h[i].ap()) for i in range(2)]
        xq2_t, xq2 = Tile("xq2"), xq2_h.ap()
        y2f_t, y2f = Tile("y2f"), y2f_h.ap()
        x2f2_t, x2f2 = y2f_t, y2f
        dmtP = [(Tile(f"dmt{i}"), dmt_h[i].ap()) for i in range(2)]
        for nm in ("g2", "be2"):
            gbc[nm] = (Tile(nm + "bc"), gb2_h[nm].ap())
            row_t, row_ap = grow[nm]
            bc_t, bc_ap = gbc[nm]
            for og in range(2):
                bt_, bank = pb[5]
                mm(ones1[0:1, 0:128], ones1_t,
                   row_ap[0:1, og * 512:og * 512 + 512], row_t,
                   bank[:, :], bt_, start=True, stop=True)
                E("scalar", lambda nc, d=bc_ap[:, og * 512:og * 512 + 512],
                  s=bank[:, :]: nc.scalar.copy(d, s),
                  reads=[bt_], writes=[bc_t])

        def transpose_fold2(src_t, src_ap, gname, bname):
            ln_multi([(src_ap[:, c * D:(c + 1) * D], src_t,
                       xh2P[c % 2][1][:, :], xh2P[c % 2][0])
                      for c in range(TC)])
            for c in range(TC):
                xh2_t, xh2 = xh2P[c % 2]
                E("vector", lambda nc, d=xh2[:, :], a=xh2[:, :],
                  b=gbc[gname][1][:, :]: nc.vector.tensor_tensor(
                      d, a, b, op=ALU.mult),
                  reads=[xh2_t, gbc[gname][0]], writes=[xh2_t])
                E("vector", lambda nc, d=xh2[:, :], a=xh2[:, :],
                  b=gbc[bname][1][:, :]: nc.vector.tensor_tensor(
                      d, a, b, op=ALU.add),
                  reads=[xh2_t, gbc[bname][0]], writes=[xh2_t])
                for g0 in range(0, DJ, 4):
                    bt_, bank = pb[6 + (g0 // 4) % 2]
                    for gg in range(4):
                        jj = g0 + gg
                        transp(bank[:, gg * 128:(gg + 1) * 128], bt_,
                               xh2[:, jj * 128:(jj + 1) * 128], xh2_t)
                    dsth = x2hi2[:, :].rearrange(
                        "q (j r) -> q j r", r=TT)[:, g0:g0 + 4,
                                                  c * 128:c * 128 + 128]
                    dstl = x2lo2[:, :].rearrange(
                        "q (j r) -> q j r", r=TT)[:, g0:g0 + 4,
                                                  c * 128:c * 128 + 128]
                    psrc = bank[:, :].rearrange("q (j r) -> q j r", r=128)
                    E("scalar", lambda nc, d=dsth, s=psrc:
                      nc.scalar.copy(d, s), reads=[bt_], writes=[x2hi2_t])
                    E("vector", lambda nc, d=dstl, a=dsth, b=psrc:
                      nc.vector.scalar_tensor_tensor(
                          d, a, -1.0, b, op0=ALU.mult, op1=ALU.add),
                      reads=[bt_, x2hi2_t], writes=[x2lo2_t])

        def tile_body(ti):
            tp = ti % 2
            xt2_t, xt2 = xt2P[tp]
            xorig_t, xorig = xorigP[tp]
            dmt_t, dmt = dmtP[tp]
            def head1():
                rows_r = xb[ti * TT:(ti + 1) * TT, :].rearrange("(c q) d -> q c d", q=128)
                xt2_3d = xt2[:, :].rearrange("q (c d) -> q c d", d=D)
                xo_3d = xorig[:, :].rearrange("q (c d) -> q c d", d=D)
                dmain("sync", xt2_3d, xt2_t, rows_r)
                E("scalar", lambda nc, d=xorig[:, :], s=xt2[:, :]:
                  nc.scalar.copy(d, s), reads=[xt2_t], writes=[xorig_t])
                dm_r = dmask[ti * TT:(ti + 1) * TT, :].rearrange("(c q) d -> q c d", q=128)
                dmain("sync", dmt[:, :].rearrange("q (c d) -> q c d", d=D), dmt_t, dm_r)

                transpose_fold2(xt2_t, xt2, "g1", "be1")

                for oc in range(H):
                    ws = (oc // 2) % 2
                    wbufhi_t, wbufhi = wbufhiP[ws]
                    wbuflo_t, wbuflo = wbufloP[ws]
                    qtmp_t, qtmp = qtmpP[oc % 2]
                    qtmp2_t, qtmp2 = qtmp2P[oc % 2]
                    if oc % 2 == 0:
                        E("gpsimd", lambda nc, d=wbufhi[:, 0:2 * D],
                          s=s_wq_hi[:, oc * D:(oc + 2) * D]:
                          nc.gpsimd.dma_start(out=d, in_=s),
                          reads=[scr_t["s_wq_hi"]], writes=[wbufhi_t],
                          is_dma=True)
                        E("gpsimd", lambda nc, d=wbuflo[:, 0:2 * D],
                          s=s_wq_lo[:, oc * D:(oc + 2) * D]:
                          nc.gpsimd.dma_start(out=d, in_=s),
                          reads=[scr_t["s_wq_lo"]], writes=[wbuflo_t],
                          is_dma=True)
                    off = (oc % 2) * D
                    bt_, bank = pb[oc % 4]
                    first = True
                    for dj in range(DJ):
                        lh = wbufhi[:, off + dj * 128:off + (dj + 1) * 128]
                        ll = wbuflo[:, off + dj * 128:off + (dj + 1) * 128]
                        rh = x2hi2[:, dj * TT:(dj + 1) * TT]
                        rl = x2lo2[:, dj * TT:(dj + 1) * TT]
                        mm(lh, wbufhi_t, rh, x2hi2_t, bank[:, 0:TT], bt_,
                           start=first, stop=False)
                        first = False
                        mm(lh, wbufhi_t, rl, x2lo2_t, bank[:, 0:TT], bt_,
                           start=False, stop=False)
                        mm(ll, wbuflo_t, rh, x2hi2_t, bank[:, 0:TT], bt_,
                           start=False, stop=(dj == DJ - 1))
                    sact(qtmp[:, :], qtmp_t, bank[:, 0:TT], bt_, AF.Relu,
                         scale=-1.0)
                    sact(qtmp[:, :], qtmp_t, qtmp[:, :], qtmp_t, AF.Exp,
                         scale=-1.0)
                    E("vector", lambda nc, d=qtmp2[:, :], s=bank[:, 0:TT],
                      e2=qtmp[:, :]: nc.vector.scalar_tensor_tensor(
                          d, s, 0.0, e2, op0=ALU.max, op1=ALU.add),
                      reads=[bt_, qtmp_t], writes=[qtmp2_t])
                    E("scalar", lambda nc, d=Qhi[:, oc * TT:(oc + 1) * TT],
                      s=qtmp2[:, :]: nc.scalar.copy(d, s),
                      reads=[qtmp2_t], writes=[Qhi_t])
                    vstt(Qlo[:, oc * TT:(oc + 1) * TT], Qlo_t,
                         Qhi[:, oc * TT:(oc + 1) * TT], Qhi_t, -1.0,
                         qtmp2[:, :], qtmp2_t, ALU.mult, ALU.add)
                    drain_fill(4)

            def head2():
                for h in range(H):
                    for c in range(TC):
                        bt_, bank = pb[6 + (h * TC + c) % 2]
                        lh = Qhi[:, h * TT + c * 128:h * TT + (c + 1) * 128]
                        ll = Qlo[:, h * TT + c * 128:h * TT + (c + 1) * 128]
                        rh = kvh[:, h * KVW:(h + 1) * KVW]
                        rl = kvl[:, h * KVW:(h + 1) * KVW]
                        dst = bank[:, 0:KVW]
                        mm(lh, Qhi_t, rh, kvh_t, dst, bt_, start=True, stop=False)
                        mm(lh, Qhi_t, rl, kvl_t, dst, bt_, start=False,
                           stop=False)
                        mm(ll, Qlo_t, rh, kvh_t, dst, bt_, start=False, stop=True)
                        E("vector", lambda nc, d=z_ap[:, :], s=bank[:, 128:129]:
                          nc.vector.tensor_scalar(d, s, EPS_ATTN, None,
                                                  op0=ALU.add),
                          reads=[bt_], writes=[z_tt])
                        E("vector", lambda nc: nc.vector.reciprocal(
                            z_ap[:, :], z_ap[:, :]), reads=[z_tt], writes=[z_tt])
                        xap = xt2[:, c * D + h * 128:c * D + (h + 1) * 128]
                        E("vector", lambda nc, d=xap, s=bank[:, 0:128],
                          xi=xap: nc.vector.scalar_tensor_tensor(
                              d, s, z_ap[:, 0:1], xi, op0=ALU.mult, op1=ALU.add),
                          reads=[bt_, z_tt, xt2_t], writes=[xt2_t])
                        drain_fill(1)

                transpose_fold2(xt2_t, xt2, "g2", "be2")
            def ffnA():

                def a_w1_block(grp):
                    gp = grp % 2
                    hgrp_t, hgrp = hgrpP[gp]
                    hgrphi_t, hgrphi = hgrphiP[gp]
                    hgrplo_t, hgrplo = hgrploP[gp]
                    for gg in range(4):
                        oc = grp * 4 + gg
                        ws = (oc // 2) % 2
                        off = (oc % 2) * D
                        wbufhi_t, wbufhi = wbufhiP[ws]
                        wbuflo_t, wbuflo = wbufloP[ws]
                        if oc % 2 == 0:
                            E("gpsimd", lambda nc, d=wbufhi[:, 0:2 * D],
                              s=s_w1a_hi[:, oc * D:(oc + 2) * D]:
                              nc.gpsimd.dma_start(out=d, in_=s),
                              reads=[scr_t["s_w1a_hi"]], writes=[wbufhi_t],
                              is_dma=True)
                            E("gpsimd", lambda nc, d=wbuflo[:, 0:2 * D],
                              s=s_w1a_lo[:, oc * D:(oc + 2) * D]:
                              nc.gpsimd.dma_start(out=d, in_=s),
                              reads=[scr_t["s_w1a_lo"]], writes=[wbuflo_t],
                              is_dma=True)
                        bt_, bank = pb[4 + oc % 4]
                        first = True
                        for dj in range(DJ):
                            lh = wbufhi[:, off + dj * 128:off + (dj + 1) * 128]
                            ll = wbuflo[:, off + dj * 128:off + (dj + 1) * 128]
                            rh = x2hi2[:, dj * TT:(dj + 1) * TT]
                            rl = x2lo2[:, dj * TT:(dj + 1) * TT]
                            mm(lh, wbufhi_t, rh, x2hi2_t, bank[:, 0:TT], bt_,
                               start=first, stop=False)
                            first = False
                            mm(lh, wbufhi_t, rl, x2lo2_t, bank[:, 0:TT], bt_,
                               start=False, stop=False)
                            mm(ll, wbuflo_t, rh, x2hi2_t, bank[:, 0:TT], bt_,
                               start=False, stop=(dj == DJ - 1))
                        E("scalar", lambda nc, d=hgrp[:, gg * TT:(gg + 1) * TT],
                          s=bank[:, 0:TT], b=b1a[:, oc:oc + 1]:
                          nc.scalar.activation(d, s, AF.Gelu, bias=b, scale=1.0),
                          reads=[bt_, b1a_t], writes=[hgrp_t])
                        E("scalar", lambda nc,
                          d=hgrphi[:, gg * TT:(gg + 1) * TT],
                          s=hgrp[:, gg * TT:(gg + 1) * TT]:
                          nc.scalar.copy(d, s), reads=[hgrp_t],
                          writes=[hgrphi_t])
                        vstt(hgrplo[:, gg * TT:(gg + 1) * TT], hgrplo_t,
                             hgrphi[:, gg * TT:(gg + 1) * TT], hgrphi_t, -1.0,
                             hgrp[:, gg * TT:(gg + 1) * TT], hgrp_t,
                             ALU.mult, ALU.add)

                def a_w2_block(grp):
                    gp = grp % 2
                    hgrphi_t, hgrphi = hgrphiP[gp]
                    hgrplo_t, hgrplo = hgrploP[gp]
                    for gg in range(4):
                        dffc = grp * 4 + gg
                        ws = (dffc // 2) % 2
                        off = (dffc % 2) * D
                        w2bhi_t, w2bhi = w2bhiP[ws]
                        w2blo_t, w2blo = w2bloP[ws]
                        if dffc % 2 == 0:
                            E("gpsimd", lambda nc, d=w2bhi[:, 0:2 * D],
                              s=s_w2a_hi[:, dffc * D:(dffc + 2) * D]:
                              nc.gpsimd.dma_start(out=d, in_=s),
                              reads=[scr_t["s_w2a_hi"]], writes=[w2bhi_t],
                              is_dma=True)
                            E("gpsimd", lambda nc, d=w2blo[:, 0:2 * D],
                              s=s_w2a_lo[:, dffc * D:(dffc + 2) * D]:
                              nc.gpsimd.dma_start(out=d, in_=s),
                              reads=[scr_t["s_w2a_lo"]], writes=[w2blo_t],
                              is_dma=True)
                        for c in range(TC):
                            lh = hgrphi[:, gg * TT + c * 128:
                                        gg * TT + (c + 1) * 128]
                            ll = hgrplo[:, gg * TT + c * 128:
                                        gg * TT + (c + 1) * 128]
                            for og in range(2):
                                bt_, bank = pb[c * 2 + og]
                                rh = w2bhi[:, off + og * 512:
                                           off + og * 512 + 512]
                                rl = w2blo[:, off + og * 512:
                                           off + og * 512 + 512]
                                mm(lh, hgrphi_t, rh, w2bhi_t, bank[:, :], bt_,
                                   start=(dffc == 0), stop=False)
                                mm(lh, hgrphi_t, rl, w2blo_t, bank[:, :], bt_,
                                   start=False, stop=False)
                                mm(ll, hgrplo_t, rh, w2bhi_t, bank[:, :], bt_,
                                   start=False, stop=False)

                a_w1_block(0)
                drain_fill(4)
                for grp in range(1, FJ // 4):
                    a_w1_block(grp)
                    drain_fill(4)
                    a_w2_block(grp - 1)
                    drain_fill(4)
                a_w2_block(FJ // 4 - 1)
                drain_fill()
                for c in range(TC):
                    for og in range(2):
                        bt_, bank = pb[c * 2 + og]
                        mm(ones1[0:1, 0:128], ones1_t,
                           b2a[0:1, og * 512:og * 512 + 512], b2a_t,
                           bank[:, :], bt_, start=False, stop=True)
                        xap = xt2[:, c * D + og * 512:c * D + og * 512 + 512]
                        E("vector", lambda nc, d=xap, a=xap, b=bank[:, :]:
                          nc.vector.tensor_tensor(d, a, b, op=ALU.add),
                          reads=[bt_, xt2_t], writes=[xt2_t])

                for c in range(TC):
                    xap = xt2[:, c * D:(c + 1) * D]
                    E("vector", lambda nc, d=xap, a=xap,
                      b=xorig[:, c * D:(c + 1) * D]:
                      nc.vector.tensor_tensor(d, a, b, op=ALU.add),
                      reads=[xt2_t, xorig_t], writes=[xt2_t])
                    xh2_t, xh2 = xh2P[c % 2]
                    ln_norm(xap, xt2_t, xh2[:, :], xh2_t)
                    vcp(xap, xt2_t, xh2[:, :], xh2_t)

                for c in range(TC):
                    for g0 in range(0, DJ, 4):
                        bt_, bank = pb[6 + (g0 // 4) % 2]
                        for gg in range(4):
                            jj = g0 + gg
                            transp(bank[:, gg * 128:(gg + 1) * 128], bt_,
                                   xt2[:, c * D + jj * 128:c * D + (jj + 1) * 128],
                                   xt2_t)
                        dstf = x2f2[:, :].rearrange(
                            "q (j r) -> q j r", r=TT)[:, g0:g0 + 4,
                                                      c * 128:c * 128 + 128]
                        psrc = bank[:, :].rearrange("q (j r) -> q j r", r=128)
                        E("scalar", lambda nc, d=dstf, s=psrc:
                          nc.scalar.copy(d, s), reads=[bt_], writes=[x2f2_t])
            def ffnQ():
                for blk in range(DJ * TT // QW):
                    quant_chain(x2f2[:, blk * QW:(blk + 1) * QW], x2f2_t,
                                xq2[:, blk * QW:(blk + 1) * QW], xq2_t,
                                QW, 2, -14)

            def ffnB():
                drain_fill()

                def q_w1_block(grp):
                    gp = grp % 2
                    y1g_t, y1g = y1gP[gp]
                    hqg_t, hqg = hqgP[gp]
                    for gg in range(4):
                        oc = grp * 4 + gg
                        ws = (oc // 2) % 2
                        off = (oc % 2) * D
                        wbufhi_t, wbufhi = wbufhiP[ws]
                        if oc % 2 == 0:
                            E("gpsimd", lambda nc, d=wbufhi[:, 0:2 * D],
                              s=s_W1q[:, oc * D:(oc + 2) * D]:
                              nc.gpsimd.dma_start(out=d, in_=s),
                              reads=[scr_t["s_W1q"]], writes=[wbufhi_t],
                              is_dma=True)
                        bt_, bank = pb[4 + oc % 4]
                        first = True
                        for dj in range(DJ):
                            l_ = wbufhi[:, off + dj * 128:off + (dj + 1) * 128]
                            r_ = xq2[:, dj * TT:(dj + 1) * TT]
                            mm(l_, wbufhi_t, r_, xq2_t, bank[:, 0:TT], bt_,
                               start=first, stop=(dj == DJ - 1))
                            first = False
                        E("scalar", lambda nc, d=y1g[:, gg * TT:(gg + 1) * TT],
                          s=bank[:, 0:TT], b=b1fq[:, oc:oc + 1]:
                          nc.scalar.activation(d, s, AF.Identity, bias=b, scale=1.0),
                          reads=[bt_, b1fq_t], writes=[y1g_t])
                    hq_chain(y1g[:, 0:4 * TT], y1g_t, hqg[:, 0:4 * TT], hqg_t,
                             4 * TT)

                def q_w2_block(grp):
                    gp = grp % 2
                    hqg_t, hqg = hqgP[gp]
                    for gg in range(4):
                        dffc = grp * 4 + gg
                        ws = (dffc // 2) % 2
                        off = (dffc % 2) * D
                        w2blo_t, w2blo = w2bloP[ws]
                        if dffc % 2 == 0:
                            E("gpsimd", lambda nc, d=w2blo[:, 0:2 * D],
                              s=s_W2q[:, dffc * D:(dffc + 2) * D]:
                              nc.gpsimd.dma_start(out=d, in_=s),
                              reads=[scr_t["s_W2q"]], writes=[w2blo_t],
                              is_dma=True)
                        for c in range(TC):
                            l_ = hqg[:, gg * TT + c * 128:gg * TT + (c + 1) * 128]
                            for og in range(2):
                                bt_, bank = pb[c * 2 + og]
                                r_ = w2blo[:, off + og * 512:
                                           off + og * 512 + 512]
                                mm(l_, hqg_t, r_, w2blo_t, bank[:, :], bt_,
                                   start=(dffc == 0), stop=False)

                q_w1_block(0)
                for grp in range(1, FJ // 4):
                    q_w1_block(grp)
                    q_w2_block(grp - 1)
                q_w2_block(FJ // 4 - 1)
                for c in range(TC):
                    for og in range(2):
                        bt_, bank = pb[c * 2 + og]
                        mm(ones1[0:1, 0:128], ones1_t,
                           b2fq[0:1, og * 512:og * 512 + 512], b2fq_t,
                           bank[:, :], bt_, start=False, stop=True)
                        vcp(y2f[:, c * D + og * 512:c * D + og * 512 + 512],
                            y2f_t, bank[:, :], bt_)
            def tail():
                for blk in range(TC * D // QW):
                    quant_chain(y2f[:, blk * QW:(blk + 1) * QW], y2f_t,
                                y2f[:, blk * QW:(blk + 1) * QW], y2f_t,
                                QW, 7, -95)
                for c in range(TC):
                    xap = xt2[:, c * D:(c + 1) * D]
                    E("vector", lambda nc, d=y2f[:, c * D:(c + 1) * D],
                      a=y2f[:, c * D:(c + 1) * D], b=dmt[:, c * D:(c + 1) * D]:
                      nc.vector.tensor_tensor(d, a, b, op=ALU.mult),
                      reads=[y2f_t, dmt_t], writes=[y2f_t])
                    E("vector", lambda nc, d=xap, a=xap,
                      b=y2f[:, c * D:(c + 1) * D]:
                      nc.vector.tensor_tensor(d, a, b, op=ALU.add),
                      reads=[xt2_t, y2f_t], writes=[xt2_t])
                    xh2_t, xh2 = xh2P[c % 2]
                    ln_norm(xap, xt2_t, xh2[:, :], xh2_t)
                    vcp(xap, xt2_t, xh2[:, :], xh2_t)
                orow = out[ti * TT:(ti + 1) * TT, :].rearrange("(c q) d -> q c d", q=128)
                dmaout("gpsimd", orow, xt2[:, :].rearrange("q (c d) -> q c d", d=D), xt2_t)
            return head1, head2, ffnA, ffnQ, ffnB, tail

        stages = [tile_body(ti) for ti in range(NTILE)]
        H1, H2, FA, FQ, FB, TL = 0, 1, 2, 3, 4, 5
        stages[0][H1](); stages[0][H2](); stages[0][FA]()
        for ti in range(NTILE):
            p._cap = []
            stages[ti][FQ]()
            fill_q.extend(p._cap); p._cap = None
            if ti + 1 < NTILE:
                stages[ti + 1][H1]()
                stages[ti + 1][H2]()
            stages[ti][FB]()
            p._cap = []
            stages[ti][TL]()
            fill_q.extend(p._cap); p._cap = None
            if ti + 1 < NTILE:
                stages[ti + 1][FA]()
        drain_fill()


        last_dma = [op.idx for op in p.ops
                    if op.is_dma and op.engine == "gpsimd"]
        fidx = p.emit("gpsimd", lambda nc: None, extra_deps=last_dma[-NTILE:])
        p.ops[fidx].virtual = True

    for cm in reversed(scope2):
        cm.__exit__(None, None, None)
    p.finalize()
    with nc.Block() as block:
        @block.sync
        def _(eng):
            p.run_engine(nc, "sync", eng, sems, dma_sems)

        @block.tensor
        def _(eng):
            p.run_engine(nc, "tensor", eng, sems, dma_sems)

        @block.vector
        def _(eng):
            p.run_engine(nc, "vector", eng, sems, dma_sems)

        @block.scalar
        def _(eng):
            p.run_engine(nc, "scalar", eng, sems, dma_sems)

        @block.gpsimd
        def _(eng):
            p.run_engine(nc, "gpsimd", eng, sems, dma_sems)

    for cm in reversed(ctxs):
        cm.__exit__(None, None, None)
    return nc


def kernel(L0_attn_x, L0_attn_wq, L0_attn_wk, L0_attn_wv, L0_attn_w1,
           L0_attn_b1, L0_attn_w2, L0_attn_b2, L0_attn_g1, L0_attn_be1,
           L0_attn_g2, L0_attn_be2, L0_ffn_W1, L0_ffn_b1, L0_ffn_W2,
           L0_ffn_b2):
    x = np.ascontiguousarray(np.asarray(L0_attn_x, dtype=np.float32))
    B, L, C = x.shape

    import os as _os
    nel = B * L * C
    idx = np.arange(nel, dtype=np.uint64)
    o0, o1 = _threefry2x32(np.uint32(0), np.uint32(42),
                           (idx >> np.uint64(32)).astype(np.uint32),
                           (idx & np.uint64(0xffffffff)).astype(np.uint32))
    if _os.environ.get("BASS_SIM_FP64") == "1":
        b64 = (o0.astype(np.uint64) << np.uint64(32)) | o1.astype(np.uint64)
        f = ((b64 >> np.uint64(12)) |
             np.uint64(0x3FF0000000000000)).view(np.float64) - 1.0
        keep = f < 0.8
    else:
        b32 = o0 ^ o1
        f = ((b32 >> np.uint32(9)) |
             np.uint32(0x3f800000)).view(np.float32) - np.float32(1.0)
        keep = f < np.float32(0.8)
    keep = keep.reshape(B, L, C)
    dmask_full = keep.astype(np.float32) * np.float32(1.25)

    def cols(v, nj):
        return np.ascontiguousarray(
            np.asarray(v, np.float32).reshape(nj, 128).T)

    common = {
        "wq": np.ascontiguousarray(np.asarray(L0_attn_wq, np.float32)),
        "wk": np.ascontiguousarray(np.asarray(L0_attn_wk, np.float32)),
        "wv": np.ascontiguousarray(np.asarray(L0_attn_wv, np.float32)),
        "w1a": np.ascontiguousarray(np.asarray(L0_attn_w1, np.float32)),
        "w2a": np.ascontiguousarray(np.asarray(L0_attn_w2, np.float32)),
        "W1f": np.ascontiguousarray(np.asarray(L0_ffn_W1, np.float32)),
        "W2f": np.ascontiguousarray(np.asarray(L0_ffn_W2, np.float32)),
        "b1a_c": cols(L0_attn_b1, FJ),
        "b2a_r": np.ascontiguousarray(
            np.asarray(L0_attn_b2, np.float32).reshape(1, D)),
        "b1f_c": cols(L0_ffn_b1, FJ),
        "b2f_r": np.ascontiguousarray(
            np.asarray(L0_ffn_b2, np.float32).reshape(1, D)),
        "g1_r": np.ascontiguousarray(
            np.asarray(L0_attn_g1, np.float32).reshape(1, D)),
        "be1_r": np.ascontiguousarray(
            np.asarray(L0_attn_be1, np.float32).reshape(1, D)),
        "g2_r": np.ascontiguousarray(
            np.asarray(L0_attn_g2, np.float32).reshape(1, D)),
        "be2_r": np.ascontiguousarray(
            np.asarray(L0_attn_be2, np.float32).reshape(1, D)),
        "eye": np.eye(128, dtype=np.float32),
    }

    in_maps = []
    for c in range(8):
        n, half = c // 2, c % 2
        own = x[n, half * 2048:(half + 1) * 2048]
        other = x[n, (1 - half) * 2048:(2 - half) * 2048]
        m = dict(common)
        m["xb"] = np.ascontiguousarray(np.concatenate([own, other], axis=0))
        m["dmask"] = np.ascontiguousarray(
            dmask_full[n, half * 2048:(half + 1) * 2048].astype(
                ml_dtypes.bfloat16))
        in_maps.append(m)

    nc = build_nc()
    res = run_bass_kernel_spmd(nc, in_maps, core_ids=list(range(8)))
    outp = np.empty((B, L, C), dtype=np.float32)
    for c in range(8):
        n, half = c // 2, c % 2
        outp[n, half * 2048:(half + 1) * 2048] = res.results[c]["out"]
    return outp
